# version 05
# baseline (speedup 1.0000x reference)
import numpy as np
from contextlib import ExitStack

import concourse.bass as bass
import concourse.mybir as mybir
from concourse.bass_utils import run_bass_kernel_spmd

F32 = mybir.dt.float32
BF16 = mybir.dt.bfloat16
I32 = mybir.dt.int32
AF = mybir.ActivationFunctionType
ALU = mybir.AluOpType
AX = mybir.AxisListType

T, D, H, HD = 4096, 1024, 16, 64
NB = 32
OB = 16
OT = OB * 128

DEBUG = {}
EXC_MODE = 'full'

ENGINES = ("sync", "vector", "scalar", "tensor", "gpsimd")


class Sem:
    def __init__(self, handle, name):
        self.h = handle
        self.name = name
        self.n = 0

    def bump(self, amt):
        self.n += amt
        return self.n


AUTO_CHAIN = {"vector", "scalar", "gpsimd"}


class Emitter:

    def __init__(self, kb):
        self.kb = kb
        self.steps = {e: [] for e in ENGINES}

    def on(self, eng, fn, wait=(), inc=None, kind="op"):
        kb = self.kb
        waits = list(wait)
        auto = eng in AUTO_CHAIN and fn is not None and kind == "op"
        if auto:
            sig = kb.chain_sig.get(eng)
            if sig is not None:
                waits.append(sig)
            if inc is None:
                inc = (kb.sem("auto_" + eng), 1)
        post = None
        if inc is not None:
            sem, amt = inc
            post = sem.bump(amt)
            if auto:
                kb.chain_sig[eng] = (sem, post)
        self.steps[eng].append((fn, tuple(waits), None if inc is None else (inc[0], inc[1])))
        return post

    def dma(self, eng, fn, wait=(), sem=None):
        return self.on(eng, fn, wait=wait, inc=(sem, 16), kind="dma")

    def emit(self, block):
        for eng_name in getattr(self, "engine_order", None) or ENGINES:
            steps = self.steps[eng_name]
            if not steps:
                continue

            def make_body(steps=steps):
                def body(eng):
                    for fn, waits, inc in steps:
                        for sem, val in waits:
                            eng.wait_ge(sem.h, val)
                        if fn is None:
                            continue
                        inst = fn(eng)
                        if inc is not None:
                            inst.then_inc(inc[0].h, inc[1])
                return body

            getattr(block, eng_name)(make_body())
        self.steps = {e: [] for e in ENGINES}


class Phase:

    def __init__(self, kb):
        self.kb = kb
        self.em = Emitter(kb)

    def __enter__(self):
        return self.em

    def __exit__(self, et, ev, tb):
        if et is not None:
            return False
        kb = self.kb
        kb.phase_no += 1
        n = kb.phase_no
        for e in ENGINES:
            self.em.on(e, lambda x: x.nop(), inc=(kb.phase_sems[e], 1))
        for e in ENGINES:
            for e2 in ENGINES:
                if e2 != e:
                    self.em.on(e, None, wait=[(kb.phase_sems[e2], n)])
        self.em.emit(kb.block)
        return False


class KB:
    pass


def ceil_div(a, b):
    return -(-a // b)


def ln_steps(kb, em, src_ap, dst_ap, m, vvar, sq, eps, first_wait=()):
    C = kb.C
    sv = kb.sem(kb.cur_sv)
    ssc = kb.sem(kb.cur_ssc)
    if not isinstance(sq, bass.AP):
        sq = sq[:, :]
    epsap = C.fc[:, 0:1] if eps == 1e-5 else C.fc[:, 1:2]
    em.on("vector", lambda e: e.reduce_sum(m, src_ap, AX.X), wait=first_wait, inc=(sv, 1))
    em.on("vector", lambda e: e.tensor_scalar_mul(m, m, -1.0 / D), inc=(sv, 1))
    em.on("vector", lambda e: e.tensor_scalar_add(dst_ap, src_ap, m), inc=(sv, 1))
    pv = sv.n
    em.on("scalar", lambda e: e.activation(sq, dst_ap, AF.Square, accum_out=vvar),
          wait=[(sv, pv)], inc=(ssc, 1))
    sc0 = ssc.n
    em.on("vector", lambda e: e.tensor_scalar_mul(vvar, vvar, 1.0 / D),
          wait=[(ssc, sc0)], inc=(sv, 1))
    pv = sv.n
    em.on("scalar", lambda e: e.activation(vvar, vvar, AF.Sqrt, bias=epsap),
          wait=[(sv, pv)], inc=(ssc, 1))
    sc0 = ssc.n
    em.on("vector", lambda e: e.reciprocal(vvar, vvar), wait=[(ssc, sc0)], inc=(sv, 1))
    em.on("vector", lambda e: e.tensor_scalar_mul(dst_ap, dst_ap, vvar), inc=(sv, 1))



def build_nc(upto=99):
    nc = bass.Bass("TRN2", target_bir_lowering=False, num_devices=8)
    kb = KB()
    kb.nc = nc
    kb.phase_no = 0
    kb.out_written = False
    kb.chain_sig = {}
    kb.rt = {}

    def rtv(e, eng, name):
        key = (eng, name)
        sv = kb.rt.get(key)
        if sv is None:
            if name == "par":
                sv = e.partition_id() % 2
            elif name == "par128":
                sv = rtv(e, eng, "par") * 128
            elif name == "par127":
                sv = rtv(e, eng, "par") * 127
            elif name == "par2048":
                sv = rtv(e, eng, "par") * 2048
            elif name == "truthy":
                sv = e.partition_id() >= 0
            elif name == "zsv":
                e.reg_mov(kb.exreg, 0)
                sv = e.snap(kb.exreg, min_val=0, max_val=1)
            kb.rt[key] = sv
        return sv

    kb.rtv = rtv

    I = {}

    def din(name, shape):
        I[name] = nc.dram_tensor(name, list(shape), F32, kind="ExternalInput").ap()
        return I[name]

    din("xseq", (T, D))
    for nm, sh in [
        ("l0_wq", (D, D)), ("l0_bq", (D,)), ("l0_wk", (D, D)), ("l0_bk", (D,)),
        ("l0_wv", (D, D)), ("l0_bv", (D,)), ("l0_wo", (D, D)), ("l0_bo", (D,)),
        ("ln_g", (D,)), ("ln_b", (D,)), ("rW", (D, 8)), ("rb", (8,)),
        ("W1", (8, D, 2048)), ("b1", (8, 2048)), ("W2", (8, 2048, D)), ("b2", (8, D)),
        ("gam", (D,)), ("bet", (D,)),
        ("l1_wq", (D, D)), ("l1_bq", (D,)), ("l1_wk", (D, D)), ("l1_bk", (D,)),
        ("l1_wv", (D, D)), ("l1_bv", (D,)), ("l1_wo", (D, D)), ("l1_bo", (D,)),
        ("mu", (1, 1)),
        ("w1f", (4096, D)), ("convw", (4096, 3)), ("convb", (4096,)),
        ("w2f", (D, 4096)),
        ("parconst", (3, 1)),
    ]:
        din(nm, sh)
    kb.I = I

    out = nc.dram_tensor("xout", [OT, D], F32, kind="ExternalOutput").ap()
    kb.out = out
    kb.dbg = {}
    for nm, sh in DEBUG.items():
        kb.dbg[nm] = nc.dram_tensor("dbg_" + nm, list(sh), F32, kind="ExternalOutput").ap()

    kb.dKT = nc.dram_tensor("dKT", [8, D, 512], F32).ap()
    kb.dV = nc.dram_tensor("dV", [8, 512, 16 * 65], F32).ap()
    kb.dQT = nc.dram_tensor("dQT", [NB, D, 128], F32).ap()
    kb.dA = nc.dram_tensor("dA", [OB, 128, D], F32).ap()
    kb.dAS = nc.dram_tensor("dAS", [OB, 128, D], F32).ap()
    kb.xsh = nc.dram_tensor("xsh", [T + 1, D], F32, addr_space="Shared").ap()
    kb.x1tmp = nc.dram_tensor("x1tmp", [T, D], F32).ap()
    kb.bar_in = nc.dram_tensor("bar_in", [1, 16], F32).ap()
    kb.bar_out = nc.dram_tensor("bar_out", [1, 16], F32).ap()
    kb.x1pad = nc.dram_tensor("x1pad", [17 * 256 + 4, D], F32).ap()
    kb.x1v = kb.x1pad[1:T + 1, :]
    kb.dQTown = nc.dram_tensor("dQTown", [OB, D, 128], F32).ap()
    kb.xown = nc.dram_tensor("xown", [OB, 128, D], F32).ap()
    kb.x1own = nc.dram_tensor("x1own", [OB, 128, D], F32).ap()
    kb.dW1qT = nc.dram_tensor("dW1qT", [D, 4096], BF16).ap()
    kb.dW2qT = nc.dram_tensor("dW2qT", [4096, D], BF16).ap()
    kb.dUN = None

    es = ExitStack()
    kb.es = es

    kb.sems = {}

    def sem(name):
        if name not in kb.sems:
            kb.sems[name] = Sem(es.enter_context(nc.semaphore(name)), name)
        return kb.sems[name]

    kb.sem = sem
    kb.phase_sems = {e: sem("ph_" + e) for e in ENGINES}

    def sb(name, shape, dtype=F32, stack=None):
        return (stack or es).enter_context(nc.sbuf_tensor(name, list(shape), dtype))

    kb.sb = sb

    C = KB()
    kb.C = C
    C.ident = sb("c_ident", [128, 128])
    C.iota_km = sb("c_iotakm", [128, 128])
    C.maskA = sb("c_maskA", [128, 128])
    C.maskB = sb("c_maskB", [128, 128])
    C.par = sb("c_par", [128, 1])
    C.bq = sb("c_bq", [128, 8])
    C.bk = sb("c_bk", [128, 8])
    C.bv_bc = sb("c_bv_bc", [128, D])
    C.bo_bc = sb("c_bo_bc", [128, D])
    C.rb_bc = sb("c_rb_bc", [128, 8])
    C.lng_bc = sb("c_lng_bc", [128, D])
    C.lnb_bc = sb("c_lnb_bc", [128, D])
    C.gam_bc = sb("c_gam_bc", [128, D])
    C.bet_bc = sb("c_bet_bc", [128, D])
    C.b1 = sb("c_b1", [128, 8 * 16])
    C.b2 = sb("c_b2", [8, D])
    C.negmu = sb("c_negmu", [128, 1])
    C.cw = sb("c_cw", [128, 32 * 3])
    C.cb = sb("c_cb", [128, 32])
    C.ones = sb("c_ones", [128, 1])
    C.scratch1 = sb("c_scr1", [128, 1])
    C.fc = sb("c_fc", [128, 8])
    C.onesrow = sb("c_onesrow", [128, 128])
    C.identB = sb("c_identB", [128, 128])

    kb.exreg = es.enter_context(nc.gpsimd.register("exreg"))
    kb.block = es.enter_context(nc.Block())

    def par_sv(eng):
        return eng.partition_id() % 2

    kb.par_sv = par_sv

    build_P0(kb)
    if upto >= 1:
        with nc.named_scope("P1prep0"):
            build_prep(kb, layer=0)
    if upto >= 2:
        with nc.named_scope("P2attn0"):
            build_attn(kb, layer=0)
    if upto >= 3:
        with nc.named_scope("P3moe"):
            build_moe(kb)
    if upto >= 4:
        with nc.named_scope("P4exch"):
            build_exchange(kb)
    if upto >= 5:
        with nc.named_scope("P5prep1"):
            build_prep(kb, layer=1)
    if upto >= 6:
        with nc.named_scope("P6attn1"):
            build_attn(kb, layer=1)
    if upto >= 7:
        with nc.named_scope("P7bitffn"):
            build_bitffn(kb)

    finish(kb)
    es.close()
    return nc



def build_P0(kb):
    nc, C, I, sem = kb.nc, kb.C, kb.I, kb.sem
    sd = sem("p0_dma")
    sc = sem("p0_cmp")
    with nc.psum_tensor("p0_ps0", [128, 512], F32) as _p0ps, Phase(kb) as em:
        kb.ps0 = _p0ps
        def ld(dst, src, wait=(), slow=False):
            em.dma("sync", lambda e, d=dst, s=src, slow=slow:
                   e.dma_start(out=d, in_=s, allow_slow_non_contiguous=slow),
                   wait=wait, sem=sd)

        ld(C.bq[:, :], I["l0_bq"].rearrange("(c p) -> p c", p=128), slow=True)
        ld(C.bk[:, :], I["l0_bk"].rearrange("(c p) -> p c", p=128), slow=True)
        ld(C.b1[:, :].rearrange("p (e c) -> p e c", e=8), I["b1"].rearrange("e (c p) -> p e c", p=128), slow=True)
        ld(C.b2[:, :], I["b2"][:, :])
        ld(C.cw[:, :].rearrange("p (c j) -> p c j", j=3), I["convw"].rearrange("(c p) j -> p c j", p=128), slow=True)
        ld(C.cb[:, :], I["convb"].rearrange("(c p) -> p c", p=128), slow=True)
        def par_load(e):
            parv = e.partition_id() % 2
            return e.dma_start(out=C.par[0:1, 0:1], in_=I["parconst"][bass.ds(parv + 1, 1), :])
        spar = sem("p0_pdma")
        em.dma("gpsimd", par_load, sem=spar)

        ps0 = kb.ps0
        stn = sem("p0_ten")

        em.on("gpsimd", lambda e: e.iota(C.iota_km[:, :], pattern=[[1, 128]], base=0,
                                         channel_multiplier=-1,
                                         allow_small_or_imprecise_dtypes=True),
              inc=(sc, 1))
        em.on("vector", lambda e: e.tensor_single_scalar(C.ident[:, :], C.iota_km[:, :], 0.0, ALU.is_equal),
              wait=[(sc, sc.n)], inc=(sc, 1))
        em.on("vector", lambda e: e.tensor_single_scalar(C.identB[:, :], C.iota_km[:, :], 64.0, ALU.is_equal),
              inc=(sc, 1))
        em.on("vector", lambda e: e.memset(C.onesrow[:, :], 1.0), inc=(sc, 1))
        onesrow_done = sc.n

        def bcast_mm(dst, src_row, n, wait=()):
            w = list(wait) + [(sc, onesrow_done), (sc, sc.n)]
            em.on("tensor", lambda e, dst=dst, src_row=src_row, n=n:
                  e.matmul(ps0[:, 0:n], C.onesrow[0:1, 0:128], src_row, start=True, stop=True),
                  wait=w, inc=(stn, 1))
            tv = stn.n
            em.on("vector", lambda e, dst=dst, n=n: e.tensor_copy(dst, ps0[:, 0:n]),
                  wait=[(stn, tv)], inc=(sc, 1))

        bcast_mm(C.par[:, 0:1], C.par[0:1, 0:1], 1, wait=[(spar, spar.n)])
        em.on("vector", lambda e: e.tensor_scalar_mul(C.scratch1[:, :], C.par[:, :], -128.0),
              wait=[(sc, sc.n)], inc=(sc, 1))
        em.on("vector", lambda e: e.tensor_single_scalar(C.maskA[:, :], C.iota_km[:, :], C.scratch1[:, 0:1], ALU.is_ge),
              wait=[(sc, sc.n)], inc=(sc, 1))
        em.on("vector", lambda e: e.tensor_scalar(C.scratch1[:, :], C.par[:, :], -128.0, 128.0, ALU.mult, ALU.add),
              wait=[(sc, sc.n)], inc=(sc, 1))
        em.on("vector", lambda e: e.tensor_single_scalar(C.maskB[:, :], C.iota_km[:, :], C.scratch1[:, 0:1], ALU.is_ge),
              wait=[(sc, sc.n)], inc=(sc, 1))
        em.on("vector", lambda e: e.memset(C.ones[:, :], 1.0), inc=(sc, 1))
        for j, v in [(0, 1e-5), (1, 1e-6), (2, 0.125), (3, 0.5)]:
            em.on("vector", lambda e, j=j, v=v: e.memset(C.fc[:, j:j + 1], v), inc=(sc, 1))

        row = kb.sb("p0_row", [1, D])

        def bcast(dst, nm, n=D, rearr="(a d) -> a d"):
            ld(row[0:1, 0:n], I[nm].rearrange(rearr, a=1) if rearr else I[nm],
               wait=[(sc, sc.n)])
            nl = sd.n
            for j in range(0, n, 512):
                w = min(512, n - j)
                bcast_mm(dst[:, j:j + w], row[0:1, j:j + w], w, wait=[(sd, nl)])

        bcast(C.bv_bc[:, :], "l0_bv")
        bcast(C.bo_bc[:, :], "l0_bo")
        bcast(C.lng_bc[:, :], "ln_g")
        bcast(C.lnb_bc[:, :], "ln_b")
        bcast(C.gam_bc[:, :], "gam")
        bcast(C.bet_bc[:, :], "bet")
        bcast(C.rb_bc[:, :], "rb", n=8)
        ld(row[0:1, 0:1], I["mu"][0:1, 0:1], wait=[(sc, sc.n)])
        nl = sd.n
        em.on("vector", lambda e: e.tensor_scalar_mul(row[0:1, 0:1], row[0:1, 0:1], -1.0),
              wait=[(sd, nl)], inc=(sc, 1))
        bcast_mm(C.negmu[:, :], row[0:1, 0:1], 1)
        em.on("sync", None, wait=[(sd, sd.n)])
        em.on("gpsimd", None, wait=[(sc, sc.n), (spar, spar.n)])



def build_prep(kb, layer):
    nc, C, I, sem = kb.nc, kb.C, kb.I, kb.sem
    L = f"l{layer}"
    with ExitStack() as ps:
        sbl = lambda n, s, d=F32: kb.sb(f"{L}p_" + n, s, d, stack=ps)
        wq = sbl("wq", [128, 8 * D])
        wk = sbl("wk", [128, 8 * D])
        wv = sbl("wv", [128, 8 * D])
        wstage = sbl("wstage", [128, D])
        xs = sbl("xs", [128, 4 * D])
        xn = sbl("xn", [128, 4 * D])
        xnT = sbl("xnT", [128, 8 * 512])
        kevac = sbl("kevac", [128, 512])
        vevac = sbl("vevac", [128, 16 * 65])
        stat = sbl("stat", [128, 8])
        sq = sbl("sqs", [128, D])
        ps_tr = ps.enter_context(nc.psum_tensor(f"{L}p_tr", [128, 1024], F32))
        kb.ps0 = ps.enter_context(nc.psum_tensor(f"{L}p_ps0", [128, 512], F32))
        ps_k = ps.enter_context(nc.psum_tensor(f"{L}p_k", [128, 1024], F32))
        ps_v = ps.enter_context(nc.psum_tensor(f"{L}p_v", [128, 1024], F32))

        kb.cur_sv = f"{L}p_vec"
        kb.cur_ssc = f"{L}p_scl"
        sd = sem(f"{L}p_dma")
        sw = sem(f"{L}p_wdma")
        so = sem(f"{L}p_odma")
        sv = sem(f"{L}p_vec")
        st = sem(f"{L}p_ten")
        ssc = sem(f"{L}p_scl")

        with Phase(kb) as em:
            tr_k = [0]
            tr_copy_done = []

            def transpose_tile(em, src_ap, dst_ap, extra_wait=()):
                col = (tr_k[0] % 2) * 512
                w = list(extra_wait)
                if tr_k[0] >= 2:
                    w.append((sv, tr_copy_done[tr_k[0] - 2]))
                em.on("tensor", lambda e, src_ap=src_ap, col=col:
                      e.transpose(ps_tr[:, col:col + 128], src_ap, C.ident[:, :]),
                      wait=w, inc=(st, 1))
                tv = st.n
                em.on("vector", lambda e, dst_ap=dst_ap, col=col:
                      e.tensor_copy(dst_ap, ps_tr[:, col:col + 128]),
                      wait=[(st, tv)], inc=(sv, 1))
                tr_copy_done.append(sv.n)
                tr_k[0] += 1

            if layer == 1:
                em.dma("sync", lambda e: e.dma_start(out=C.bq[:, :], in_=I["l1_bq"].rearrange("(c p) -> p c", p=128),
                                                     allow_slow_non_contiguous=True), sem=sw)
                em.dma("sync", lambda e: e.dma_start(out=C.bk[:, :], in_=I["l1_bk"].rearrange("(c p) -> p c", p=128),
                                                     allow_slow_non_contiguous=True), sem=sw)
                row1 = sbl("row1", [1, D])
                em.dma("sync", lambda e: e.dma_start(out=row1[0:1, :], in_=I["l1_bv"].rearrange("(a d) -> a d", a=1)),
                       sem=sw)
                rowdma = sw.n
                for j in range(0, D, 512):
                    em.on("tensor", lambda e, j=j:
                          e.matmul(kb.ps0[:, 0:512], C.onesrow[0:1, 0:128], row1[0:1, j:j + 512],
                                   start=True, stop=True),
                          wait=[(sw, rowdma), (sv, sv.n)], inc=(st, 1))
                    tv = st.n
                    em.on("vector", lambda e, j=j: e.tensor_copy(C.bv_bc[:, j:j + 512], kb.ps0[:, 0:512]),
                          wait=[(st, tv)], inc=(sv, 1))

            wnames = {0: ("l0_wq", "l0_wk", "l0_wv"), 1: ("l1_wq", "l1_wk", "l1_wv")}[layer]
            for wi, (wt, nm) in enumerate(zip((wq, wk, wv), wnames)):
                for c in range(8):
                    if layer == 0:
                        em.dma("sync", lambda e, wt=wt, nm=nm, c=c:
                               e.dma_start(out=wt[:, c * D:(c + 1) * D], in_=I[nm][c * 128:(c + 1) * 128, :]),
                               sem=sw)
                    else:
                        em.dma("sync", lambda e, nm=nm, c=c:
                               e.dma_start(out=wstage[:, :], in_=I[nm][c * 128:(c + 1) * 128, :]),
                               wait=[(st, st.n)], sem=sw)
                        need = sw.n
                        for j in range(8):
                            transpose_tile(em, wstage[:, j * 128:(j + 1) * 128],
                                           wt[:, j * D + c * 128: j * D + c * 128 + 128],
                                           extra_wait=[(sw, need)] if j == 0 else ())
            wload = sw.n

            src = kb.x1v if layer == 1 else I["xseq"]
            guard_tr = [0]

            for sup in range(8):
                for b in range(4):
                    em.dma("sync", lambda e, sup=sup, b=b:
                           e.dma_start(out=xs[:, b * D:(b + 1) * D],
                                       in_=src[(4 * sup + b) * 128:(4 * sup + b + 1) * 128, :]),
                           wait=[(sv, sv.n), (ssc, ssc.n)] if b == 0 else (), sem=sd)
                ld = sd.n

                for b in range(4):
                    xsb = xs[:, b * D:(b + 1) * D]
                    xnb = xn[:, b * D:(b + 1) * D]
                    m = stat[:, b:b + 1]
                    vvar = stat[:, 4 + b:5 + b]

                    ln_steps(kb, em, xsb, xnb, m, vvar, sq, 1e-5,
                             first_wait=[(sd, ld), (st, guard_tr[0])] if b == 0 else ())
                    if layer == 1:
                        ln_steps(kb, em, xnb, xnb, m, vvar, sq, 1e-6)
                        em.on("vector", lambda e, xnb=xnb: e.tensor_mul(xnb, xnb, C.gam_bc[:, :]), inc=(kb.sem(f"{L}p_vec"), 1))
                        em.on("vector", lambda e, xnb=xnb: e.tensor_add(xnb, xnb, C.bet_bc[:, :]), inc=(kb.sem(f"{L}p_vec"), 1))
                xn_done = sv.n
                if sup == 0 and "XN0" in kb.dbg:
                    em.dma("gpsimd", lambda e: e.dma_start(out=kb.dbg["XN0"], in_=xn[:, :]),
                           wait=[(sv, xn_done)], sem=so)
                    em.on("vector", None, wait=[(so, so.n)])

                for b in range(4):
                    for c in range(8):
                        w = []
                        if b == 0 and c == 0:
                            w.append((sv, xn_done))
                            if layer == 0 and sup == 0:
                                w.append((sw, wload))
                        transpose_tile(em, xn[:, b * D + c * 128: b * D + c * 128 + 128],
                                       xnT[:, c * 512 + b * 128: c * 512 + b * 128 + 128],
                                       extra_wait=w)
                tr_done = sv.n
                guard_tr[0] = st.n

                for wt_, bias_, isq in ((wk, C.bk, False), (wq, C.bq, True)):
                    for kt in range(8):
                        pb = kt % 2
                        for dc in range(8):
                            w = []
                            if dc == 0:
                                w.append((sv, sv.n))
                                if kt == 0 and not isq:
                                    w.append((sw, wload))
                            em.on("tensor", lambda e, wt_=wt_, kt=kt, dc=dc, pb=pb:
                                  e.matmul(ps_k[:, pb * 512:pb * 512 + 512],
                                           wt_[:, dc * D + kt * 128: dc * D + kt * 128 + 128],
                                           xnT[:, dc * 512:(dc + 1) * 512],
                                           start=(dc == 0), stop=(dc == 7)),
                                  wait=w, inc=(st, 1) if dc == 7 else None)
                        tv = st.n
                        em.on("vector", lambda e, bias_=bias_, kt=kt, pb=pb:
                              e.tensor_scalar_add(kevac[:, :], ps_k[:, pb * 512:pb * 512 + 512], bias_[:, kt:kt + 1]),
                              wait=[(st, tv)], inc=(sv, 1))
                        ev = sv.n
                        if not isq:
                            em.dma("gpsimd", lambda e, sup=sup, kt=kt:
                                   e.dma_start(out=kb.dKT[sup, kt * 128:(kt + 1) * 128, :], in_=kevac[:, :]),
                                   wait=[(sv, ev)], sem=so)
                        else:
                            for b in range(4):
                                em.dma("gpsimd", lambda e, sup=sup, kt=kt, b=b:
                                       e.dma_start(out=kb.dQT[4 * sup + b, kt * 128:(kt + 1) * 128, :],
                                                   in_=kevac[:, b * 128:(b + 1) * 128]),
                                       wait=[(sv, ev)] if b == 0 else (), sem=so)
                        em.on("vector", None, wait=[(so, so.n)])

                vev3 = vevac[:, :].rearrange("p (h j) -> p h j", j=65)
                for s4 in range(4):
                    for dw in range(2):
                        pb = dw
                        for dc in range(8):
                            w = [(sv, sv.n)] if dc == 0 else []
                            em.on("tensor", lambda e, s4=s4, dw=dw, dc=dc, pb=pb:
                                  e.matmul(ps_v[:, pb * 512:pb * 512 + 512],
                                           xnT[:, dc * 512 + s4 * 128: dc * 512 + s4 * 128 + 128],
                                           wv[:, dc * D + dw * 512: dc * D + dw * 512 + 512],
                                           start=(dc == 0), stop=(dc == 7)),
                                  wait=w, inc=(st, 1) if dc == 7 else None)
                        tv = st.n
                        em.on("vector", lambda e, dw=dw, pb=pb:
                              e.tensor_tensor(
                                  vev3[:, dw * 8:(dw + 1) * 8, 0:64],
                                  ps_v[:, pb * 512:pb * 512 + 512],
                                  C.bv_bc[:, dw * 512:(dw + 1) * 512], ALU.add),
                              wait=[(st, tv)], inc=(sv, 1))
                    em.on("vector", lambda e: e.memset(vev3[:, :, 64:65], 1.0),
                          inc=(sv, 1))
                    ev = sv.n
                    em.dma("gpsimd", lambda e, sup=sup, s4=s4:
                           e.dma_start(out=kb.dV[sup, s4 * 128:(s4 + 1) * 128, :], in_=vevac[:, :]),
                           wait=[(sv, ev)], sem=so)
                    em.on("vector", None, wait=[(so, so.n)])

            em.on("gpsimd", None, wait=[(so, so.n)])
            em.on("sync", None, wait=[(sd, sd.n), (sw, sw.n)])


def build_attn(kb, layer):
    nc, C, I, sem = kb.nc, kb.C, kb.I, kb.sem
    L = f"l{layer}"
    with ExitStack() as ps:
        sbl = lambda n, s, d=F32: kb.sb(f"{L}a_" + n, s, d, stack=ps)
        qwin = sbl("qwin", [128, 4096])
        ksup = [sbl(f"ksup{i}", [128, 4096]) for i in range(2)]
        vsup = [sbl(f"vsup{i}", [128, 4160]) for i in range(2)]
        num = [sbl(f"num{i}", [128, 1024]) for i in range(3)]
        sig = [sbl(f"sig{i}", [128, 1024]) for i in range(2)] if layer == 1 else None
        U_sb = sbl("U", [128, 8192])
        aevac = sbl("aevac", [128, 512])
        wo = sbl("wo", [128, 8192])

        ps_sc = ps.enter_context(nc.psum_tensor(f"{L}a_sc", [128, 3072], F32))
        ps_u = ps.enter_context(nc.psum_tensor(f"{L}a_u", [128, 1024], F32))

        sd = sem(f"{L}a_dma")
        sqd = sem(f"{L}a_qdma")
        so = sem(f"{L}a_odma")
        sv = sem(f"{L}a_vec")
        st = sem(f"{L}a_ten")
        ssc = sem(f"{L}a_scl")
        kb.cur_sv = f"{L}a_vec"
        kb.cur_ssc = f"{L}a_scl"

        with Phase(kb) as em:
            tr_k = [0]
            tr_done = []

            def transpose_tile(src_ap, dst_ap, extra_wait=()):
                col = (tr_k[0] % 2) * 512
                w = list(extra_wait)
                if tr_k[0] >= 2:
                    w.append((sv, tr_done[tr_k[0] - 2]))
                em.on("tensor", lambda e, src_ap=src_ap, col=col:
                      e.transpose(ps_sc[:, col:col + 128], src_ap, C.ident[:, :]),
                      wait=w, inc=(st, 1))
                tv = st.n
                em.on("vector", lambda e, dst_ap=dst_ap, col=col:
                      e.tensor_copy(dst_ap, ps_sc[:, col:col + 128]),
                      wait=[(st, tv)], inc=(sv, 1))
                tr_done.append(sv.n)
                tr_k[0] += 1

            wnm = "l0_wo" if layer == 0 else "l1_wo"
            if layer == 1:
                row1 = kb.sb("l1a_row1", [1, D], F32, stack=ps)
                em.dma("sync", lambda e: e.dma_start(out=row1[0:1, :], in_=I["l1_bo"].rearrange("(a d) -> a d", a=1)),
                       sem=sd)
                rowdma = sd.n
                for j in range(0, D, 512):
                    em.on("tensor", lambda e, j=j:
                          e.matmul(ps_sc[:, 0:512], C.onesrow[0:1, 0:128], row1[0:1, j:j + 512],
                                   start=True, stop=True),
                          wait=[(sd, rowdma), (sv, sv.n)], inc=(st, 1))
                    tv = st.n
                    em.on("vector", lambda e, j=j: e.tensor_copy(C.bo_bc[:, j:j + 512], ps_sc[:, 0:512]),
                          wait=[(st, tv)], inc=(sv, 1))
            for c in range(8):
                if layer == 0:
                    em.dma("sync", lambda e, c=c:
                           e.dma_start(out=wo[:, c * D:(c + 1) * D], in_=I[wnm][c * 128:(c + 1) * 128, :]),
                           sem=sd)
                else:
                    em.dma("sync", lambda e, c=c:
                           e.dma_start(out=qwin[:, 0:1024], in_=I[wnm][c * 128:(c + 1) * 128, :]),
                           wait=[(st, st.n)], sem=sd)
                    need = sd.n
                    for j in range(8):
                        transpose_tile(qwin[:, j * 128:(j + 1) * 128],
                                       wo[:, j * D + c * 128: j * D + c * 128 + 128],
                                       extra_wait=[(sd, need)] if j == 0 else ())
            wo_dma = sd.n
            wo_ready = sv.n

            def qbulk(e):
                par = kb.rtv(e, "gpsimd", "par")
                src = kb.dQT.rearrange("(i b) d q -> i b d q", b=2)[:, bass.ds(par, 1), :, :]
                return e.dma_start(out=kb.dQTown[:, :, :].rearrange("i (a d) q -> i a d q", a=1), in_=src)
            em.dma("gpsimd", qbulk, wait=[(so, so.n)], sem=sqd)
            qbulk_done = sqd.n

            nsup_of = (lambda w: 2 * w + 2) if layer == 0 else (lambda w: 8)

            ksup_free = [0, 0]
            vsup_free = [0, 0]
            num_free = [0, 0, 0]
            num_written = [0, 0, 0]
            sig_written = [0, 0]
            qwin_free = [0]

            for qw in range(4):
                for jq in range(4):
                    def qload(e, qw=qw, jq=jq):
                        src = kb.dQTown[4 * qw + jq, :, :].rearrange("(c p) q -> p c q", p=128)
                        dst = qwin[:, :].rearrange("p (c q) -> p c q", c=8)[:, :, 128 * jq:128 * jq + 128]
                        return e.dma_start(out=dst, in_=src)
                    em.dma("gpsimd", qload,
                           wait=([(st, qwin_free[0])] if jq == 0 else [])
                           + ([(sqd, qbulk_done), (sv, wo_ready)] if qw == 0 and jq == 0 else []),
                           sem=sqd)
                qwin_dma = sqd.n

                nsup = nsup_of(qw)
                for sup in range(nsup):
                    buf = sup % 2
                    em.dma("sync", lambda e, sup=sup, buf=buf:
                           e.dma_start(out=ksup[buf][:, :].rearrange("p (c t) -> p c t", c=8),
                                       in_=kb.dKT[sup, :, :].rearrange("(c p) t -> p c t", p=128)),
                           wait=[(st, ksup_free[buf])], sem=sd)
                    em.dma("sync", lambda e, sup=sup, buf=buf:
                           e.dma_start(out=vsup[buf][:, :].rearrange("p (s v) -> p s v", s=4),
                                       in_=kb.dV[sup, :, :].rearrange("(s p) v -> p s v", p=128)),
                           wait=[(st, vsup_free[buf])], sem=sd)
                    kv_dma = sd.n

                    for hp in range(8):
                        hA, hB = 2 * hp, 2 * hp + 1

                        def scores(s, em=em, qw=qw, sup=sup, hp=hp, buf=buf):
                            slot = (4 * hp + s) % 3
                            w = []
                            if hp == 0 and s <= 1:
                                w.append((sd, kv_dma))
                                if sup == 0 and s == 0:
                                    w.append((sqd, qwin_dma))
                            if num_written[slot]:
                                w.append((ssc if layer == 0 else sv, num_written[slot]))
                            for hh in range(2):
                                h = 2 * hp + hh
                                rb = (h % 2) * 64
                                ct = h // 2
                                em.on("tensor", lambda e, h=h, rb=rb, ct=ct, s=s, slot=slot, buf=buf, hh=hh:
                                      e.matmul(ps_sc[:, slot * 1024 + hh * 512: slot * 1024 + hh * 512 + 512],
                                               ksup[buf][rb:rb + 64, ct * 512 + 128 * s: ct * 512 + 128 * s + 128],
                                               qwin[rb:rb + 64, ct * 512:(ct + 1) * 512],
                                               start=True, stop=True),
                                      wait=w if hh == 0 else (), inc=(st, 1))
                            return st.n

                        def numf(s, em=em, qw=qw, sup=sup, hp=hp, sc_done=None):
                            slot = (4 * hp + s) % 3
                            kc = 4 * sup + s
                            if layer == 0:
                                em.on("scalar", lambda e, slot=slot:
                                      e.activation(num[slot][:, :], ps_sc[:, slot * 1024:slot * 1024 + 1024],
                                                   AF.Exp, scale=C.fc[:, 2:3]),
                                      wait=[(st, sc_done), (st, num_free[slot])], inc=(ssc, 1))
                                num_written[slot] = ssc.n
                                done_sem, done_val = ssc, ssc.n
                                mops = []
                                for jj in range(4):
                                    Ga = 8 * qw + 2 * jj
                                    if kc == Ga:
                                        mops.append((jj, C.maskA))
                                    elif kc == Ga + 1:
                                        mops.append((jj, C.maskB))
                                    elif kc > Ga + 1:
                                        mops.append((jj, None))
                                for jj, msk in mops:
                                    for hh in range(2):
                                        cl = hh * 512 + 128 * jj
                                        if msk is None:
                                            em.on("vector", lambda e, slot=slot, cl=cl:
                                                  e.memset(num[slot][:, cl:cl + 128], 0.0),
                                                  wait=[(ssc, num_written[slot])], inc=(sv, 1))
                                        else:
                                            em.on("vector", lambda e, slot=slot, cl=cl, msk=msk:
                                                  e.tensor_mul(num[slot][:, cl:cl + 128], num[slot][:, cl:cl + 128], msk[:, :]),
                                                  wait=[(ssc, num_written[slot])], inc=(sv, 1))
                                if mops:
                                    done_sem, done_val = sv, sv.n
                                return (done_sem, done_val)
                            else:
                                sl2 = (4 * hp + s) % 2
                                em.on("scalar", lambda e, slot=slot, sl2=sl2:
                                      e.activation(sig[sl2][:, :], ps_sc[:, slot * 1024:slot * 1024 + 1024],
                                                   AF.Sigmoid, scale=C.fc[:, 2:3], bias=C.negmu[:, 0:1]),
                                      wait=[(st, sc_done), (sv, num_written[slot]), (st, num_free[slot]),
                                            (sv, sig_written[sl2])],
                                      inc=(ssc, 1))
                                em.on("scalar", lambda e, slot=slot:
                                      e.activation(num[slot][:, :], ps_sc[:, slot * 1024:slot * 1024 + 1024],
                                                   AF.Exp, scale=C.fc[:, 2:3]),
                                      inc=(ssc, 1))
                                sc2 = ssc.n
                                em.on("vector", lambda e, slot=slot, sl2=sl2:
                                      e.tensor_mul(num[slot][:, :], num[slot][:, :], sig[sl2][:, :]),
                                      wait=[(ssc, sc2)], inc=(sv, 1))
                                num_written[slot] = sv.n
                                sig_written[sl2] = sv.n
                                return (sv, sv.n)

                        def av(s, ndone, em=em, sup=sup, hp=hp, buf=buf, qw=qw):
                            slot = (4 * hp + s) % 3
                            w = [(ndone[0], ndone[1])]
                            if s == 0:
                                w.append((sv, u_evac_done[0]))
                            for hh in range(2):
                                h = 2 * hp + hh
                                em.on("tensor", lambda e, h=h, hh=hh, s=s, slot=slot, buf=buf:
                                      e.matmul(ps_u[0:65, hh * 512:(hh + 1) * 512],
                                               vsup[buf][:, s * 1040 + 65 * h: s * 1040 + 65 * h + 65],
                                               num[slot][:, hh * 512:(hh + 1) * 512],
                                               start=(s == 0), stop=(s == 3)),
                                      wait=w if hh == 0 else (), inc=(st, 1))
                            num_free[slot] = st.n
                            return st.n

                        u_evac_done = getattr(kb, "_ued_" + L, None)
                        if u_evac_done is None:
                            u_evac_done = [0]
                            setattr(kb, "_ued_" + L, u_evac_done)

                        scd = {}
                        nd = {}
                        scd[0] = scores(0)
                        nd[0] = numf(0, sc_done=scd[0])
                        scd[1] = scores(1)
                        av(0, nd[0])
                        nd[1] = numf(1, sc_done=scd[1])
                        scd[2] = scores(2)
                        av(1, nd[1])
                        nd[2] = numf(2, sc_done=scd[2])
                        scd[3] = scores(3)
                        av(2, nd[2])
                        nd[3] = numf(3, sc_done=scd[3])
                        av_done = av(3, nd[3])
                        for hh in range(2):
                            h = 2 * hp + hh
                            if sup == 0:
                                em.on("vector", lambda e, h=h, hh=hh:
                                      e.tensor_copy(U_sb[0:65, 512 * h:512 * h + 512], ps_u[0:65, hh * 512:hh * 512 + 512]),
                                      wait=[(st, av_done)], inc=(sv, 1))
                            else:
                                em.on("vector", lambda e, h=h, hh=hh:
                                      e.tensor_tensor(U_sb[0:65, 512 * h:512 * h + 512],
                                                      U_sb[0:65, 512 * h:512 * h + 512],
                                                      ps_u[0:65, hh * 512:hh * 512 + 512], ALU.add),
                                      wait=[(st, av_done)], inc=(sv, 1))
                        u_evac_done[0] = sv.n
                    ksup_free[buf] = st.n
                    vsup_free[buf] = st.n
                qwin_free[0] = st.n

                uev = u_evac_done[0]
                if qw == 0 and layer == 0 and "USB0" in kb.dbg:
                    em.dma("gpsimd", lambda e: e.dma_start(out=kb.dbg["USB0"], in_=U_sb[:, :]),
                           wait=[(sv, uev)], sem=so)
                    em.on("vector", None, wait=[(so, so.n)])
                em.on("vector", lambda e: e.reciprocal(U_sb[64:65, :], U_sb[64:65, :]),
                      wait=[(sv, uev)], inc=(sv, 1))
                for c in range(16):
                    em.on("tensor", lambda e, c=c:
                          e.matmul(ps_sc[0:64, 0:512], C.onesrow[64:65, 0:64],
                                   U_sb[64:65, c * 512:(c + 1) * 512],
                                   start=True, stop=True),
                          wait=[(sv, sv.n)], inc=(st, 1))
                    tv = st.n
                    em.on("vector", lambda e, c=c:
                          e.tensor_tensor(U_sb[0:64, c * 512:(c + 1) * 512],
                                          U_sb[0:64, c * 512:(c + 1) * 512],
                                          ps_sc[0:64, 0:512], ALU.mult),
                          wait=[(st, tv)], inc=(sv, 1))
                un_done = sv.n
                if qw == 0 and layer == 0 and "UN0" in kb.dbg:
                    em.dma("gpsimd", lambda e: e.dma_start(out=kb.dbg["UN0"], in_=U_sb[:, :]),
                           wait=[(sv, un_done)], sem=so)
                    em.on("vector", None, wait=[(so, so.n)])
                for hp in range(8):
                    em.on("tensor", lambda e, hp=hp:
                          e.matmul(ps_sc[:, 1024:1536], C.ident[0:64, 0:128],
                                   U_sb[0:64, 512 * (2 * hp):512 * (2 * hp) + 512],
                                   start=True, stop=False),
                          wait=[(sv, un_done)] if hp == 0 else [(sv, sv.n)], inc=(st, 1))
                    em.on("tensor", lambda e, hp=hp:
                          e.matmul(ps_sc[:, 1024:1536], C.identB[0:64, 0:128],
                                   U_sb[0:64, 512 * (2 * hp + 1):512 * (2 * hp + 1) + 512],
                                   start=False, stop=True),
                          inc=(st, 1))
                    tv = st.n
                    em.on("vector", lambda e, hp=hp:
                          e.tensor_copy(U_sb[:, 1024 * hp:1024 * hp + 512], ps_sc[:, 1024:1536]),
                          wait=[(st, tv)], inc=(sv, 1))
                pk_done = sv.n
                if qw == 0 and layer == 0 and "UPK0" in kb.dbg:
                    em.dma("gpsimd", lambda e: e.dma_start(out=kb.dbg["UPK0"], in_=U_sb[:, :]),
                           wait=[(sv, pk_done)], sem=so)
                    em.on("vector", None, wait=[(so, so.n)])
                for jq in range(4):
                    for dw in range(2):
                        pb = dw
                        for hp in range(8):
                            w = []
                            if hp == 0:
                                w = [(sv, pk_done), (sv, sv.n)]
                                if jq == 0 and dw == 0:
                                    w.append((sd, wo_dma))
                            em.on("tensor", lambda e, jq=jq, dw=dw, hp=hp, pb=pb:
                                  e.matmul(ps_u[:, pb * 512:pb * 512 + 512],
                                           U_sb[:, 1024 * hp + 128 * jq: 1024 * hp + 128 * jq + 128],
                                           wo[:, 1024 * hp + 512 * dw: 1024 * hp + 512 * dw + 512],
                                           start=(hp == 0), stop=(hp == 7)),
                                  wait=w, inc=(st, 1) if hp == 7 else None)
                        tv = st.n
                        em.on("vector", lambda e, dw=dw, pb=pb:
                              e.tensor_tensor(aevac[:, :], ps_u[:, pb * 512:pb * 512 + 512],
                                              C.bo_bc[:, dw * 512:(dw + 1) * 512], ALU.add),
                              wait=[(st, tv)], inc=(sv, 1))
                        ev = sv.n
                        em.dma("gpsimd", lambda e, qw=qw, jq=jq, dw=dw:
                               e.dma_start(out=kb.dA[4 * qw + jq, :, 512 * dw:512 * dw + 512], in_=aevac[:, :]),
                               wait=[(sv, ev)], sem=so)
                        em.on("vector", None, wait=[(so, so.n)])
                u_evac_done[0] = sv.n
                ksup_free[0] = st.n
                ksup_free[1] = st.n
                vsup_free[0] = st.n
                vsup_free[1] = st.n
                num_written[0] = 0
                num_written[1] = 0

            em.on("gpsimd", None, wait=[(so, so.n), (sqd, sqd.n)])
            em.on("sync", None, wait=[(sd, sd.n)])


def build_moe(kb):
    nc, C, I, sem = kb.nc, kb.C, kb.I, kb.sem
    with ExitStack() as ps:
        sbl = lambda n, s, d=F32: kb.sb("m_" + n, s, d, stack=ps)
        zT = sbl("zT", [128, 8 * 2048])
        xo = sbl("xo", [128, D])
        xn = sbl("xn", [128, D])
        zb = sbl("zb", [128, D])
        sq = sbl("sq", [128, D])
        stat = sbl("stat", [128, 8])
        asb = sbl("asb", [128, D])
        lsb = sbl("lsb", [128, 8])
        tp2 = sbl("tp2", [128, 8])
        w_sb = sbl("w_sb", [128, 8 * 16])
        wrow = sbl("wrow", [1, 256])
        b2f = sbl("b2f", [1, 8192])
        wbc = sbl("wbc", [128, 256])
        rw_sb = sbl("rw", [128, 64])
        w1b = [sbl(f"w1b{i}", [128, 8 * 512]) for i in range(2)]
        w2b = [sbl(f"w2b{i}", [128, 1024]) for i in range(2)]
        gbuf = sbl("gbuf", [128, 768])
        x1b = sbl("x1b", [128, 512])
        ps_h = ps.enter_context(nc.psum_tensor("m_h", [128, 512], F32))
        ps_h2 = ps.enter_context(nc.psum_tensor("m_h2", [128, 512], F32))
        ps_h3 = ps.enter_context(nc.psum_tensor("m_h3", [128, 512], F32))
        kb.ps0 = ps.enter_context(nc.psum_tensor("m_ps0", [128, 512], F32))
        ps_x1 = ps.enter_context(nc.psum_tensor("m_x1", [128, 2048], F32))

        sd = sem("m_dma")
        sg = sem("m_gdma")
        so = sem("m_odma")
        sv = sem("m_vec")
        st = sem("m_ten")
        ssc = sem("m_scl")
        swd = sem("m_w2d")
        kb.cur_sv = "m_vec"
        kb.cur_ssc = "m_scl"

        with Phase(kb) as em:
            gb_free = [0]

            def xbulk(e):
                p128 = kb.rtv(e, "gpsimd", "par128")
                src = I["xseq"].rearrange("(i r) d -> i r d", r=256)[:, bass.ds(p128, 128), :]
                return e.dma_start(out=kb.xown[:, :, :], in_=src)
            em.dma("gpsimd", xbulk, sem=sg)
            xbulk_done = sg.n
            em.dma("sync", lambda e: e.dma_start(out=b2f[0:1, :],
                                                 in_=I["b2"].rearrange("e d -> (e d)").rearrange("(a x) -> a x", a=1)),
                   sem=sd)
            for dc in range(8):
                em.dma("sync", lambda e, dc=dc: e.dma_start(out=rw_sb[:, dc * 8:(dc + 1) * 8],
                                                            in_=I["rW"][dc * 128:(dc + 1) * 128, :]),
                       sem=sd)
            tr_k = [0]
            tr_copy_done = []

            def transpose_tile(src_ap, dst_ap, extra_wait=(), psslot=ps_h):
                col = 0
                w = list(extra_wait)
                if tr_k[0] >= 1:
                    w.append((sv, tr_copy_done[tr_k[0] - 1]))
                em.on("tensor", lambda e, src_ap=src_ap:
                      e.transpose(ps_h[:, 0:128], src_ap, C.ident[:, :]),
                      wait=w, inc=(st, 1))
                tv = st.n
                em.on("vector", lambda e, dst_ap=dst_ap:
                      e.tensor_copy(dst_ap, ps_h[:, 0:128]),
                      wait=[(st, tv)], inc=(sv, 1))
                tr_copy_done.append(sv.n)
                tr_k[0] += 1

            for i in range(OB):
                em.dma("gpsimd", lambda e, i=i: e.dma_start(out=xo[:, :], in_=kb.xown[i, :, :]),
                       wait=[(sv, sv.n), (ssc, ssc.n)] + ([(sg, xbulk_done)] if i == 0 else []), sem=sg)
                em.dma("sync", lambda e, i=i: e.dma_start(out=asb[:, :], in_=kb.dA[i, :, :]),
                       wait=[(sv, sv.n)], sem=sd)
                ld = sg.n
                ld2 = sd.n
                ln_steps(kb, em, xo[:, :], xn[:, :], stat[:, 0:1], stat[:, 1:2], sq, 1e-5,
                         first_wait=[(sg, ld)])
                em.on("vector", lambda e: e.tensor_add(asb[:, :], asb[:, :], xo[:, :]),
                      wait=[(sd, ld2)], inc=(sv, 1))
                em.on("vector", lambda e: e.tensor_add(asb[:, :], asb[:, :], xn[:, :]), inc=(sv, 1))
                asd = sv.n
                em.dma("gpsimd", lambda e, i=i: e.dma_start(out=kb.dAS[i, :, :], in_=asb[:, :]),
                       wait=[(sv, asd)], sem=so)
                ln_steps(kb, em, xn[:, :], zb[:, :], stat[:, 2:3], stat[:, 3:4], sq, 1e-5)
                em.on("vector", lambda e: e.tensor_mul(zb[:, :], zb[:, :], C.lng_bc[:, :]), inc=(sv, 1))
                em.on("vector", lambda e: e.tensor_add(zb[:, :], zb[:, :], C.lnb_bc[:, :]), inc=(sv, 1))
                z_done = sv.n
                for dc in range(8):
                    transpose_tile(zb[:, dc * 128:(dc + 1) * 128],
                                   zT[:, dc * 2048 + 128 * i: dc * 2048 + 128 * i + 128],
                                   extra_wait=[(sv, z_done)] if dc == 0 else ())
                ztr = sv.n
                for dc in range(8):
                    em.on("tensor", lambda e, dc=dc, i=i:
                          e.matmul(ps_h2[:, 0:8],
                                   zT[:, dc * 2048 + 128 * i: dc * 2048 + 128 * i + 128],
                                   rw_sb[:, dc * 8:(dc + 1) * 8],
                                   start=(dc == 0), stop=(dc == 7)),
                          wait=[(sv, ztr), (sv, sv.n)] if dc == 0 else (), inc=(st, 1) if dc == 7 else None)
                tv = st.n
                em.on("vector", lambda e: e.tensor_tensor(lsb[:, :], ps_h2[:, 0:8], C.rb_bc[:, :], ALU.add),
                      wait=[(st, tv)], inc=(sv, 1))
                em.on("vector", lambda e: e.reduce_max(stat[:, 4:5], lsb[:, :], AX.X, negate=True), inc=(sv, 1))
                pv = sv.n
                em.on("scalar", lambda e: e.activation(lsb[:, :], lsb[:, :], AF.Exp, bias=stat[:, 4:5]),
                      wait=[(sv, pv)], inc=(ssc, 1))
                sc0 = ssc.n
                em.on("vector", lambda e: e.reduce_sum(stat[:, 5:6], lsb[:, :], AX.X),
                      wait=[(ssc, sc0)], inc=(sv, 1))
                em.on("vector", lambda e: e.reciprocal(stat[:, 5:6], stat[:, 5:6]), inc=(sv, 1))
                em.on("vector", lambda e: e.tensor_scalar_mul(lsb[:, :], lsb[:, :], stat[:, 5:6]), inc=(sv, 1))
                em.on("vector", lambda e: e.reduce_max(stat[:, 4:5], lsb[:, :], AX.X), inc=(sv, 1))
                em.on("vector", lambda e: e.tensor_single_scalar(tp2[:, :], lsb[:, :], stat[:, 4:5], ALU.is_ge), inc=(sv, 1))
                em.on("vector", lambda e: e.reduce_sum(stat[:, 6:7], tp2[:, :], AX.X), inc=(sv, 1))
                em.on("vector", lambda e: e.scalar_tensor_tensor(tp2[:, :], tp2[:, :], -1e9, lsb[:, :], ALU.mult, ALU.add), inc=(sv, 1))
                em.on("vector", lambda e: e.reduce_max(stat[:, 5:6], tp2[:, :], AX.X), inc=(sv, 1))
                em.on("vector", lambda e: e.tensor_single_scalar(stat[:, 6:7], stat[:, 6:7], 1.5, ALU.is_ge), inc=(sv, 1))
                em.on("vector", lambda e: e.tensor_tensor(stat[:, 7:8], stat[:, 4:5], stat[:, 5:6], ALU.subtract), inc=(sv, 1))
                em.on("vector", lambda e: e.tensor_tensor(stat[:, 6:7], stat[:, 6:7], stat[:, 7:8], ALU.mult), inc=(sv, 1))
                em.on("vector", lambda e: e.tensor_tensor(stat[:, 6:7], stat[:, 6:7], stat[:, 5:6], ALU.add), inc=(sv, 1))
                em.on("vector", lambda e: e.tensor_single_scalar(tp2[:, :], lsb[:, :], stat[:, 6:7], ALU.is_ge), inc=(sv, 1))
                em.on("vector", lambda e: e.tensor_mul(tp2[:, :], tp2[:, :], lsb[:, :]), inc=(sv, 1))
                em.on("vector", lambda e: e.reduce_sum(stat[:, 7:8], tp2[:, :], AX.X), inc=(sv, 1))
                em.on("vector", lambda e: e.tensor_scalar_max(stat[:, 7:8], stat[:, 7:8], 1e-8), inc=(sv, 1))
                em.on("vector", lambda e: e.reciprocal(stat[:, 7:8], stat[:, 7:8]), inc=(sv, 1))
                em.on("vector", lambda e, i=i: e.tensor_scalar_mul(w_sb[:, 8 * i:8 * i + 8], tp2[:, :], stat[:, 7:8]), inc=(sv, 1))
                wv_done = sv.n

            if "W8" in kb.dbg:
                em.dma("gpsimd", lambda e: e.dma_start(out=kb.dbg["W8"], in_=w_sb[:, :]),
                       wait=[(sv, sv.n)], sem=so)
            w1_free = [0, 0]
            w2_free = [0, 0]
            gb_free2 = [0, 0, 0]
            psh_free = [0, 0, 0]
            pshs = None
            pend = []

            def emit_mm2():
                if not pend:
                    return
                ft, b2_, gw_done, w2_dma, first = pend.pop()
                gslot = ft % 3
                for ts in range(2):
                    for dw in range(2):
                        em.on("tensor", lambda e, ts=ts, dw=dw, b2_=b2_, first=first, gslot=gslot:
                              e.matmul(ps_x1[:, (2 * ts + dw) * 512:(2 * ts + dw) * 512 + 512],
                                       gbuf[:, 256 * gslot + 128 * ts: 256 * gslot + 128 * ts + 128],
                                       w2b[b2_][:, 512 * dw:512 * dw + 512],
                                       start=first, stop=False),
                              wait=[(sv, gw_done), (swd, w2_dma)] if (ts == 0 and dw == 0) else (),
                              inc=(st, 1))
                w2_free[b2_] = st.n
                gb_free2[ft % 3] = st.n

            for twin in range(8):
                for e_ in range(8):
                    for ts in range(2):
                        i_b = 2 * twin + ts
                        em.on("tensor", lambda e, e_=e_, i_b=i_b:
                              e.transpose(ps_h2[0:1, 256:384], w_sb[:, 8 * i_b + e_:8 * i_b + e_ + 1], C.ident[:, :]),
                              wait=[(sv, sv.n)], inc=(st, 1))
                        tv = st.n
                        em.on("vector", lambda e, ts=ts: e.tensor_copy(wrow[0:1, 128 * ts:128 * ts + 128], ps_h2[0:1, 256:384]),
                              wait=[(st, tv)], inc=(sv, 1))
                    wrow_done = sv.n
                    em.on("tensor", lambda e:
                          e.matmul(kb.ps0[:, 0:256], C.onesrow[0:1, 0:128], wrow[0:1, 0:256],
                                   start=True, stop=True),
                          wait=[(sv, wrow_done)], inc=(st, 1))
                    tv = st.n
                    em.on("vector", lambda e: e.tensor_copy(wbc[:, :], kb.ps0[:, 0:256]),
                          wait=[(st, tv)], inc=(sv, 1))
                    wbc_done = sv.n
                    for fw in range(4):
                        b1_ = (e_ * 4 + fw) % 2
                        em.dma("sync", lambda e, e_=e_, fw=fw, b1_=b1_:
                               e.dma_start(out=w1b[b1_][:, :].rearrange("p (c f) -> p c f", c=8),
                                           in_=I["W1"][e_, :, 512 * fw:512 * fw + 512].rearrange("(c p) f -> p c f", p=128)),
                               wait=[(st, w1_free[b1_])], sem=sd)
                        w1_dma = sd.n
                        for fi in range(4):
                            ft = 4 * fw + fi
                            b2_ = ft % 2
                            em.dma("gpsimd", lambda e, e_=e_, ft=ft, b2_=b2_:
                                   e.dma_start(out=w2b[b2_][:, :], in_=I["W2"][e_, 128 * ft:128 * ft + 128, :]),
                                   wait=[(st, w2_free[b2_])], sem=swd)
                            w2_dma = swd.n
                            psh = (ps_h, ps_h2, ps_h3)[ft % 3]
                            for dc in range(8):
                                w = []
                                if dc == 0:
                                    w = [(sd, w1_dma), (ssc, psh_free[ft % 3])]
                                em.on("tensor", lambda e, dc=dc, fi=fi, twin=twin, psh=psh, b1_=b1_:
                                      e.matmul(psh[:, 0:256],
                                               w1b[b1_][:, dc * 512 + 128 * fi: dc * 512 + 128 * fi + 128],
                                               zT[:, dc * 2048 + 256 * twin: dc * 2048 + 256 * twin + 256],
                                               start=(dc == 0), stop=(dc == 7)),
                                      wait=w, inc=(st, 1) if dc == 7 else None)
                            tv = st.n
                            emit_mm2()
                            gslot = ft % 3
                            em.on("scalar", lambda e, e_=e_, ft=ft, psh=psh, gslot=gslot:
                                  e.activation(gbuf[:, 256 * gslot:256 * gslot + 256], psh[:, 0:256],
                                               AF.Gelu, bias=C.b1[:, 16 * e_ + ft:16 * e_ + ft + 1]),
                                  wait=[(st, tv), (st, gb_free2[gslot])], inc=(ssc, 1))
                            psh_free[ft % 3] = ssc.n
                            sc0 = ssc.n
                            em.on("vector", lambda e, gslot=gslot:
                                  e.tensor_mul(gbuf[:, 256 * gslot:256 * gslot + 256],
                                               gbuf[:, 256 * gslot:256 * gslot + 256], wbc[:, :]),
                                  wait=[(ssc, sc0), (sv, wbc_done)], inc=(sv, 1))
                            gw_done = sv.n
                            pend.append((ft, b2_, gw_done, w2_dma, (e_ == 0 and ft == 0)))
                        w1_free[b1_] = st.n
                    emit_mm2()
                    for ts in range(2):
                        for dw in range(2):
                            em.on("tensor", lambda e, ts=ts, dw=dw, e_=e_, twin=twin:
                                  e.matmul(ps_x1[:, (2 * ts + dw) * 512:(2 * ts + dw) * 512 + 512],
                                           wrow[0:1, 128 * ts:128 * ts + 128],
                                           b2f[0:1, 1024 * e_ + 512 * dw:1024 * e_ + 512 * dw + 512],
                                           start=False, stop=(e_ == 7)),
                                  inc=(st, 1))
                mix_done = st.n
                for ts in range(2):
                    i = 2 * twin + ts
                    em.dma("sync", lambda e, i=i: e.dma_start(out=asb[:, :], in_=kb.dAS[i, :, :]),
                           wait=[(sv, sv.n), (so, so.n)], sem=sd)
                    asd = sd.n
                    for dw in range(2):
                        em.on("vector", lambda e, ts=ts, dw=dw:
                              e.tensor_tensor(x1b[:, :], ps_x1[:, (2 * ts + dw) * 512:(2 * ts + dw) * 512 + 512],
                                              asb[:, dw * 512:dw * 512 + 512], ALU.add),
                              wait=[(st, mix_done), (sd, asd)], inc=(sv, 1))
                        ev = sv.n
                        em.dma("gpsimd", lambda e, i=i, dw=dw:
                               e.dma_start(out=kb.x1own[i, :, 512 * dw:512 * dw + 512], in_=x1b[:, :]),
                               wait=[(sv, ev)], sem=so)
                        em.on("vector", None, wait=[(so, so.n)])

            def x1bulk(e):
                p2048 = kb.rtv(e, "gpsimd", "par2048")
                dst = kb.xsh[0:T + 1, :][bass.ds(p2048, 2048), :]
                return e.dma_start(out=dst, in_=kb.x1own[:, :, :].rearrange("i r d -> (i r) d"))
            em.dma("gpsimd", x1bulk, wait=[(so, so.n)], sem=so)
            em.on("gpsimd", None, wait=[(so, so.n), (sg, sg.n), (swd, swd.n)])
            em.on("sync", None, wait=[(sd, sd.n)])


def build_exchange(kb):
    nc, C, sem = kb.nc, kb.C, kb.sem
    with ExitStack() as ps:
        imp = kb.sb("ex_imp", [128, 4096], F32, stack=ps)
        _order = ("gpsimd", "sync", "vector", "scalar", "tensor")
        zr = kb.sb("ex_zr", [4, D], F32, stack=ps)
        sg = sem("ex_dma")
        sv = sem("ex_vec")
        with Phase(kb) as em:
            em.engine_order = _order
            em.on("vector", lambda e: e.memset(zr[:, :], 0.0), inc=(sv, 1))
            zv = sv.n
            em.dma("gpsimd", lambda e: e.dma_start(out=kb.x1pad[0:1, :], in_=zr[0:1, :]),
                   wait=[(sv, zv)], sem=sg)
            em.dma("gpsimd", lambda e: e.dma_start(out=kb.x1pad[T + 1:T + 5, :], in_=zr[:, :]),
                   sem=sg)
            if EXC_MODE != "nobarrier":
                bsem = sem("ex_bar")
                em.dma("gpsimd", lambda e: e.dma_start(out=kb.bar_in[0:1, :], in_=zr[0:1, 0:16]),
                       wait=[(sv, zv)], sem=sg)
                em.on("gpsimd", lambda e: e.collective_compute(
                          "AllReduce", ALU.add, replica_groups=[list(range(8))],
                          ins=[kb.bar_in.opt()], outs=[kb.bar_out.opt()]),
                      wait=[(sg, sg.n)], inc=(bsem, 1), kind="dma")
                em.on("gpsimd", None, wait=[(bsem, bsem.n)])
            if "MRK" in kb.dbg:
                def marker(e):
                    p2048 = kb.rtv(e, "gpsimd", "par2048")
                    return e.dma_start(out=kb.xsh[0:T + 1, :][bass.ds(p2048, 1), 0:16],
                                       in_=C.onesrow[0:1, 0:16])
                em.dma("gpsimd", marker, wait=[(sg, sg.n)], sem=sg)
            if EXC_MODE != "noimport":
                def imp_read(e):
                    zsv = kb.rtv(e, "gpsimd", "zsv")
                    tru = kb.rtv(e, "gpsimd", "truthy")
                    src = kb.xsh[0:T + 1, :][bass.ds(zsv, T), :]
                    return e.dma_start(out=kb.x1tmp[:, :], in_=src, cond=tru)
                em.dma("gpsimd", imp_read, wait=[(sg, sg.n)], sem=sg)
                for g in range(NB):
                    srow = 2048 * (g % 2) + 128 * (g // 2)
                    em.dma("gpsimd", lambda e, g=g, srow=srow:
                           e.dma_start(out=kb.x1pad[1 + 128 * g:1 + 128 * g + 128, :],
                                       in_=kb.x1tmp[srow:srow + 128, :]),
                           wait=[(sg, sg.n)] if g == 0 else (), sem=sg)
            em.on("gpsimd", None, wait=[(sg, sg.n)])


def build_bitffn(kb):
    nc, C, I, sem = kb.nc, kb.C, kb.I, kb.sem
    with ExitStack() as ps:
        sbl = lambda n, s_, d=F32: kb.sb("b_" + n, s_, d, stack=ps)
        sd = sem("b_dma")
        sp7 = sem("b_pool")
        sg = sem("b_gdma")
        so = sem("b_odma")
        sv = sem("b_vec")
        st = sem("b_ten")
        ssc = sem("b_scl")
        kb.cur_sv = "b_vec"
        kb.cur_ssc = "b_scl"

        kb.ps_b = ps.enter_context(nc.psum_tensor("ps_b", [128, 1024], F32))
        kb.ps_h7 = ps.enter_context(nc.psum_tensor("ps_h7", [128, 1024], F32))
        kb.ps_o7 = ps.enter_context(nc.psum_tensor("ps_o7", [128, 1024], F32))
        rs_w1_bc = sbl("rsw1", [128, 1])
        rs_w2_bc = sbl("rsw2", [128, 1])
        stat = sbl("stat", [128, 8])
        sc11 = sbl("sc11", [1, 4])

        with ExitStack() as pq:
            sbq = lambda n, s_, d=F32: kb.sb("bq_" + n, s_, d, stack=pq)
            wst = sbq("wst", [128, 4096])
            i32t = pq.enter_context(nc.sbuf_tensor("bq_i32", [128, 4096], I32))
            qf = sbq("qf", [128, 4096])
            cc = sbq("cc", [128, 4096])
            tstage = sbq("tstage", [128, 4096], BF16)
            acc = sbq("acc", [128, 2])

            with Phase(kb) as em:
                tr_k = [0]
                tr_done = []

                def transpose_tile(src_ap, dst_ap, extra_wait=(), cast=True):
                    col = (tr_k[0] % 2) * 512
                    w = list(extra_wait)
                    if tr_k[0] >= 2:
                        w.append((sv, tr_done[tr_k[0] - 2]))
                    em.on("tensor", lambda e, src_ap=src_ap, col=col:
                          e.transpose(kb.ps_b[:, col:col + 128], src_ap, C.ident[:, :]),
                          wait=w, inc=(st, 1))
                    tv = st.n
                    em.on("vector", lambda e, dst_ap=dst_ap, col=col:
                          e.tensor_copy(dst_ap, kb.ps_b[:, col:col + 128]),
                          wait=[(st, tv)], inc=(sv, 1))
                    tr_done.append(sv.n)
                    tr_k[0] += 1

                for wi, (wnm, nch, ncol, dT, accj) in enumerate(
                        (("w1f", 32, 1024, kb.dW1qT, 0), ("w2f", 8, 4096, kb.dW2qT, 1))):
                    for c in range(nch):
                        em.dma("sync", lambda e, wnm=wnm, c=c, ncol=ncol:
                               e.dma_start(out=wst[:, 0:ncol], in_=I[wnm][c * 128:(c + 1) * 128, :]),
                               wait=[(sv, sv.n)], sem=sd)
                        ld = sd.n
                        em.on("vector", lambda e, ncol=ncol: e.reduce_sum(stat[:, 0:1], wst[:, 0:ncol], AX.X,
                                                                          apply_absolute_value=True),
                              wait=[(sd, ld)], inc=(sv, 1))
                        if c == 0:
                            em.on("vector", lambda e, accj=accj: e.tensor_copy(acc[:, accj:accj + 1], stat[:, 0:1]),
                                  inc=(sv, 1))
                        else:
                            em.on("vector", lambda e, accj=accj: e.tensor_add(acc[:, accj:accj + 1],
                                                                              acc[:, accj:accj + 1], stat[:, 0:1]),
                                  inc=(sv, 1))
                    accv = sv.n
                    em.on("tensor", lambda e, accj=accj:
                          e.matmul(kb.ps_b[0:1, 512:513], acc[:, accj:accj + 1], C.ones[:, 0:1],
                                   start=True, stop=True),
                          wait=[(sv, accv), (sv, sv.n)], inc=(st, 1))
                    tv = st.n
                    em.on("vector", lambda e: e.tensor_copy(sc11[0:1, 0:1], kb.ps_b[0:1, 512:513]),
                          wait=[(st, tv)], inc=(sv, 1))
                    em.on("vector", lambda e, nch=nch, ncol=ncol:
                          e.tensor_scalar_mul(sc11[0:1, 0:1], sc11[0:1, 0:1], 1.0 / (nch * 128 * ncol)), inc=(sv, 1))
                    em.on("vector", lambda e: e.tensor_scalar_max(sc11[0:1, 0:1], sc11[0:1, 0:1], 1e-5), inc=(sv, 1))
                    em.on("vector", lambda e: e.reciprocal(sc11[0:1, 1:2], sc11[0:1, 0:1]), inc=(sv, 1))
                    em.on("vector", lambda e: e.reciprocal(sc11[0:1, 2:3], sc11[0:1, 1:2]), inc=(sv, 1))
                    scv = sv.n
                    dstb = rs_w1_bc if wi == 0 else rs_w2_bc
                    em.on("tensor", lambda e:
                          e.matmul(kb.ps_b[:, 512:513], C.onesrow[0:1, 0:128], sc11[0:1, 1:2],
                                   start=True, stop=True),
                          wait=[(sv, sv.n)], inc=(st, 1))
                    tv = st.n
                    em.on("vector", lambda e: e.tensor_copy(stat[:, 1:2], kb.ps_b[:, 512:513]),
                          wait=[(st, tv)], inc=(sv, 1))
                    em.on("tensor", lambda e:
                          e.matmul(kb.ps_b[:, 512:513], C.onesrow[0:1, 0:128], sc11[0:1, 2:3],
                                   start=True, stop=True),
                          wait=[(sv, sv.n)], inc=(st, 1))
                    tv = st.n
                    em.on("vector", lambda e, dstb=dstb: e.tensor_copy(dstb[:, 0:1], kb.ps_b[:, 512:513]),
                          wait=[(st, tv)], inc=(sv, 1))
                    for c in range(nch):
                        em.dma("sync", lambda e, wnm=wnm, c=c, ncol=ncol:
                               e.dma_start(out=wst[:, 0:ncol], in_=I[wnm][c * 128:(c + 1) * 128, :]),
                               wait=[(sv, sv.n)], sem=sd)
                        ld = sd.n
                        em.on("vector", lambda e, ncol=ncol:
                              e.tensor_scalar_mul(qf[:, 0:ncol], wst[:, 0:ncol], stat[:, 1:2]),
                              wait=[(sd, ld)], inc=(sv, 1))
                        em.on("vector", lambda e, ncol=ncol: e.tensor_copy(i32t[:, 0:ncol], qf[:, 0:ncol]), inc=(sv, 1))
                        em.on("vector", lambda e, ncol=ncol: e.tensor_copy(cc[:, 0:ncol], i32t[:, 0:ncol]), inc=(sv, 1))
                        em.on("vector", lambda e, ncol=ncol: e.tensor_sub(qf[:, 0:ncol], qf[:, 0:ncol], cc[:, 0:ncol]), inc=(sv, 1))
                        em.on("vector", lambda e, ncol=ncol: e.tensor_single_scalar(wst[:, 0:ncol], qf[:, 0:ncol], 0.5, ALU.is_ge), inc=(sv, 1))
                        em.on("vector", lambda e, ncol=ncol: e.tensor_add(cc[:, 0:ncol], cc[:, 0:ncol], wst[:, 0:ncol]), inc=(sv, 1))
                        em.on("vector", lambda e, ncol=ncol: e.tensor_single_scalar(wst[:, 0:ncol], qf[:, 0:ncol], -0.5, ALU.is_le), inc=(sv, 1))
                        em.on("vector", lambda e, ncol=ncol: e.tensor_sub(cc[:, 0:ncol], cc[:, 0:ncol], wst[:, 0:ncol]), inc=(sv, 1))
                        em.on("vector", lambda e, ncol=ncol:
                              e.tensor_scalar(cc[:, 0:ncol], cc[:, 0:ncol], -1.0, 1.0, ALU.max, ALU.min), inc=(sv, 1))
                        qv = sv.n
                        for j in range(ncol // 128):
                            transpose_tile(cc[:, j * 128:(j + 1) * 128],
                                           tstage[:, j * 128:(j + 1) * 128],
                                           extra_wait=[(sv, qv), (so, so.n)] if j == 0 else ())
                        trv = sv.n
                        em.dma("gpsimd", lambda e, c=c, dT=dT, ncol=ncol:
                               e.dma_start(out=dT.rearrange("(j p) x -> p j x", p=128)[:, :, c * 128:(c + 1) * 128],
                                           in_=tstage[:, 0:ncol].rearrange("p (j q) -> p j q", q=128)),
                               wait=[(sv, trv)], sem=so)
                em.on("gpsimd", None, wait=[(so, so.n)])
                em.on("sync", None, wait=[(sd, sd.n)])

        w1q = ps.enter_context(nc.sbuf_tensor("b_w1q", [128, 8 * 4096], BF16))
        w2qb = [ps.enter_context(nc.sbuf_tensor(f"b_w2q{i}", [128, 1024], BF16)) for i in range(2)]
        xo = sbl("xo", [128, D])
        xnb = sbl("xnb", [128, D])
        vq = sbl("vq", [128, 4096])
        i32b = ps.enter_context(nc.sbuf_tensor("b_i32b", [128, 4096], I32))
        hcb = sbl("hcb", [128, 4096])
        hext = sbl("hext", [128, 32 * 130])
        aqb = ps.enter_context(nc.sbuf_tensor("b_aqb", [128, 4096], BF16))
        xqT = ps.enter_context(nc.sbuf_tensor("b_xqT", [128, 1024], BF16))
        hT = ps.enter_context(nc.sbuf_tensor("b_hT", [128, 256], BF16))
        hh_sb = sbl("hh", [128, 1024])
        hsc = sbl("hsc", [128, 32])
        rrow = sbl("rrow", [128, 128])
        row1 = sbl("row1", [1, 128])
        asb = sbl("asb", [128, D])
        obuf = [sbl("obuf0", [128, 512]), sbl("obuf1", [128, 512])]

        with Phase(kb) as em:
            tr_k = [0]
            tr_done = []

            def transpose_tile(src_ap, dst_ap, extra_wait=(), wout=128, kin=128):
                col = (tr_k[0] % 2) * 512
                w = list(extra_wait)
                if tr_k[0] >= 2:
                    w.append((sv, tr_done[tr_k[0] - 2]))
                em.on("tensor", lambda e, src_ap=src_ap, col=col, wout=wout, kin=kin:
                      e.transpose(kb.ps_b[:, col:col + wout], src_ap, C.ident[0:kin, 0:kin]),
                      wait=w, inc=(st, 1))
                tv = st.n
                em.on("vector", lambda e, dst_ap=dst_ap, col=col, wout=wout:
                      e.tensor_copy(dst_ap, kb.ps_b[:, col:col + wout]),
                      wait=[(st, tv)], inc=(sv, 1))
                tr_done.append(sv.n)
                tr_k[0] += 1

            for dc in range(8):
                em.dma("sync", lambda e, dc=dc:
                       e.dma_start(out=w1q[:, dc * 4096:(dc + 1) * 4096], in_=kb.dW1qT[dc * 128:(dc + 1) * 128, :]),
                       sem=sd)
            w1dma = sd.n
            def xbulk(e):
                p128 = kb.rtv(e, "gpsimd", "par128")
                src = kb.x1v.rearrange("(i r) d -> i r d", r=256)[:, bass.ds(p128, 128), :]
                return e.dma_start(out=kb.xown[:, :, :], in_=src)
            em.dma("gpsimd", xbulk, sem=sg)
            xbulkv = sg.n
            def hload(e, base, dst0):
                p128 = kb.rtv(e, "gpsimd", "par128")
                src = kb.x1pad[base:base + 16 * 256, :].rearrange("(i r) d -> i r d", r=256)[:, bass.ds(p128, 1), :]
                return e.dma_start(out=xnb[dst0:dst0 + 16, :].rearrange("i (a d) -> i a d", a=1), in_=src)
            em.dma("gpsimd", lambda e: hload(e, 0, 0), sem=sg)
            em.dma("gpsimd", lambda e: hload(e, 129, 16), sem=sg)
            hloadv = sg.n

            for pb_ in (32, 64, 96):
                em.on("vector", lambda e, pb_=pb_: e.memset(xnb[pb_:pb_ + 32, :], 0.0), inc=(sv, 1))
            ln_steps(kb, em, xnb[:, :], xnb[:, :], stat[:, 0:1], stat[:, 1:2],
                     vq[:, 0:1024], 1e-5, first_wait=[(sg, hloadv)])
            em.on("vector", lambda e: e.reduce_max(stat[0:32, 2:3], xnb[0:32, :], AX.X,
                                                   apply_absolute_value=True), inc=(sv, 1))
            em.on("vector", lambda e: e.tensor_scalar_max(stat[0:32, 2:3], stat[0:32, 2:3], 1e-5), inc=(sv, 1))
            em.on("vector", lambda e: e.reciprocal(stat[0:32, 3:4], stat[0:32, 2:3]), inc=(sv, 1))
            em.on("vector", lambda e: e.tensor_scalar_mul(stat[0:32, 3:4], stat[0:32, 3:4], 127.0), inc=(sv, 1))
            em.on("vector", lambda e: e.reciprocal(stat[0:32, 4:5], stat[0:32, 3:4]), inc=(sv, 1))
            em.on("vector", lambda e: e.tensor_scalar_mul(vq[0:32, 0:1024], xnb[0:32, :], stat[0:32, 3:4]), inc=(sv, 1))
            em.on("vector", lambda e: e.tensor_copy(i32b[0:32, 0:1024], vq[0:32, 0:1024]), inc=(sv, 1))
            em.on("vector", lambda e: e.tensor_copy(hcb[0:32, 0:1024], i32b[0:32, 0:1024]), inc=(sv, 1))
            em.on("vector", lambda e: e.tensor_sub(vq[0:32, 0:1024], vq[0:32, 0:1024], hcb[0:32, 0:1024]), inc=(sv, 1))
            em.on("vector", lambda e: e.tensor_single_scalar(hext[0:32, 0:1024], vq[0:32, 0:1024], 0.5, ALU.is_ge), inc=(sv, 1))
            em.on("vector", lambda e: e.tensor_add(hcb[0:32, 0:1024], hcb[0:32, 0:1024], hext[0:32, 0:1024]), inc=(sv, 1))
            em.on("vector", lambda e: e.tensor_single_scalar(hext[0:32, 0:1024], vq[0:32, 0:1024], -0.5, ALU.is_le), inc=(sv, 1))
            em.on("vector", lambda e: e.tensor_sub(hcb[0:32, 0:1024], hcb[0:32, 0:1024], hext[0:32, 0:1024]), inc=(sv, 1))
            em.on("vector", lambda e: e.tensor_copy(xnb[0:32, :], hcb[0:32, 0:1024]), inc=(sv, 1))
            hq = sv.n
            for dc in range(8):
                transpose_tile(xnb[0:32, dc * 128:(dc + 1) * 128], hT[:, dc * 32:(dc + 1) * 32],
                               extra_wait=[(sv, hq)] if dc == 0 else (), wout=32, kin=32)
            em.on("tensor", lambda e: e.transpose(kb.ps_b[0:1, 512:544], stat[0:32, 4:5], C.ident[0:32, 0:32]),
                  wait=[(sv, sv.n)], inc=(st, 1))
            tv = st.n
            em.on("vector", lambda e: e.tensor_copy(row1[0:1, 0:32], kb.ps_b[0:1, 512:544]),
                  wait=[(st, tv)], inc=(sv, 1))
            rv = sv.n
            em.on("tensor", lambda e: e.matmul(kb.ps_b[:, 512:544], C.onesrow[0:1, 0:128], row1[0:1, 0:32],
                                               start=True, stop=True),
                  wait=[(sv, rv)], inc=(st, 1))
            tv = st.n
            em.on("vector", lambda e: e.tensor_copy(hsc[:, :], kb.ps_b[:, 512:544]),
                  wait=[(st, tv)], inc=(sv, 1))
            em.on("vector", lambda e: e.tensor_scalar_mul(hsc[:, :], hsc[:, :], rs_w1_bc[:, 0:1]), inc=(sv, 1))
            hscv = sv.n
            for ct in range(32):
                pb = ct % 2
                for dc in range(8):
                    w = []
                    if dc == 0:
                        w = [(sv, sv.n)]
                        if ct == 0:
                            w.append((sd, w1dma))
                    em.on("tensor", lambda e, ct=ct, dc=dc, pb=pb:
                          e.matmul(kb.ps_h7[:, pb * 512:pb * 512 + 32],
                                   w1q[:, dc * 4096 + 128 * ct: dc * 4096 + 128 * ct + 128],
                                   hT[:, dc * 32:(dc + 1) * 32],
                                   start=(dc == 0), stop=(dc == 7)),
                          wait=w, inc=(st, 1) if dc == 7 else None)
                tv = st.n
                em.on("vector", lambda e, ct=ct, pb=pb:
                      e.tensor_tensor(hh_sb[:, 32 * ct:32 * ct + 32], kb.ps_h7[:, pb * 512:pb * 512 + 32],
                                      hsc[:, :], ALU.mult),
                      wait=[(st, tv), (sv, hscv)], inc=(sv, 1))
            em.on("vector", lambda e: e.tensor_scalar(C.scratch1[:, :], C.par[:, :], -1.0, 1.0, ALU.mult, ALU.add),
                  inc=(sv, 1))
            hh3 = hh_sb[:, :].rearrange("p (c h) -> p c h", h=32)
            em.on("vector", lambda e: e.tensor_scalar_mul(hh3[:, :, 0:1], hh3[:, :, 0:1], C.par[:, 0:1]), inc=(sv, 1))
            em.on("vector", lambda e: e.tensor_scalar_mul(hh3[:, :, 31:32], hh3[:, :, 31:32], C.scratch1[:, 0:1]), inc=(sv, 1))
            hh_done = sv.n
            if "BH" in kb.dbg:
                em.dma("gpsimd", lambda e: e.dma_start(out=kb.dbg["BH"], in_=hh_sb[:, :]),
                       wait=[(sv, hh_done)], sem=sg)

            w2_free = [0, 0]
            obuf_free = [0, 0]
            for i in range(OB):
                em.dma("gpsimd", lambda e, i=i: e.dma_start(out=xo[:, :], in_=kb.xown[i, :, :]),
                       wait=[(sv, sv.n), (ssc, ssc.n)] + ([(sg, xbulkv)] if i == 0 else []), sem=sg)
                ldx = sg.n
                em.dma("sync", lambda e, i=i: e.dma_start(out=asb[:, :], in_=kb.dA[i, :, :]),
                       wait=[(sv, sv.n)], sem=sd)
                lda = sd.n
                ln_steps(kb, em, xo[:, :], xnb[:, :], stat[:, 0:1], stat[:, 1:2], vq[:, 0:1024], 1e-5,
                         first_wait=[(sg, ldx)])
                em.on("vector", lambda e: e.reduce_max(stat[:, 2:3], xnb[:, :], AX.X, apply_absolute_value=True), inc=(sv, 1))
                em.on("vector", lambda e: e.tensor_scalar_max(stat[:, 2:3], stat[:, 2:3], 1e-5), inc=(sv, 1))
                em.on("vector", lambda e: e.reciprocal(stat[:, 3:4], stat[:, 2:3]), inc=(sv, 1))
                em.on("vector", lambda e: e.tensor_scalar_mul(stat[:, 3:4], stat[:, 3:4], 127.0), inc=(sv, 1))
                em.on("vector", lambda e: e.reciprocal(stat[:, 4:5], stat[:, 3:4]), inc=(sv, 1))
                em.on("vector", lambda e: e.tensor_scalar_mul(vq[:, 0:1024], xnb[:, :], stat[:, 3:4]), inc=(sv, 1))
                em.on("vector", lambda e: e.tensor_copy(i32b[:, 0:1024], vq[:, 0:1024]), inc=(sv, 1))
                em.on("vector", lambda e: e.tensor_copy(hcb[:, 0:1024], i32b[:, 0:1024]), inc=(sv, 1))
                em.on("vector", lambda e: e.tensor_sub(vq[:, 0:1024], vq[:, 0:1024], hcb[:, 0:1024]), inc=(sv, 1))
                em.on("vector", lambda e: e.tensor_single_scalar(hext[:, 0:1024], vq[:, 0:1024], 0.5, ALU.is_ge), inc=(sv, 1))
                em.on("vector", lambda e: e.tensor_add(hcb[:, 0:1024], hcb[:, 0:1024], hext[:, 0:1024]), inc=(sv, 1))
                em.on("vector", lambda e: e.tensor_single_scalar(hext[:, 0:1024], vq[:, 0:1024], -0.5, ALU.is_le), inc=(sv, 1))
                em.on("vector", lambda e: e.tensor_sub(hcb[:, 0:1024], hcb[:, 0:1024], hext[:, 0:1024]), inc=(sv, 1))
                xqv = sv.n
                for dc in range(8):
                    transpose_tile(hcb[:, dc * 128:(dc + 1) * 128], xqT[:, dc * 128:(dc + 1) * 128],
                                   extra_wait=[(sv, xqv)] if dc == 0 else ())
                em.on("tensor", lambda e: e.transpose(kb.ps_b[0:1, 512:640], stat[:, 4:5], C.ident[:, :]),
                      wait=[(sv, sv.n)], inc=(st, 1))
                tv = st.n
                em.on("vector", lambda e: e.tensor_copy(row1[0:1, 0:128], kb.ps_b[0:1, 512:640]),
                      wait=[(st, tv)], inc=(sv, 1))
                rv = sv.n
                em.on("tensor", lambda e: e.matmul(kb.ps_b[:, 512:640], C.onesrow[0:1, 0:128], row1[0:1, 0:128],
                                                   start=True, stop=True),
                      wait=[(sv, rv)], inc=(st, 1))
                tv = st.n
                em.on("vector", lambda e: e.tensor_copy(rrow[:, :], kb.ps_b[:, 512:640]),
                      wait=[(st, tv), (sp7, sp7.n)], inc=(sv, 1))
                em.on("vector", lambda e: e.tensor_scalar_mul(rrow[:, :], rrow[:, :], rs_w1_bc[:, 0:1]), inc=(sv, 1))
                rrv = sv.n
                for ct in range(32):
                    pb = ct % 2
                    for dc in range(8):
                        w = [(sv, sv.n)] if dc == 0 else []
                        em.on("tensor", lambda e, ct=ct, dc=dc, pb=pb:
                              e.matmul(kb.ps_h7[:, pb * 512:pb * 512 + 128],
                                       w1q[:, dc * 4096 + 128 * ct: dc * 4096 + 128 * ct + 128],
                                       xqT[:, dc * 128:(dc + 1) * 128],
                                       start=(dc == 0), stop=(dc == 7)),
                              wait=w, inc=(st, 1) if dc == 7 else None)
                    tv = st.n
                    em.on("vector", lambda e, ct=ct, pb=pb:
                          e.tensor_tensor(hext[:, 130 * ct + 1:130 * ct + 129],
                                          kb.ps_h7[:, pb * 512:pb * 512 + 128], rrow[:, :], ALU.mult),
                          wait=[(st, tv), (sv, rrv)] + ([(sp7, sp7.n)] if ct == 0 else []), inc=(sv, 1))
                hx3 = hext[:, :].rearrange("p (c w) -> p c w", w=130)
                em.on("vector", lambda e, i=i: e.tensor_copy(hx3[:, :, 0:1], hh3[:, :, i:i + 1]),
                      wait=[(sv, hh_done)], inc=(sv, 1))
                em.on("vector", lambda e, i=i: e.tensor_copy(hx3[:, :, 129:130], hh3[:, :, 16 + i:17 + i]),
                      inc=(sv, 1))
                hx_done = sv.n
                for ct in range(32):
                    ceng = "vector"
                    csem = sv
                    w0 = []
                    em.on(ceng, lambda e, ct=ct:
                          e.tensor_scalar(hcb[:, 128 * ct:128 * ct + 128], hext[:, 130 * ct + 1:130 * ct + 129],
                                          C.cw[:, 3 * ct + 1:3 * ct + 2], C.cb[:, ct:ct + 1], ALU.mult, ALU.add),
                          wait=w0, inc=(csem, 1))
                    em.on(ceng, lambda e, ct=ct:
                          e.scalar_tensor_tensor(hcb[:, 128 * ct:128 * ct + 128], hext[:, 130 * ct:130 * ct + 128],
                                                 C.cw[:, 3 * ct:3 * ct + 1], hcb[:, 128 * ct:128 * ct + 128],
                                                 ALU.mult, ALU.add),
                          inc=(csem, 1))
                    em.on(ceng, lambda e, ct=ct:
                          e.scalar_tensor_tensor(hcb[:, 128 * ct:128 * ct + 128], hext[:, 130 * ct + 2:130 * ct + 130],
                                                 C.cw[:, 3 * ct + 2:3 * ct + 3], hcb[:, 128 * ct:128 * ct + 128],
                                                 ALU.mult, ALU.add),
                          inc=(csem, 1))
                cvv = sv.n
                cvv_g = sp7.n
                if i == 0 and "BX0" in kb.dbg:
                    em.dma("gpsimd", lambda e: e.dma_start(out=kb.dbg["BX0"], in_=hext[:, :]),
                           wait=[(sv, cvv)], sem=sg)
                    em.on("vector", None, wait=[(sg, sg.n)])
                if i == 0 and "BC0" in kb.dbg:
                    em.dma("gpsimd", lambda e: e.dma_start(out=kb.dbg["BC0"], in_=hcb[:, :]),
                           wait=[(sv, cvv)], sem=sg)
                    em.on("vector", None, wait=[(sg, sg.n)])
                em.on("scalar", lambda e: e.activation(hext[:, 0:4096], hcb[:, :], AF.Silu),
                      wait=[(sv, cvv), (sp7, cvv_g)], inc=(ssc, 1))
                silv = ssc.n
                em.on("scalar", lambda e: e.activation(vq[:, :], hext[:, 0:4096], AF.Abs),
                      wait=[(ssc, silv)], inc=(ssc, 1))
                absv = ssc.n
                em.on("vector", lambda e: e.tensor_tensor(vq[:, 0:2048], vq[:, 0:2048], vq[:, 2048:4096], ALU.max),
                      wait=[(ssc, absv)], inc=(sv, 1))
                em.on("vector", lambda e: e.tensor_tensor(vq[:, 0:1024], vq[:, 0:1024], vq[:, 1024:2048], ALU.max), inc=(sv, 1))
                em.on("vector", lambda e: e.tensor_tensor(vq[:, 0:512], vq[:, 0:512], vq[:, 512:1024], ALU.max), inc=(sv, 1))
                em.on("vector", lambda e: e.tensor_tensor(vq[:, 0:256], vq[:, 0:256], vq[:, 256:512], ALU.max), inc=(sv, 1))
                em.on("vector", lambda e: e.tensor_tensor(vq[:, 0:128], vq[:, 0:128], vq[:, 128:256], ALU.max), inc=(sv, 1))
                mtv = sv.n
                em.on("tensor", lambda e: e.transpose(kb.ps_b[:, 512:640], vq[:, 0:128], C.ident[:, :]),
                      wait=[(sv, mtv), (sv, sv.n)], inc=(st, 1))
                tv = st.n
                em.on("vector", lambda e: e.tensor_copy(rrow[:, :], kb.ps_b[:, 512:640]),
                      wait=[(st, tv)], inc=(sv, 1))
                em.on("vector", lambda e: e.reduce_max(stat[:, 2:3], rrow[:, :], AX.X), inc=(sv, 1))
                em.on("vector", lambda e: e.tensor_scalar_max(stat[:, 2:3], stat[:, 2:3], 1e-5), inc=(sv, 1))
                em.on("vector", lambda e: e.reciprocal(stat[:, 3:4], stat[:, 2:3]), inc=(sv, 1))
                em.on("vector", lambda e: e.tensor_scalar_mul(stat[:, 3:4], stat[:, 3:4], 127.0), inc=(sv, 1))
                em.on("vector", lambda e: e.reciprocal(stat[:, 4:5], stat[:, 3:4]), inc=(sv, 1))
                em.on("tensor", lambda e: e.transpose(kb.ps_b[0:1, 512:640], stat[:, 3:4], C.ident[:, :]),
                      wait=[(sv, sv.n)], inc=(st, 1))
                tv = st.n
                em.on("vector", lambda e: e.tensor_copy(row1[0:1, 0:128], kb.ps_b[0:1, 512:640]),
                      wait=[(st, tv)], inc=(sv, 1))
                rv = sv.n
                em.on("tensor", lambda e: e.matmul(kb.ps_b[:, 512:640], C.onesrow[0:1, 0:128], row1[0:1, 0:128],
                                                   start=True, stop=True),
                      wait=[(sv, rv)], inc=(st, 1))
                tv = st.n
                em.on("vector", lambda e: e.tensor_copy(rrow[:, :], kb.ps_b[:, 512:640]),
                      wait=[(st, tv)], inc=(sv, 1))
                scbv = sv.n
                for ct in range(32):
                    ceng = "vector" if ct < 16 else "gpsimd"
                    csem = sv if ct < 16 else sp7
                    w0 = [(sv, scbv)] if ct in (0, 16) else []
                    em.on(ceng, lambda e, ct=ct:
                          e.tensor_tensor(vq[:, 128 * ct:128 * ct + 128], hext[:, 128 * ct:128 * ct + 128],
                                          rrow[:, :], ALU.mult),
                          wait=w0, inc=(csem, 1))
                for ceng, csem, c0, c1 in (("vector", sv, 0, 2048), ("gpsimd", sp7, 2048, 4096)):
                    em.on(ceng, lambda e, c0=c0, c1=c1: e.tensor_copy(i32b[:, c0:c1], vq[:, c0:c1]), inc=(csem, 1))
                    em.on(ceng, lambda e, c0=c0, c1=c1: e.tensor_copy(hcb[:, c0:c1], i32b[:, c0:c1]), inc=(csem, 1))
                    em.on(ceng, lambda e, c0=c0, c1=c1: e.tensor_sub(vq[:, c0:c1], vq[:, c0:c1], hcb[:, c0:c1]), inc=(csem, 1))
                    em.on(ceng, lambda e, c0=c0, c1=c1: e.tensor_single_scalar(hext[:, c0:c1], vq[:, c0:c1], 0.5, ALU.is_ge), inc=(csem, 1))
                    em.on(ceng, lambda e, c0=c0, c1=c1: e.tensor_add(hcb[:, c0:c1], hcb[:, c0:c1], hext[:, c0:c1]), inc=(csem, 1))
                    em.on(ceng, lambda e, c0=c0, c1=c1: e.tensor_single_scalar(hext[:, c0:c1], vq[:, c0:c1], -0.5, ALU.is_le), inc=(csem, 1))
                    em.on(ceng, lambda e, c0=c0, c1=c1: e.tensor_sub(hcb[:, c0:c1], hcb[:, c0:c1], hext[:, c0:c1]), inc=(csem, 1))
                    em.on(ceng, lambda e, c0=c0, c1=c1: e.tensor_copy(aqb[:, c0:c1], hcb[:, c0:c1]), inc=(csem, 1))
                aqv = sv.n
                aqv_g = sp7.n
                if i == 0 and "BQ0" in kb.dbg:
                    em.dma("gpsimd", lambda e: e.dma_start(out=kb.dbg["BQ0"], in_=hcb[:, :]),
                           wait=[(sv, aqv), (sp7, aqv_g)], sem=sg)
                    em.on("vector", None, wait=[(sg, sg.n)])
                em.on("vector", lambda e: e.tensor_tensor(stat[:, 5:6], stat[:, 4:5], rs_w2_bc[:, 0:1], ALU.mult),
                      inc=(sv, 1))
                csv = sv.n
                for ck in range(32):
                    b2_ = ck % 2
                    qd = sd if b2_ == 0 else sg
                    em.dma("sync" if b2_ == 0 else "gpsimd", lambda e, ck=ck, b2_=b2_:
                           e.dma_start(out=w2qb[b2_][:, :], in_=kb.dW2qT[128 * ck:128 * ck + 128, :]),
                           wait=[(st, w2_free[b2_])], sem=qd)
                    wdma = qd.n
                    for dw in range(2):
                        w = []
                        if dw == 0:
                            w = [(qd, wdma)]
                            if ck == 0:
                                w.append((sv, aqv))
                            if ck == 16:
                                w.append((sp7, aqv_g))
                        em.on("tensor", lambda e, ck=ck, dw=dw, b2_=b2_:
                              e.matmul(kb.ps_o7[:, dw * 512:dw * 512 + 512],
                                       aqb[:, 128 * ck:128 * ck + 128],
                                       w2qb[b2_][:, dw * 512:dw * 512 + 512],
                                       start=(ck == 0), stop=(ck == 31)),
                              wait=w, inc=(st, 1) if (dw == 1) else None)
                    w2_free[b2_] = st.n
                mmv = st.n
                for dw in range(2):
                    em.on("vector", lambda e, dw=dw:
                          e.scalar_tensor_tensor(obuf[dw][:, :], kb.ps_o7[:, dw * 512:dw * 512 + 512],
                                                 stat[:, 5:6], xo[:, dw * 512:dw * 512 + 512],
                                                 ALU.mult, ALU.add),
                          wait=[(st, mmv), (sv, csv), (sd, lda), (so, obuf_free[dw])], inc=(sv, 1))
                    em.on("vector", lambda e, dw=dw:
                          e.tensor_tensor(obuf[dw][:, :], obuf[dw][:, :], asb[:, dw * 512:dw * 512 + 512], ALU.add),
                          inc=(sv, 1))
                    ev = sv.n
                    em.dma("gpsimd", lambda e, i=i, dw=dw:
                           e.dma_start(out=kb.out[128 * i:128 * i + 128, 512 * dw:512 * dw + 512], in_=obuf[dw][:, :]),
                           wait=[(sv, ev)], sem=so)
                    obuf_free[dw] = so.n

            em.on("gpsimd", None, wait=[(so, so.n), (sg, sg.n)])
            em.on("sync", None, wait=[(sd, sd.n)])
    kb.out_written = True


DEBUG_SRC = {}


def finish(kb):
    nc, sem = kb.nc, kb.sem
    sd = sem("fin_dma")
    with Phase(kb) as em:
        if not kb.out_written:
            z = kb.sb("fin_zero", [128, D])
            em.on("vector", lambda e: e.memset(z[:, :], 0.0), inc=(sem("fin_v"), 1))
            vv = kb.sems["fin_v"].n
            for i in range(OB):
                em.dma("gpsimd", lambda e, i=i: e.dma_start(out=kb.out[i * 128:(i + 1) * 128, :], in_=z[:, :]),
                       wait=[(sem("fin_v"), vv)], sem=sd)
        for nm, t in kb.dbg.items():
            fn = DEBUG_SRC.get(nm)
            if fn is None:
                continue
            import inspect as _ins
            if len(_ins.signature(fn).parameters) == 2:
                em.dma("gpsimd", lambda e, t=t, fn=fn: e.dma_start(out=t, in_=fn(kb, e)), sem=sd)
            else:
                src = fn(kb)
                em.dma("gpsimd", lambda e, t=t, src=src: e.dma_start(out=t, in_=src), sem=sd)
        em.on("gpsimd", None, wait=[(sd, sd.n)])



def stage_inputs(inputs, core):
    s = core // 2
    x = np.ascontiguousarray(inputs["L0_attn_x"][s])
    m = {
        "xseq": x,
        "l0_wq": inputs["L0_attn_wq"], "l0_bq": inputs["L0_attn_bq"],
        "l0_wk": inputs["L0_attn_wk"], "l0_bk": inputs["L0_attn_bk"],
        "l0_wv": inputs["L0_attn_wv"], "l0_bv": inputs["L0_attn_bv"],
        "l0_wo": inputs["L0_attn_wo"], "l0_bo": inputs["L0_attn_bo"],
        "ln_g": inputs["L0_ffn_ln_g"], "ln_b": inputs["L0_ffn_ln_b"],
        "rW": inputs["L0_ffn_rW"], "rb": inputs["L0_ffn_rb"],
        "W1": inputs["L0_ffn_W1"], "b1": inputs["L0_ffn_b1"],
        "W2": inputs["L0_ffn_W2"], "b2": inputs["L0_ffn_b2"],
        "gam": inputs["L1_attn_ln_gamma"], "bet": inputs["L1_attn_ln_beta"],
        "l1_wq": inputs["L1_attn_Wq"], "l1_bq": inputs["L1_attn_bq"],
        "l1_wk": inputs["L1_attn_Wk"], "l1_bk": inputs["L1_attn_bk"],
        "l1_wv": inputs["L1_attn_Wv"], "l1_bv": inputs["L1_attn_bv"],
        "l1_wo": inputs["L1_attn_Wo"], "l1_bo": inputs["L1_attn_bo"],
        "mu": inputs["L1_attn_mu"].reshape(1, 1),
        "w1f": inputs["L1_ffn_w1"], "convw": inputs["L1_ffn_conv_w"].reshape(4096, 3),
        "convb": inputs["L1_ffn_conv_b"], "w2f": inputs["L1_ffn_w2"],
        "parconst": np.array([[0.0], [0.0], [1.0]], dtype=np.float32),
    }
    return {k: np.ascontiguousarray(np.asarray(v, dtype=np.float32)) for k, v in m.items()}


_NC_CACHE = {}


def run(inputs, upto=99):
    key = upto
    if key not in _NC_CACHE:
        _NC_CACHE[key] = build_nc(upto)
    nc = _NC_CACHE[key]
    in_maps = [stage_inputs(inputs, c) for c in range(8)]
    res = run_bass_kernel_spmd(nc, in_maps, core_ids=list(range(8)))
    return res


def kernel(**inputs):
    res = run(inputs)
    out = np.zeros((4, T, D), dtype=np.float32)
    for c in range(8):
        s, p = c // 2, c % 2
        xo = np.asarray(res.results[c]["xout"])
        for i in range(OB):
            g = 2 * i + p
            out[s, g * 128:(g + 1) * 128, :] = xo[i * 128:(i + 1) * 128, :]
    return out


# version 06
# speedup vs baseline: 1.0071x; 1.0071x over previous
import numpy as np
from contextlib import ExitStack

import concourse.bass as bass
import concourse.mybir as mybir
from concourse.bass_utils import run_bass_kernel_spmd

F32 = mybir.dt.float32
BF16 = mybir.dt.bfloat16
I32 = mybir.dt.int32
AF = mybir.ActivationFunctionType
ALU = mybir.AluOpType
AX = mybir.AxisListType

T, D, H, HD = 4096, 1024, 16, 64
NB = 32
OB = 16
OT = OB * 128

DEBUG = {}
EXC_MODE = 'full'

ENGINES = ("sync", "vector", "scalar", "tensor", "gpsimd")


class Sem:
    def __init__(self, handle, name):
        self.h = handle
        self.name = name
        self.n = 0

    def bump(self, amt):
        self.n += amt
        return self.n


AUTO_CHAIN = {"vector", "scalar", "gpsimd"}


class Emitter:

    def __init__(self, kb):
        self.kb = kb
        self.steps = {e: [] for e in ENGINES}

    def on(self, eng, fn, wait=(), inc=None, kind="op"):
        kb = self.kb
        waits = list(wait)
        auto = eng in AUTO_CHAIN and fn is not None and kind == "op"
        if auto:
            sig = kb.chain_sig.get(eng)
            if sig is not None:
                waits.append(sig)
            if inc is None:
                inc = (kb.sem("auto_" + eng), 1)
        post = None
        if inc is not None:
            sem, amt = inc
            post = sem.bump(amt)
            if auto:
                kb.chain_sig[eng] = (sem, post)
        self.steps[eng].append((fn, tuple(waits), None if inc is None else (inc[0], inc[1])))
        return post

    def dma(self, eng, fn, wait=(), sem=None):
        return self.on(eng, fn, wait=wait, inc=(sem, 16), kind="dma")

    def emit(self, block):
        for eng_name in getattr(self, "engine_order", None) or ENGINES:
            steps = self.steps[eng_name]
            if not steps:
                continue

            def make_body(steps=steps):
                def body(eng):
                    for fn, waits, inc in steps:
                        for sem, val in waits:
                            eng.wait_ge(sem.h, val)
                        if fn is None:
                            continue
                        inst = fn(eng)
                        if inc is not None:
                            inst.then_inc(inc[0].h, inc[1])
                return body

            getattr(block, eng_name)(make_body())
        self.steps = {e: [] for e in ENGINES}


class Phase:

    def __init__(self, kb):
        self.kb = kb
        self.em = Emitter(kb)

    def __enter__(self):
        return self.em

    def __exit__(self, et, ev, tb):
        if et is not None:
            return False
        kb = self.kb
        kb.phase_no += 1
        n = kb.phase_no
        for e in ENGINES:
            self.em.on(e, lambda x: x.nop(), inc=(kb.phase_sems[e], 1))
        for e in ENGINES:
            for e2 in ENGINES:
                if e2 != e:
                    self.em.on(e, None, wait=[(kb.phase_sems[e2], n)])
        self.em.emit(kb.block)
        return False


class KB:
    pass


def ceil_div(a, b):
    return -(-a // b)


def ln_steps(kb, em, src_ap, dst_ap, m, vvar, sq, eps, first_wait=()):
    C = kb.C
    sv = kb.sem(kb.cur_sv)
    ssc = kb.sem(kb.cur_ssc)
    if not isinstance(sq, bass.AP):
        sq = sq[:, :]
    epsap = C.fc[:, 0:1] if eps == 1e-5 else C.fc[:, 1:2]
    em.on("vector", lambda e: e.reduce_sum(m, src_ap, AX.X), wait=first_wait, inc=(sv, 1))
    em.on("vector", lambda e: e.tensor_scalar_mul(m, m, -1.0 / D), inc=(sv, 1))
    em.on("vector", lambda e: e.tensor_scalar_add(dst_ap, src_ap, m), inc=(sv, 1))
    pv = sv.n
    em.on("scalar", lambda e: e.activation(sq, dst_ap, AF.Square, accum_out=vvar),
          wait=[(sv, pv)], inc=(ssc, 1))
    sc0 = ssc.n
    em.on("vector", lambda e: e.tensor_scalar_mul(vvar, vvar, 1.0 / D),
          wait=[(ssc, sc0)], inc=(sv, 1))
    pv = sv.n
    em.on("scalar", lambda e: e.activation(vvar, vvar, AF.Sqrt, bias=epsap),
          wait=[(sv, pv)], inc=(ssc, 1))
    sc0 = ssc.n
    em.on("vector", lambda e: e.reciprocal(vvar, vvar), wait=[(ssc, sc0)], inc=(sv, 1))
    em.on("vector", lambda e: e.tensor_scalar_mul(dst_ap, dst_ap, vvar), inc=(sv, 1))



def build_nc(upto=99):
    nc = bass.Bass("TRN2", target_bir_lowering=False, num_devices=8)
    kb = KB()
    kb.nc = nc
    kb.phase_no = 0
    kb.out_written = False
    kb.chain_sig = {}
    kb.rt = {}

    def rtv(e, eng, name):
        key = (eng, name)
        sv = kb.rt.get(key)
        if sv is None:
            if name == "par":
                sv = e.partition_id() % 2
            elif name == "par128":
                sv = rtv(e, eng, "par") * 128
            elif name == "par127":
                sv = rtv(e, eng, "par") * 127
            elif name == "par2048":
                sv = rtv(e, eng, "par") * 2048
            elif name == "truthy":
                sv = e.partition_id() >= 0
            elif name == "zsv":
                e.reg_mov(kb.exreg, 0)
                sv = e.snap(kb.exreg, min_val=0, max_val=1)
            kb.rt[key] = sv
        return sv

    kb.rtv = rtv

    I = {}

    def din(name, shape):
        I[name] = nc.dram_tensor(name, list(shape), F32, kind="ExternalInput").ap()
        return I[name]

    din("xseq", (T, D))
    for nm, sh in [
        ("l0_wq", (D, D)), ("l0_bq", (D,)), ("l0_wk", (D, D)), ("l0_bk", (D,)),
        ("l0_wv", (D, D)), ("l0_bv", (D,)), ("l0_wo", (D, D)), ("l0_bo", (D,)),
        ("ln_g", (D,)), ("ln_b", (D,)), ("rW", (D, 8)), ("rb", (8,)),
        ("W1", (8, D, 2048)), ("b1", (8, 2048)), ("W2", (8, 2048, D)), ("b2", (8, D)),
        ("gam", (D,)), ("bet", (D,)),
        ("l1_wq", (D, D)), ("l1_bq", (D,)), ("l1_wk", (D, D)), ("l1_bk", (D,)),
        ("l1_wv", (D, D)), ("l1_bv", (D,)), ("l1_wo", (D, D)), ("l1_bo", (D,)),
        ("mu", (1, 1)),
        ("w1f", (4096, D)), ("convw", (4096, 3)), ("convb", (4096,)),
        ("w2f", (D, 4096)),
        ("parconst", (3, 1)),
    ]:
        din(nm, sh)
    kb.I = I

    out = nc.dram_tensor("xout", [OT, D], F32, kind="ExternalOutput").ap()
    kb.out = out
    kb.dbg = {}
    for nm, sh in DEBUG.items():
        kb.dbg[nm] = nc.dram_tensor("dbg_" + nm, list(sh), F32, kind="ExternalOutput").ap()

    kb.dKT = nc.dram_tensor("dKT", [8, D, 512], F32).ap()
    kb.dV = nc.dram_tensor("dV", [8, 512, 16 * 65], F32).ap()
    kb.dQT = nc.dram_tensor("dQT", [NB, D, 128], F32).ap()
    kb.dA = nc.dram_tensor("dA", [OB, 128, D], F32).ap()
    kb.dAS = nc.dram_tensor("dAS", [OB, 128, D], F32).ap()
    kb.xsh = nc.dram_tensor("xsh", [T + 1, D], F32, addr_space="Shared").ap()
    kb.x1tmp = nc.dram_tensor("x1tmp", [T, D], F32).ap()
    kb.bar_in = nc.dram_tensor("bar_in", [1, 16], F32).ap()
    kb.bar_out = nc.dram_tensor("bar_out", [1, 16], F32).ap()
    kb.x1pad = nc.dram_tensor("x1pad", [17 * 256 + 4, D], F32).ap()
    kb.x1v = kb.x1pad[1:T + 1, :]
    kb.dQTown = nc.dram_tensor("dQTown", [OB, D, 128], F32).ap()
    kb.xown = nc.dram_tensor("xown", [OB, 128, D], F32).ap()
    kb.x1own = nc.dram_tensor("x1own", [OB, 128, D], F32).ap()
    kb.dW1qT = nc.dram_tensor("dW1qT", [D, 4096], BF16).ap()
    kb.dW2qT = nc.dram_tensor("dW2qT", [4096, D], BF16).ap()
    kb.dUN = None

    es = ExitStack()
    kb.es = es

    kb.sems = {}

    def sem(name):
        if name not in kb.sems:
            kb.sems[name] = Sem(es.enter_context(nc.semaphore(name)), name)
        return kb.sems[name]

    kb.sem = sem
    kb.phase_sems = {e: sem("ph_" + e) for e in ENGINES}

    def sb(name, shape, dtype=F32, stack=None):
        return (stack or es).enter_context(nc.sbuf_tensor(name, list(shape), dtype))

    kb.sb = sb

    C = KB()
    kb.C = C
    C.ident = sb("c_ident", [128, 128])
    C.iota_km = sb("c_iotakm", [128, 128])
    C.maskA = sb("c_maskA", [128, 128])
    C.maskB = sb("c_maskB", [128, 128])
    C.par = sb("c_par", [128, 1])
    C.bq = sb("c_bq", [128, 8])
    C.bk = sb("c_bk", [128, 8])
    C.bv_bc = sb("c_bv_bc", [128, D])
    C.bo_bc = sb("c_bo_bc", [128, D])
    C.rb_bc = sb("c_rb_bc", [128, 8])
    C.lng_bc = sb("c_lng_bc", [128, D])
    C.lnb_bc = sb("c_lnb_bc", [128, D])
    C.gam_bc = sb("c_gam_bc", [128, D])
    C.bet_bc = sb("c_bet_bc", [128, D])
    C.b1 = sb("c_b1", [128, 8 * 16])
    C.b2 = sb("c_b2", [8, D])
    C.negmu = sb("c_negmu", [128, 1])
    C.cw = sb("c_cw", [128, 32 * 3])
    C.cb = sb("c_cb", [128, 32])
    C.ones = sb("c_ones", [128, 1])
    C.scratch1 = sb("c_scr1", [128, 1])
    C.fc = sb("c_fc", [128, 8])
    C.onesrow = sb("c_onesrow", [128, 128])
    C.identB = sb("c_identB", [128, 128])

    kb.exreg = es.enter_context(nc.gpsimd.register("exreg"))
    kb.block = es.enter_context(nc.Block())

    def par_sv(eng):
        return eng.partition_id() % 2

    kb.par_sv = par_sv

    build_P0(kb)
    if upto >= 1:
        with nc.named_scope("P1prep0"):
            build_prep(kb, layer=0)
    if upto >= 2:
        with nc.named_scope("P2attn0"):
            build_attn(kb, layer=0)
    if upto >= 3:
        with nc.named_scope("P3moe"):
            build_moe(kb)
    if upto >= 4:
        with nc.named_scope("P4exch"):
            build_exchange(kb)
    if upto >= 5:
        with nc.named_scope("P5prep1"):
            build_prep(kb, layer=1)
    if upto >= 6:
        with nc.named_scope("P6attn1"):
            build_attn(kb, layer=1)
    if upto >= 7:
        with nc.named_scope("P7bitffn"):
            build_bitffn(kb)

    finish(kb)
    es.close()
    return nc



def build_P0(kb):
    nc, C, I, sem = kb.nc, kb.C, kb.I, kb.sem
    sd = sem("p0_dma")
    sc = sem("p0_cmp")
    with nc.psum_tensor("p0_ps0", [128, 512], F32) as _p0ps, Phase(kb) as em:
        kb.ps0 = _p0ps
        def ld(dst, src, wait=(), slow=False):
            em.dma("sync", lambda e, d=dst, s=src, slow=slow:
                   e.dma_start(out=d, in_=s, allow_slow_non_contiguous=slow),
                   wait=wait, sem=sd)

        ld(C.bq[:, :], I["l0_bq"].rearrange("(c p) -> p c", p=128), slow=True)
        ld(C.bk[:, :], I["l0_bk"].rearrange("(c p) -> p c", p=128), slow=True)
        ld(C.b1[:, :].rearrange("p (e c) -> p e c", e=8), I["b1"].rearrange("e (c p) -> p e c", p=128), slow=True)
        ld(C.b2[:, :], I["b2"][:, :])
        ld(C.cw[:, :].rearrange("p (c j) -> p c j", j=3), I["convw"].rearrange("(c p) j -> p c j", p=128), slow=True)
        ld(C.cb[:, :], I["convb"].rearrange("(c p) -> p c", p=128), slow=True)
        def par_load(e):
            parv = e.partition_id() % 2
            return e.dma_start(out=C.par[0:1, 0:1], in_=I["parconst"][bass.ds(parv + 1, 1), :])
        spar = sem("p0_pdma")
        em.dma("gpsimd", par_load, sem=spar)

        ps0 = kb.ps0
        stn = sem("p0_ten")

        em.on("gpsimd", lambda e: e.iota(C.iota_km[:, :], pattern=[[1, 128]], base=0,
                                         channel_multiplier=-1,
                                         allow_small_or_imprecise_dtypes=True),
              inc=(sc, 1))
        em.on("vector", lambda e: e.tensor_single_scalar(C.ident[:, :], C.iota_km[:, :], 0.0, ALU.is_equal),
              wait=[(sc, sc.n)], inc=(sc, 1))
        em.on("vector", lambda e: e.tensor_single_scalar(C.identB[:, :], C.iota_km[:, :], 64.0, ALU.is_equal),
              inc=(sc, 1))
        em.on("vector", lambda e: e.memset(C.onesrow[:, :], 1.0), inc=(sc, 1))
        onesrow_done = sc.n

        def bcast_mm(dst, src_row, n, wait=()):
            w = list(wait) + [(sc, onesrow_done), (sc, sc.n)]
            em.on("tensor", lambda e, dst=dst, src_row=src_row, n=n:
                  e.matmul(ps0[:, 0:n], C.onesrow[0:1, 0:128], src_row, start=True, stop=True),
                  wait=w, inc=(stn, 1))
            tv = stn.n
            em.on("vector", lambda e, dst=dst, n=n: e.tensor_copy(dst, ps0[:, 0:n]),
                  wait=[(stn, tv)], inc=(sc, 1))

        bcast_mm(C.par[:, 0:1], C.par[0:1, 0:1], 1, wait=[(spar, spar.n)])
        em.on("vector", lambda e: e.tensor_scalar_mul(C.scratch1[:, :], C.par[:, :], -128.0),
              wait=[(sc, sc.n)], inc=(sc, 1))
        em.on("vector", lambda e: e.tensor_single_scalar(C.maskA[:, :], C.iota_km[:, :], C.scratch1[:, 0:1], ALU.is_ge),
              wait=[(sc, sc.n)], inc=(sc, 1))
        em.on("vector", lambda e: e.tensor_scalar(C.scratch1[:, :], C.par[:, :], -128.0, 128.0, ALU.mult, ALU.add),
              wait=[(sc, sc.n)], inc=(sc, 1))
        em.on("vector", lambda e: e.tensor_single_scalar(C.maskB[:, :], C.iota_km[:, :], C.scratch1[:, 0:1], ALU.is_ge),
              wait=[(sc, sc.n)], inc=(sc, 1))
        em.on("vector", lambda e: e.memset(C.ones[:, :], 1.0), inc=(sc, 1))
        for j, v in [(0, 1e-5), (1, 1e-6), (2, 0.125), (3, 0.5)]:
            em.on("vector", lambda e, j=j, v=v: e.memset(C.fc[:, j:j + 1], v), inc=(sc, 1))

        row = kb.sb("p0_row", [1, D])

        def bcast(dst, nm, n=D, rearr="(a d) -> a d"):
            ld(row[0:1, 0:n], I[nm].rearrange(rearr, a=1) if rearr else I[nm],
               wait=[(sc, sc.n)])
            nl = sd.n
            for j in range(0, n, 512):
                w = min(512, n - j)
                bcast_mm(dst[:, j:j + w], row[0:1, j:j + w], w, wait=[(sd, nl)])

        bcast(C.bv_bc[:, :], "l0_bv")
        bcast(C.bo_bc[:, :], "l0_bo")
        bcast(C.lng_bc[:, :], "ln_g")
        bcast(C.lnb_bc[:, :], "ln_b")
        bcast(C.gam_bc[:, :], "gam")
        bcast(C.bet_bc[:, :], "bet")
        bcast(C.rb_bc[:, :], "rb", n=8)
        ld(row[0:1, 0:1], I["mu"][0:1, 0:1], wait=[(sc, sc.n)])
        nl = sd.n
        em.on("vector", lambda e: e.tensor_scalar_mul(row[0:1, 0:1], row[0:1, 0:1], -1.0),
              wait=[(sd, nl)], inc=(sc, 1))
        bcast_mm(C.negmu[:, :], row[0:1, 0:1], 1)
        em.on("sync", None, wait=[(sd, sd.n)])
        em.on("gpsimd", None, wait=[(sc, sc.n), (spar, spar.n)])



def build_prep(kb, layer):
    nc, C, I, sem = kb.nc, kb.C, kb.I, kb.sem
    L = f"l{layer}"
    with ExitStack() as ps:
        sbl = lambda n, s, d=F32: kb.sb(f"{L}p_" + n, s, d, stack=ps)
        wq = sbl("wq", [128, 8 * D])
        wk = sbl("wk", [128, 8 * D])
        wv = sbl("wv", [128, 8 * D])
        wstage = sbl("wstage", [128, D])
        xs = sbl("xs", [128, 4 * D])
        xn = sbl("xn", [128, 4 * D])
        xnT = sbl("xnT", [128, 8 * 512])
        kevac = sbl("kevac", [128, 512])
        vevac = sbl("vevac", [128, 16 * 65])
        stat = sbl("stat", [128, 8])
        sq = sbl("sqs", [128, D])
        ps_tr = ps.enter_context(nc.psum_tensor(f"{L}p_tr", [128, 1024], F32))
        kb.ps0 = ps.enter_context(nc.psum_tensor(f"{L}p_ps0", [128, 512], F32))
        ps_k = ps.enter_context(nc.psum_tensor(f"{L}p_k", [128, 1024], F32))
        ps_v = ps.enter_context(nc.psum_tensor(f"{L}p_v", [128, 1024], F32))

        kb.cur_sv = f"{L}p_vec"
        kb.cur_ssc = f"{L}p_scl"
        sd = sem(f"{L}p_dma")
        sw = sem(f"{L}p_wdma")
        so = sem(f"{L}p_odma")
        sv = sem(f"{L}p_vec")
        st = sem(f"{L}p_ten")
        ssc = sem(f"{L}p_scl")

        with Phase(kb) as em:
            tr_k = [0]
            tr_copy_done = []

            def transpose_tile(em, src_ap, dst_ap, extra_wait=()):
                col = (tr_k[0] % 2) * 512
                w = list(extra_wait)
                if tr_k[0] >= 2:
                    w.append((sv, tr_copy_done[tr_k[0] - 2]))
                em.on("tensor", lambda e, src_ap=src_ap, col=col:
                      e.transpose(ps_tr[:, col:col + 128], src_ap, C.ident[:, :]),
                      wait=w, inc=(st, 1))
                tv = st.n
                em.on("vector", lambda e, dst_ap=dst_ap, col=col:
                      e.tensor_copy(dst_ap, ps_tr[:, col:col + 128]),
                      wait=[(st, tv)], inc=(sv, 1))
                tr_copy_done.append(sv.n)
                tr_k[0] += 1

            if layer == 1:
                em.dma("sync", lambda e: e.dma_start(out=C.bq[:, :], in_=I["l1_bq"].rearrange("(c p) -> p c", p=128),
                                                     allow_slow_non_contiguous=True), sem=sw)
                em.dma("sync", lambda e: e.dma_start(out=C.bk[:, :], in_=I["l1_bk"].rearrange("(c p) -> p c", p=128),
                                                     allow_slow_non_contiguous=True), sem=sw)
                row1 = sbl("row1", [1, D])
                em.dma("sync", lambda e: e.dma_start(out=row1[0:1, :], in_=I["l1_bv"].rearrange("(a d) -> a d", a=1)),
                       sem=sw)
                rowdma = sw.n
                for j in range(0, D, 512):
                    em.on("tensor", lambda e, j=j:
                          e.matmul(kb.ps0[:, 0:512], C.onesrow[0:1, 0:128], row1[0:1, j:j + 512],
                                   start=True, stop=True),
                          wait=[(sw, rowdma), (sv, sv.n)], inc=(st, 1))
                    tv = st.n
                    em.on("vector", lambda e, j=j: e.tensor_copy(C.bv_bc[:, j:j + 512], kb.ps0[:, 0:512]),
                          wait=[(st, tv)], inc=(sv, 1))

            wnames = {0: ("l0_wq", "l0_wk", "l0_wv"), 1: ("l1_wq", "l1_wk", "l1_wv")}[layer]
            for wi, (wt, nm) in enumerate(zip((wq, wk, wv), wnames)):
                for c in range(8):
                    if layer == 0:
                        em.dma("sync", lambda e, wt=wt, nm=nm, c=c:
                               e.dma_start(out=wt[:, c * D:(c + 1) * D], in_=I[nm][c * 128:(c + 1) * 128, :]),
                               sem=sw)
                    else:
                        em.dma("sync", lambda e, nm=nm, c=c:
                               e.dma_start(out=wstage[:, :], in_=I[nm][c * 128:(c + 1) * 128, :]),
                               wait=[(st, st.n)], sem=sw)
                        need = sw.n
                        for j in range(8):
                            transpose_tile(em, wstage[:, j * 128:(j + 1) * 128],
                                           wt[:, j * D + c * 128: j * D + c * 128 + 128],
                                           extra_wait=[(sw, need)] if j == 0 else ())
            wload = sw.n

            src = kb.x1v if layer == 1 else I["xseq"]
            guard_tr = [0]

            for sup in range(8):
                for b in range(4):
                    em.dma("sync", lambda e, sup=sup, b=b:
                           e.dma_start(out=xs[:, b * D:(b + 1) * D],
                                       in_=src[(4 * sup + b) * 128:(4 * sup + b + 1) * 128, :]),
                           wait=[(sv, sv.n), (ssc, ssc.n)] if b == 0 else (), sem=sd)
                ld = sd.n

                for b in range(4):
                    xsb = xs[:, b * D:(b + 1) * D]
                    xnb = xn[:, b * D:(b + 1) * D]
                    m = stat[:, b:b + 1]
                    vvar = stat[:, 4 + b:5 + b]

                    ln_steps(kb, em, xsb, xnb, m, vvar, sq, 1e-5,
                             first_wait=[(sd, ld), (st, guard_tr[0])] if b == 0 else ())
                    if layer == 1:
                        ln_steps(kb, em, xnb, xnb, m, vvar, sq, 1e-6)
                        em.on("vector", lambda e, xnb=xnb: e.tensor_mul(xnb, xnb, C.gam_bc[:, :]), inc=(kb.sem(f"{L}p_vec"), 1))
                        em.on("vector", lambda e, xnb=xnb: e.tensor_add(xnb, xnb, C.bet_bc[:, :]), inc=(kb.sem(f"{L}p_vec"), 1))
                xn_done = sv.n
                if sup == 0 and "XN0" in kb.dbg:
                    em.dma("gpsimd", lambda e: e.dma_start(out=kb.dbg["XN0"], in_=xn[:, :]),
                           wait=[(sv, xn_done)], sem=so)
                    em.on("vector", None, wait=[(so, so.n)])

                for b in range(4):
                    for c in range(8):
                        w = []
                        if b == 0 and c == 0:
                            w.append((sv, xn_done))
                            if layer == 0 and sup == 0:
                                w.append((sw, wload))
                        transpose_tile(em, xn[:, b * D + c * 128: b * D + c * 128 + 128],
                                       xnT[:, c * 512 + b * 128: c * 512 + b * 128 + 128],
                                       extra_wait=w)
                tr_done = sv.n
                guard_tr[0] = st.n

                for wt_, bias_, isq in ((wk, C.bk, False), (wq, C.bq, True)):
                    for kt in range(8):
                        pb = kt % 2
                        for dc in range(8):
                            w = []
                            if dc == 0:
                                w.append((sv, sv.n))
                                if kt == 0 and not isq:
                                    w.append((sw, wload))
                            em.on("tensor", lambda e, wt_=wt_, kt=kt, dc=dc, pb=pb:
                                  e.matmul(ps_k[:, pb * 512:pb * 512 + 512],
                                           wt_[:, dc * D + kt * 128: dc * D + kt * 128 + 128],
                                           xnT[:, dc * 512:(dc + 1) * 512],
                                           start=(dc == 0), stop=(dc == 7)),
                                  wait=w, inc=(st, 1) if dc == 7 else None)
                        tv = st.n
                        em.on("vector", lambda e, bias_=bias_, kt=kt, pb=pb:
                              e.tensor_scalar_add(kevac[:, :], ps_k[:, pb * 512:pb * 512 + 512], bias_[:, kt:kt + 1]),
                              wait=[(st, tv)], inc=(sv, 1))
                        ev = sv.n
                        if not isq:
                            em.dma("gpsimd", lambda e, sup=sup, kt=kt:
                                   e.dma_start(out=kb.dKT[sup, kt * 128:(kt + 1) * 128, :], in_=kevac[:, :]),
                                   wait=[(sv, ev)], sem=so)
                        else:
                            for b in range(4):
                                em.dma("gpsimd", lambda e, sup=sup, kt=kt, b=b:
                                       e.dma_start(out=kb.dQT[4 * sup + b, kt * 128:(kt + 1) * 128, :],
                                                   in_=kevac[:, b * 128:(b + 1) * 128]),
                                       wait=[(sv, ev)] if b == 0 else (), sem=so)
                        em.on("vector", None, wait=[(so, so.n)])

                vev3 = vevac[:, :].rearrange("p (h j) -> p h j", j=65)
                for s4 in range(4):
                    for dw in range(2):
                        pb = dw
                        for dc in range(8):
                            w = [(sv, sv.n)] if dc == 0 else []
                            em.on("tensor", lambda e, s4=s4, dw=dw, dc=dc, pb=pb:
                                  e.matmul(ps_v[:, pb * 512:pb * 512 + 512],
                                           xnT[:, dc * 512 + s4 * 128: dc * 512 + s4 * 128 + 128],
                                           wv[:, dc * D + dw * 512: dc * D + dw * 512 + 512],
                                           start=(dc == 0), stop=(dc == 7)),
                                  wait=w, inc=(st, 1) if dc == 7 else None)
                        tv = st.n
                        em.on("vector", lambda e, dw=dw, pb=pb:
                              e.tensor_tensor(
                                  vev3[:, dw * 8:(dw + 1) * 8, 0:64],
                                  ps_v[:, pb * 512:pb * 512 + 512],
                                  C.bv_bc[:, dw * 512:(dw + 1) * 512], ALU.add),
                              wait=[(st, tv)], inc=(sv, 1))
                    em.on("vector", lambda e: e.memset(vev3[:, :, 64:65], 1.0),
                          inc=(sv, 1))
                    ev = sv.n
                    em.dma("gpsimd", lambda e, sup=sup, s4=s4:
                           e.dma_start(out=kb.dV[sup, s4 * 128:(s4 + 1) * 128, :], in_=vevac[:, :]),
                           wait=[(sv, ev)], sem=so)
                    em.on("vector", None, wait=[(so, so.n)])

            em.on("gpsimd", None, wait=[(so, so.n)])
            em.on("sync", None, wait=[(sd, sd.n), (sw, sw.n)])


def build_attn(kb, layer):
    nc, C, I, sem = kb.nc, kb.C, kb.I, kb.sem
    L = f"l{layer}"
    with ExitStack() as ps:
        sbl = lambda n, s, d=F32: kb.sb(f"{L}a_" + n, s, d, stack=ps)
        qwin = sbl("qwin", [128, 4096])
        ksup = [sbl(f"ksup{i}", [128, 4096]) for i in range(2)]
        vsup = [sbl(f"vsup{i}", [128, 4160]) for i in range(2)]
        num = [sbl(f"num{i}", [128, 1024]) for i in range(3)]
        sig = [sbl(f"sig{i}", [128, 1024]) for i in range(2)] if layer == 1 else None
        U_sb = sbl("U", [128, 8192])
        aevac = sbl("aevac", [128, 512])
        wo = sbl("wo", [128, 8192])

        ps_sc = ps.enter_context(nc.psum_tensor(f"{L}a_sc", [128, 3072], F32))
        ps_u = ps.enter_context(nc.psum_tensor(f"{L}a_u", [128, 1024], F32))

        sd = sem(f"{L}a_dma")
        sqd = sem(f"{L}a_qdma")
        so = sem(f"{L}a_odma")
        sv = sem(f"{L}a_vec")
        st = sem(f"{L}a_ten")
        ssc = sem(f"{L}a_scl")
        kb.cur_sv = f"{L}a_vec"
        kb.cur_ssc = f"{L}a_scl"

        with Phase(kb) as em:
            tr_k = [0]
            tr_done = []

            def transpose_tile(src_ap, dst_ap, extra_wait=()):
                col = (tr_k[0] % 2) * 512
                w = list(extra_wait)
                if tr_k[0] >= 2:
                    w.append((sv, tr_done[tr_k[0] - 2]))
                em.on("tensor", lambda e, src_ap=src_ap, col=col:
                      e.transpose(ps_sc[:, col:col + 128], src_ap, C.ident[:, :]),
                      wait=w, inc=(st, 1))
                tv = st.n
                em.on("vector", lambda e, dst_ap=dst_ap, col=col:
                      e.tensor_copy(dst_ap, ps_sc[:, col:col + 128]),
                      wait=[(st, tv)], inc=(sv, 1))
                tr_done.append(sv.n)
                tr_k[0] += 1

            wnm = "l0_wo" if layer == 0 else "l1_wo"
            if layer == 1:
                row1 = kb.sb("l1a_row1", [1, D], F32, stack=ps)
                em.dma("sync", lambda e: e.dma_start(out=row1[0:1, :], in_=I["l1_bo"].rearrange("(a d) -> a d", a=1)),
                       sem=sd)
                rowdma = sd.n
                for j in range(0, D, 512):
                    em.on("tensor", lambda e, j=j:
                          e.matmul(ps_sc[:, 0:512], C.onesrow[0:1, 0:128], row1[0:1, j:j + 512],
                                   start=True, stop=True),
                          wait=[(sd, rowdma), (sv, sv.n)], inc=(st, 1))
                    tv = st.n
                    em.on("vector", lambda e, j=j: e.tensor_copy(C.bo_bc[:, j:j + 512], ps_sc[:, 0:512]),
                          wait=[(st, tv)], inc=(sv, 1))
            for c in range(8):
                if layer == 0:
                    em.dma("sync", lambda e, c=c:
                           e.dma_start(out=wo[:, c * D:(c + 1) * D], in_=I[wnm][c * 128:(c + 1) * 128, :]),
                           sem=sd)
                else:
                    em.dma("sync", lambda e, c=c:
                           e.dma_start(out=qwin[:, 0:1024], in_=I[wnm][c * 128:(c + 1) * 128, :]),
                           wait=[(st, st.n)], sem=sd)
                    need = sd.n
                    for j in range(8):
                        transpose_tile(qwin[:, j * 128:(j + 1) * 128],
                                       wo[:, j * D + c * 128: j * D + c * 128 + 128],
                                       extra_wait=[(sd, need)] if j == 0 else ())
            wo_dma = sd.n
            wo_ready = sv.n

            def qbulk(e):
                par = kb.rtv(e, "gpsimd", "par")
                src = kb.dQT.rearrange("(i b) d q -> i b d q", b=2)[:, bass.ds(par, 1), :, :]
                return e.dma_start(out=kb.dQTown[:, :, :].rearrange("i (a d) q -> i a d q", a=1), in_=src)
            em.dma("gpsimd", qbulk, wait=[(so, so.n)], sem=sqd)
            qbulk_done = sqd.n

            nsup_of = (lambda w: 2 * w + 2) if layer == 0 else (lambda w: 8)

            ksup_free = [0, 0]
            vsup_free = [0, 0]
            num_free = [0, 0, 0]
            num_written = [0, 0, 0]
            sig_written = [0, 0]
            qwin_free = [0]

            for qw in range(4):
                for jq in range(4):
                    def qload(e, qw=qw, jq=jq):
                        src = kb.dQTown[4 * qw + jq, :, :].rearrange("(c p) q -> p c q", p=128)
                        dst = qwin[:, :].rearrange("p (c q) -> p c q", c=8)[:, :, 128 * jq:128 * jq + 128]
                        return e.dma_start(out=dst, in_=src)
                    em.dma("gpsimd", qload,
                           wait=([(st, qwin_free[0])] if jq == 0 else [])
                           + ([(sqd, qbulk_done), (sv, wo_ready)] if qw == 0 and jq == 0 else []),
                           sem=sqd)
                qwin_dma = sqd.n

                nsup = nsup_of(qw)
                for sup in range(nsup):
                    buf = sup % 2
                    em.dma("sync", lambda e, sup=sup, buf=buf:
                           e.dma_start(out=ksup[buf][:, :].rearrange("p (c t) -> p c t", c=8),
                                       in_=kb.dKT[sup, :, :].rearrange("(c p) t -> p c t", p=128)),
                           wait=[(st, ksup_free[buf])], sem=sd)
                    em.dma("sync", lambda e, sup=sup, buf=buf:
                           e.dma_start(out=vsup[buf][:, :].rearrange("p (s v) -> p s v", s=4),
                                       in_=kb.dV[sup, :, :].rearrange("(s p) v -> p s v", p=128)),
                           wait=[(st, vsup_free[buf])], sem=sd)
                    kv_dma = sd.n

                    for hp in range(8):
                        hA, hB = 2 * hp, 2 * hp + 1

                        def scores(s, em=em, qw=qw, sup=sup, hp=hp, buf=buf):
                            slot = (4 * hp + s) % 3
                            w = []
                            if hp == 0 and s <= 1:
                                w.append((sd, kv_dma))
                                if sup == 0 and s == 0:
                                    w.append((sqd, qwin_dma))
                            if num_written[slot]:
                                w.append((ssc if layer == 0 else sv, num_written[slot]))
                            for hh in range(2):
                                h = 2 * hp + hh
                                rb = (h % 2) * 64
                                ct = h // 2
                                em.on("tensor", lambda e, h=h, rb=rb, ct=ct, s=s, slot=slot, buf=buf, hh=hh:
                                      e.matmul(ps_sc[:, slot * 1024 + hh * 512: slot * 1024 + hh * 512 + 512],
                                               ksup[buf][rb:rb + 64, ct * 512 + 128 * s: ct * 512 + 128 * s + 128],
                                               qwin[rb:rb + 64, ct * 512:(ct + 1) * 512],
                                               start=True, stop=True),
                                      wait=w if hh == 0 else (), inc=(st, 1))
                            return st.n

                        def numf(s, em=em, qw=qw, sup=sup, hp=hp, sc_done=None):
                            slot = (4 * hp + s) % 3
                            kc = 4 * sup + s
                            if layer == 0:
                                em.on("scalar", lambda e, slot=slot:
                                      e.activation(num[slot][:, :], ps_sc[:, slot * 1024:slot * 1024 + 1024],
                                                   AF.Exp, scale=C.fc[:, 2:3]),
                                      wait=[(st, sc_done), (st, num_free[slot])], inc=(ssc, 1))
                                num_written[slot] = ssc.n
                                done_sem, done_val = ssc, ssc.n
                                mops = []
                                for jj in range(4):
                                    Ga = 8 * qw + 2 * jj
                                    if kc == Ga:
                                        mops.append((jj, C.maskA))
                                    elif kc == Ga + 1:
                                        mops.append((jj, C.maskB))
                                    elif kc > Ga + 1:
                                        mops.append((jj, None))
                                for jj, msk in mops:
                                    for hh in range(2):
                                        cl = hh * 512 + 128 * jj
                                        if msk is None:
                                            em.on("vector", lambda e, slot=slot, cl=cl:
                                                  e.memset(num[slot][:, cl:cl + 128], 0.0),
                                                  wait=[(ssc, num_written[slot])], inc=(sv, 1))
                                        else:
                                            em.on("vector", lambda e, slot=slot, cl=cl, msk=msk:
                                                  e.tensor_mul(num[slot][:, cl:cl + 128], num[slot][:, cl:cl + 128], msk[:, :]),
                                                  wait=[(ssc, num_written[slot])], inc=(sv, 1))
                                if mops:
                                    done_sem, done_val = sv, sv.n
                                return (done_sem, done_val)
                            else:
                                sl2 = (4 * hp + s) % 2
                                em.on("scalar", lambda e, slot=slot, sl2=sl2:
                                      e.activation(sig[sl2][:, :], ps_sc[:, slot * 1024:slot * 1024 + 1024],
                                                   AF.Sigmoid, scale=C.fc[:, 2:3], bias=C.negmu[:, 0:1]),
                                      wait=[(st, sc_done), (sv, num_written[slot]), (st, num_free[slot]),
                                            (sv, sig_written[sl2])],
                                      inc=(ssc, 1))
                                em.on("scalar", lambda e, slot=slot:
                                      e.activation(num[slot][:, :], ps_sc[:, slot * 1024:slot * 1024 + 1024],
                                                   AF.Exp, scale=C.fc[:, 2:3]),
                                      inc=(ssc, 1))
                                sc2 = ssc.n
                                em.on("vector", lambda e, slot=slot, sl2=sl2:
                                      e.tensor_mul(num[slot][:, :], num[slot][:, :], sig[sl2][:, :]),
                                      wait=[(ssc, sc2)], inc=(sv, 1))
                                num_written[slot] = sv.n
                                sig_written[sl2] = sv.n
                                return (sv, sv.n)

                        def av(s, ndone, em=em, sup=sup, hp=hp, buf=buf, qw=qw):
                            slot = (4 * hp + s) % 3
                            w = [(ndone[0], ndone[1])]
                            if s == 0:
                                w.append((sv, u_evac_done[0]))
                            for hh in range(2):
                                h = 2 * hp + hh
                                em.on("tensor", lambda e, h=h, hh=hh, s=s, slot=slot, buf=buf:
                                      e.matmul(ps_u[0:65, hh * 512:(hh + 1) * 512],
                                               vsup[buf][:, s * 1040 + 65 * h: s * 1040 + 65 * h + 65],
                                               num[slot][:, hh * 512:(hh + 1) * 512],
                                               start=(s == 0), stop=(s == 3)),
                                      wait=w if hh == 0 else (), inc=(st, 1))
                            num_free[slot] = st.n
                            return st.n

                        u_evac_done = getattr(kb, "_ued_" + L, None)
                        if u_evac_done is None:
                            u_evac_done = [0]
                            setattr(kb, "_ued_" + L, u_evac_done)

                        scd = {}
                        nd = {}
                        scd[0] = scores(0)
                        nd[0] = numf(0, sc_done=scd[0])
                        scd[1] = scores(1)
                        av(0, nd[0])
                        nd[1] = numf(1, sc_done=scd[1])
                        scd[2] = scores(2)
                        av(1, nd[1])
                        nd[2] = numf(2, sc_done=scd[2])
                        scd[3] = scores(3)
                        av(2, nd[2])
                        nd[3] = numf(3, sc_done=scd[3])
                        av_done = av(3, nd[3])
                        for hh in range(2):
                            h = 2 * hp + hh
                            if sup == 0:
                                em.on("vector", lambda e, h=h, hh=hh:
                                      e.tensor_copy(U_sb[0:65, 512 * h:512 * h + 512], ps_u[0:65, hh * 512:hh * 512 + 512]),
                                      wait=[(st, av_done)], inc=(sv, 1))
                            else:
                                em.on("vector", lambda e, h=h, hh=hh:
                                      e.tensor_tensor(U_sb[0:65, 512 * h:512 * h + 512],
                                                      U_sb[0:65, 512 * h:512 * h + 512],
                                                      ps_u[0:65, hh * 512:hh * 512 + 512], ALU.add),
                                      wait=[(st, av_done)], inc=(sv, 1))
                        u_evac_done[0] = sv.n
                    ksup_free[buf] = st.n
                    vsup_free[buf] = st.n
                qwin_free[0] = st.n

                uev = u_evac_done[0]
                if qw == 0 and layer == 0 and "USB0" in kb.dbg:
                    em.dma("gpsimd", lambda e: e.dma_start(out=kb.dbg["USB0"], in_=U_sb[:, :]),
                           wait=[(sv, uev)], sem=so)
                    em.on("vector", None, wait=[(so, so.n)])
                em.on("vector", lambda e: e.reciprocal(U_sb[64:65, :], U_sb[64:65, :]),
                      wait=[(sv, uev)], inc=(sv, 1))
                for c in range(16):
                    em.on("tensor", lambda e, c=c:
                          e.matmul(ps_sc[0:64, 0:512], C.onesrow[64:65, 0:64],
                                   U_sb[64:65, c * 512:(c + 1) * 512],
                                   start=True, stop=True),
                          wait=[(sv, sv.n)], inc=(st, 1))
                    tv = st.n
                    em.on("vector", lambda e, c=c:
                          e.tensor_tensor(U_sb[0:64, c * 512:(c + 1) * 512],
                                          U_sb[0:64, c * 512:(c + 1) * 512],
                                          ps_sc[0:64, 0:512], ALU.mult),
                          wait=[(st, tv)], inc=(sv, 1))
                un_done = sv.n
                if qw == 0 and layer == 0 and "UN0" in kb.dbg:
                    em.dma("gpsimd", lambda e: e.dma_start(out=kb.dbg["UN0"], in_=U_sb[:, :]),
                           wait=[(sv, un_done)], sem=so)
                    em.on("vector", None, wait=[(so, so.n)])
                for hp in range(8):
                    em.on("tensor", lambda e, hp=hp:
                          e.matmul(ps_sc[:, 1024:1536], C.ident[0:64, 0:128],
                                   U_sb[0:64, 512 * (2 * hp):512 * (2 * hp) + 512],
                                   start=True, stop=False),
                          wait=[(sv, un_done)] if hp == 0 else [(sv, sv.n)], inc=(st, 1))
                    em.on("tensor", lambda e, hp=hp:
                          e.matmul(ps_sc[:, 1024:1536], C.identB[0:64, 0:128],
                                   U_sb[0:64, 512 * (2 * hp + 1):512 * (2 * hp + 1) + 512],
                                   start=False, stop=True),
                          inc=(st, 1))
                    tv = st.n
                    em.on("vector", lambda e, hp=hp:
                          e.tensor_copy(U_sb[:, 1024 * hp:1024 * hp + 512], ps_sc[:, 1024:1536]),
                          wait=[(st, tv)], inc=(sv, 1))
                pk_done = sv.n
                if qw == 0 and layer == 0 and "UPK0" in kb.dbg:
                    em.dma("gpsimd", lambda e: e.dma_start(out=kb.dbg["UPK0"], in_=U_sb[:, :]),
                           wait=[(sv, pk_done)], sem=so)
                    em.on("vector", None, wait=[(so, so.n)])
                for jq in range(4):
                    for dw in range(2):
                        pb = dw
                        for hp in range(8):
                            w = []
                            if hp == 0:
                                w = [(sv, pk_done), (sv, sv.n)]
                                if jq == 0 and dw == 0:
                                    w.append((sd, wo_dma))
                            em.on("tensor", lambda e, jq=jq, dw=dw, hp=hp, pb=pb:
                                  e.matmul(ps_u[:, pb * 512:pb * 512 + 512],
                                           U_sb[:, 1024 * hp + 128 * jq: 1024 * hp + 128 * jq + 128],
                                           wo[:, 1024 * hp + 512 * dw: 1024 * hp + 512 * dw + 512],
                                           start=(hp == 0), stop=(hp == 7)),
                                  wait=w, inc=(st, 1) if hp == 7 else None)
                        tv = st.n
                        em.on("vector", lambda e, dw=dw, pb=pb:
                              e.tensor_tensor(aevac[:, :], ps_u[:, pb * 512:pb * 512 + 512],
                                              C.bo_bc[:, dw * 512:(dw + 1) * 512], ALU.add),
                              wait=[(st, tv)], inc=(sv, 1))
                        ev = sv.n
                        em.dma("gpsimd", lambda e, qw=qw, jq=jq, dw=dw:
                               e.dma_start(out=kb.dA[4 * qw + jq, :, 512 * dw:512 * dw + 512], in_=aevac[:, :]),
                               wait=[(sv, ev)], sem=so)
                        em.on("vector", None, wait=[(so, so.n)])
                u_evac_done[0] = sv.n
                ksup_free[0] = st.n
                ksup_free[1] = st.n
                vsup_free[0] = st.n
                vsup_free[1] = st.n
                num_written[0] = 0
                num_written[1] = 0

            em.on("gpsimd", None, wait=[(so, so.n), (sqd, sqd.n)])
            em.on("sync", None, wait=[(sd, sd.n)])


def build_moe(kb):
    nc, C, I, sem = kb.nc, kb.C, kb.I, kb.sem
    with ExitStack() as ps:
        sbl = lambda n, s, d=F32: kb.sb("m_" + n, s, d, stack=ps)
        zT = sbl("zT", [128, 8 * 2048])
        stat = sbl("stat", [128, 8])
        lsb = sbl("lsb", [128, 8])
        tp2 = sbl("tp2", [128, 8])
        w_sb = sbl("w_sb", [128, 8 * 16])
        wrow = sbl("wrow", [1, 512])
        b2row = sbl("b2row", [1, 1024])
        wbc = sbl("wbc", [128, 512])
        rw_sb = sbl("rw", [128, 64])
        w1b = [sbl(f"w1b{i}", [128, 8 * 512]) for i in range(2)]
        w2b = [sbl(f"w2b{i}", [128, 1024]) for i in range(2)]
        gbuf = sbl("gbuf", [128, 8192])
        x1acc = sbl("x1acc", [128, 4096])
        ps_h = ps.enter_context(nc.psum_tensor("m_h", [128, 512], F32))
        ps_h2 = ps.enter_context(nc.psum_tensor("m_h2", [128, 512], F32))
        ps_h3 = ps.enter_context(nc.psum_tensor("m_h3", [128, 512], F32))
        kb.ps0 = ps.enter_context(nc.psum_tensor("m_ps0", [128, 512], F32))
        ps_x1 = ps.enter_context(nc.psum_tensor("m_x1", [128, 2048], F32))

        sd = sem("m_dma")
        sg = sem("m_gdma")
        so = sem("m_odma")
        sv = sem("m_vec")
        st = sem("m_ten")
        ssc = sem("m_scl")
        swd = sem("m_w2d")
        kb.cur_sv = "m_vec"
        kb.cur_ssc = "m_scl"

        with Phase(kb) as em:
            gb_free = [0]

            def xbulk(e):
                p128 = kb.rtv(e, "gpsimd", "par128")
                src = I["xseq"].rearrange("(i r) d -> i r d", r=256)[:, bass.ds(p128, 128), :]
                return e.dma_start(out=kb.xown[:, :, :], in_=src)
            em.dma("gpsimd", xbulk, sem=sg)
            xbulk_done = sg.n

            for dc in range(8):
                em.dma("sync", lambda e, dc=dc: e.dma_start(out=rw_sb[:, dc * 8:(dc + 1) * 8],
                                                            in_=I["rW"][dc * 128:(dc + 1) * 128, :]),
                       sem=sd)
            tr_k = [0]
            tr_copy_done = []

            def transpose_tile(src_ap, dst_ap, extra_wait=(), psslot=ps_h):
                col = 0
                w = list(extra_wait)
                if tr_k[0] >= 1:
                    w.append((sv, tr_copy_done[tr_k[0] - 1]))
                em.on("tensor", lambda e, src_ap=src_ap:
                      e.transpose(ps_h[:, 0:128], src_ap, C.ident[:, :]),
                      wait=w, inc=(st, 1))
                tv = st.n
                em.on("vector", lambda e, dst_ap=dst_ap:
                      e.tensor_copy(dst_ap, ps_h[:, 0:128]),
                      wait=[(st, tv)], inc=(sv, 1))
                tr_copy_done.append(sv.n)
                tr_k[0] += 1

            for i in range(OB):
                em.dma("gpsimd", lambda e, i=i: e.dma_start(out=x1acc[:, 0:1024], in_=kb.xown[i, :, :]),
                       wait=[(sv, sv.n), (ssc, ssc.n)] + ([(sg, xbulk_done)] if i == 0 else []), sem=sg)
                em.dma("sync", lambda e, i=i: e.dma_start(out=gbuf[:, 0:1024], in_=kb.dA[i, :, :]),
                       wait=[(sv, sv.n)], sem=sd)
                ld = sg.n
                ld2 = sd.n
                ln_steps(kb, em, x1acc[:, 0:1024], x1acc[:, 1024:2048], stat[:, 0:1], stat[:, 1:2], x1acc[:, 3072:4096], 1e-5,
                         first_wait=[(sg, ld)])
                em.on("vector", lambda e: e.tensor_add(gbuf[:, 0:1024], gbuf[:, 0:1024], x1acc[:, 0:1024]),
                      wait=[(sd, ld2)], inc=(sv, 1))
                em.on("vector", lambda e: e.tensor_add(gbuf[:, 0:1024], gbuf[:, 0:1024], x1acc[:, 1024:2048]), inc=(sv, 1))
                asd = sv.n
                em.dma("gpsimd", lambda e, i=i: e.dma_start(out=kb.dAS[i, :, :], in_=gbuf[:, 0:1024]),
                       wait=[(sv, asd)], sem=so)
                ln_steps(kb, em, x1acc[:, 1024:2048], x1acc[:, 2048:3072], stat[:, 2:3], stat[:, 3:4], x1acc[:, 3072:4096], 1e-5)
                em.on("vector", lambda e: e.tensor_mul(x1acc[:, 2048:3072], x1acc[:, 2048:3072], C.lng_bc[:, :]), inc=(sv, 1))
                em.on("vector", lambda e: e.tensor_add(x1acc[:, 2048:3072], x1acc[:, 2048:3072], C.lnb_bc[:, :]), inc=(sv, 1))
                z_done = sv.n
                for dc in range(8):
                    transpose_tile(x1acc[:, 2048 + dc * 128:2048 + (dc + 1) * 128],
                                   zT[:, dc * 2048 + 128 * i: dc * 2048 + 128 * i + 128],
                                   extra_wait=[(sv, z_done)] if dc == 0 else ())
                ztr = sv.n
                for dc in range(8):
                    em.on("tensor", lambda e, dc=dc, i=i:
                          e.matmul(ps_h2[:, 0:8],
                                   zT[:, dc * 2048 + 128 * i: dc * 2048 + 128 * i + 128],
                                   rw_sb[:, dc * 8:(dc + 1) * 8],
                                   start=(dc == 0), stop=(dc == 7)),
                          wait=[(sv, ztr), (sv, sv.n)] if dc == 0 else (), inc=(st, 1) if dc == 7 else None)
                tv = st.n
                em.on("vector", lambda e: e.tensor_tensor(lsb[:, :], ps_h2[:, 0:8], C.rb_bc[:, :], ALU.add),
                      wait=[(st, tv)], inc=(sv, 1))
                em.on("vector", lambda e: e.reduce_max(stat[:, 4:5], lsb[:, :], AX.X, negate=True), inc=(sv, 1))
                pv = sv.n
                em.on("scalar", lambda e: e.activation(lsb[:, :], lsb[:, :], AF.Exp, bias=stat[:, 4:5]),
                      wait=[(sv, pv)], inc=(ssc, 1))
                sc0 = ssc.n
                em.on("vector", lambda e: e.reduce_sum(stat[:, 5:6], lsb[:, :], AX.X),
                      wait=[(ssc, sc0)], inc=(sv, 1))
                em.on("vector", lambda e: e.reciprocal(stat[:, 5:6], stat[:, 5:6]), inc=(sv, 1))
                em.on("vector", lambda e: e.tensor_scalar_mul(lsb[:, :], lsb[:, :], stat[:, 5:6]), inc=(sv, 1))
                em.on("vector", lambda e: e.reduce_max(stat[:, 4:5], lsb[:, :], AX.X), inc=(sv, 1))
                em.on("vector", lambda e: e.tensor_single_scalar(tp2[:, :], lsb[:, :], stat[:, 4:5], ALU.is_ge), inc=(sv, 1))
                em.on("vector", lambda e: e.reduce_sum(stat[:, 6:7], tp2[:, :], AX.X), inc=(sv, 1))
                em.on("vector", lambda e: e.scalar_tensor_tensor(tp2[:, :], tp2[:, :], -1e9, lsb[:, :], ALU.mult, ALU.add), inc=(sv, 1))
                em.on("vector", lambda e: e.reduce_max(stat[:, 5:6], tp2[:, :], AX.X), inc=(sv, 1))
                em.on("vector", lambda e: e.tensor_single_scalar(stat[:, 6:7], stat[:, 6:7], 1.5, ALU.is_ge), inc=(sv, 1))
                em.on("vector", lambda e: e.tensor_tensor(stat[:, 7:8], stat[:, 4:5], stat[:, 5:6], ALU.subtract), inc=(sv, 1))
                em.on("vector", lambda e: e.tensor_tensor(stat[:, 6:7], stat[:, 6:7], stat[:, 7:8], ALU.mult), inc=(sv, 1))
                em.on("vector", lambda e: e.tensor_tensor(stat[:, 6:7], stat[:, 6:7], stat[:, 5:6], ALU.add), inc=(sv, 1))
                em.on("vector", lambda e: e.tensor_single_scalar(tp2[:, :], lsb[:, :], stat[:, 6:7], ALU.is_ge), inc=(sv, 1))
                em.on("vector", lambda e: e.tensor_mul(tp2[:, :], tp2[:, :], lsb[:, :]), inc=(sv, 1))
                em.on("vector", lambda e: e.reduce_sum(stat[:, 7:8], tp2[:, :], AX.X), inc=(sv, 1))
                em.on("vector", lambda e: e.tensor_scalar_max(stat[:, 7:8], stat[:, 7:8], 1e-8), inc=(sv, 1))
                em.on("vector", lambda e: e.reciprocal(stat[:, 7:8], stat[:, 7:8]), inc=(sv, 1))
                em.on("vector", lambda e, i=i: e.tensor_scalar_mul(w_sb[:, 8 * i:8 * i + 8], tp2[:, :], stat[:, 7:8]), inc=(sv, 1))
                wv_done = sv.n

            if "W8" in kb.dbg:
                em.dma("gpsimd", lambda e: e.dma_start(out=kb.dbg["W8"], in_=w_sb[:, :]),
                       wait=[(sv, sv.n)], sem=so)
            w1_free = [0, 0]
            w2_free = [0, 0]
            psh_free = [0, 0, 0]
            gball_free = [0]
            b2_free = [0]
            wm = [0] * 16
            evac_done = [0]
            for q in range(4):
                for e_ in range(8):
                    for ts in range(4):
                        i_b = 4 * q + ts
                        em.on("tensor", lambda e, e_=e_, i_b=i_b:
                              e.transpose(ps_h2[0:1, 256:384], w_sb[:, 8 * i_b + e_:8 * i_b + e_ + 1], C.ident[:, :]),
                              wait=[(sv, sv.n)], inc=(st, 1))
                        tv = st.n
                        em.on("vector", lambda e, ts=ts: e.tensor_copy(wrow[0:1, 128 * ts:128 * ts + 128], ps_h2[0:1, 256:384]),
                              wait=[(st, tv)], inc=(sv, 1))
                    wrow_done = sv.n
                    em.on("tensor", lambda e:
                          e.matmul(kb.ps0[:, 0:512], C.onesrow[0:1, 0:128], wrow[0:1, 0:512],
                                   start=True, stop=True),
                          wait=[(sv, wrow_done)], inc=(st, 1))
                    tv = st.n
                    em.on("vector", lambda e: e.tensor_copy(wbc[:, :], kb.ps0[:, 0:512]),
                          wait=[(st, tv)], inc=(sv, 1))
                    wbc_done = sv.n
                    for fw in range(4):
                        b1_ = (e_ * 4 + fw) % 2
                        em.dma("sync", lambda e, e_=e_, fw=fw, b1_=b1_:
                               e.dma_start(out=w1b[b1_][:, :].rearrange("p (c f) -> p c f", c=8),
                                           in_=I["W1"][e_, :, 512 * fw:512 * fw + 512].rearrange("(c p) f -> p c f", p=128)),
                               wait=[(st, w1_free[b1_])], sem=sd)
                        w1_dma = sd.n
                        for fi in range(4):
                            ft = 4 * fw + fi
                            psh = (ps_h, ps_h2, ps_h3)[ft % 3]
                            for dc in range(8):
                                w = []
                                if dc == 0:
                                    w = [(sd, w1_dma), (ssc, psh_free[ft % 3])]
                                    if ft == 1:
                                        w.append((sv, wbc_done))
                                em.on("tensor", lambda e, dc=dc, fi=fi, q=q, psh=psh, b1_=b1_:
                                      e.matmul(psh[:, 0:512],
                                               w1b[b1_][:, dc * 512 + 128 * fi: dc * 512 + 128 * fi + 128],
                                               zT[:, dc * 2048 + 512 * q: dc * 2048 + 512 * q + 512],
                                               start=(dc == 0), stop=(dc == 7)),
                                      wait=w, inc=(st, 1) if dc == 7 else None)
                            tv = st.n
                            gw = [(st, tv), (st, gball_free[0])]
                            if e_ == 0 and ft == 0:
                                gw.append((sv, sv.n))
                                if q == 0:
                                    gw.append((so, so.n))
                            em.on("scalar", lambda e, e_=e_, ft=ft, psh=psh:
                                  e.activation(gbuf[:, 512 * ft:512 * ft + 512], psh[:, 0:512],
                                               AF.Gelu, bias=C.b1[:, 16 * e_ + ft:16 * e_ + ft + 1]),
                                  wait=gw, inc=(ssc, 1))
                            psh_free[ft % 3] = ssc.n
                            sc0 = ssc.n
                            em.on("vector", lambda e, ft=ft:
                                  e.tensor_mul(gbuf[:, 512 * ft:512 * ft + 512],
                                               gbuf[:, 512 * ft:512 * ft + 512], wbc[:, :]),
                                  wait=[(ssc, sc0), (sv, wbc_done)], inc=(sv, 1))
                            wm[ft] = sv.n
                        w1_free[b1_] = st.n
                    gw_all = sv.n
                    em.dma("sync", lambda e, e_=e_: e.dma_start(out=b2row[0:1, :], in_=I["b2"][e_:e_ + 1, :]),
                           wait=[(st, b2_free[0])], sem=sd)
                    b2_dma = sd.n
                    for half in range(2):
                        for ft in range(16):
                            b2_ = ft % 2
                            em.dma("gpsimd", lambda e, e_=e_, ft=ft, b2_=b2_:
                                   e.dma_start(out=w2b[b2_][:, :], in_=I["W2"][e_, 128 * ft:128 * ft + 128, :]),
                                   wait=[(st, w2_free[b2_])], sem=swd)
                            w2_dma = swd.n
                            for ts in range(2):
                                tsg = 2 * half + ts
                                for dw in range(2):
                                    w = []
                                    if ts == 0 and dw == 0:
                                        w = [(swd, w2_dma), (sv, wm[ft])]
                                        if ft == 0:
                                            w.append((sv, evac_done[0]))
                                    em.on("tensor", lambda e, ft=ft, ts=ts, dw=dw, b2_=b2_, tsg=tsg:
                                          e.matmul(ps_x1[:, (2 * ts + dw) * 512:(2 * ts + dw) * 512 + 512],
                                                   gbuf[:, 512 * ft + 128 * tsg: 512 * ft + 128 * tsg + 128],
                                                   w2b[b2_][:, 512 * dw:512 * dw + 512],
                                                   start=(ft == 0), stop=False),
                                          wait=w, inc=(st, 1))
                            w2_free[b2_] = st.n
                        for ts in range(2):
                            tsg = 2 * half + ts
                            for dw in range(2):
                                em.on("tensor", lambda e, ts=ts, dw=dw, e_=e_, tsg=tsg:
                                      e.matmul(ps_x1[:, (2 * ts + dw) * 512:(2 * ts + dw) * 512 + 512],
                                               wrow[0:1, 128 * tsg:128 * tsg + 128],
                                               b2row[0:1, 512 * dw:512 * dw + 512],
                                               start=False, stop=True),
                                      wait=[(sd, b2_dma)] if (ts == 0 and dw == 0 and half == 0) else (),
                                      inc=(st, 1))
                        mmB = st.n
                        if half == 1:
                            b2_free[0] = st.n
                        for ts in range(2):
                            tsg = 2 * half + ts
                            for dw in range(2):
                                col = tsg * 1024 + dw * 512
                                w = [(st, mmB)] if (ts == 0 and dw == 0) else []
                                if e_ == 0:
                                    em.on("vector", lambda e, ts=ts, dw=dw, col=col:
                                          e.tensor_copy(x1acc[:, col:col + 512],
                                                        ps_x1[:, (2 * ts + dw) * 512:(2 * ts + dw) * 512 + 512]),
                                          wait=w, inc=(sv, 1))
                                else:
                                    em.on("vector", lambda e, ts=ts, dw=dw, col=col:
                                          e.tensor_tensor(x1acc[:, col:col + 512], x1acc[:, col:col + 512],
                                                          ps_x1[:, (2 * ts + dw) * 512:(2 * ts + dw) * 512 + 512],
                                                          ALU.add),
                                          wait=w, inc=(sv, 1))
                        evac_done[0] = sv.n
                    gball_free[0] = st.n
                for tsg in range(4):
                    i = 4 * q + tsg
                    em.dma("sync", lambda e, i=i: e.dma_start(out=gbuf[:, 0:1024], in_=kb.dAS[i, :, :]),
                           wait=[(sv, sv.n), (so, so.n)], sem=sd)
                    asd = sd.n
                    for dw in range(2):
                        em.on("vector", lambda e, tsg=tsg, dw=dw:
                              e.tensor_tensor(gbuf[:, 1024:1536],
                                              x1acc[:, tsg * 1024 + dw * 512: tsg * 1024 + dw * 512 + 512],
                                              gbuf[:, dw * 512:dw * 512 + 512], ALU.add),
                              wait=[(sd, asd)], inc=(sv, 1))
                        ev = sv.n
                        em.dma("gpsimd", lambda e, i=i, dw=dw:
                               e.dma_start(out=kb.x1own[i, :, 512 * dw:512 * dw + 512], in_=gbuf[:, 1024:1536]),
                               wait=[(sv, ev)], sem=so)
                        em.on("vector", None, wait=[(so, so.n)])

            def x1bulk(e):
                p2048 = kb.rtv(e, "gpsimd", "par2048")
                dst = kb.xsh[0:T + 1, :][bass.ds(p2048, 2048), :]
                return e.dma_start(out=dst, in_=kb.x1own[:, :, :].rearrange("i r d -> (i r) d"))
            em.dma("gpsimd", x1bulk, wait=[(so, so.n)], sem=so)
            em.on("gpsimd", None, wait=[(so, so.n), (sg, sg.n), (swd, swd.n)])
            em.on("sync", None, wait=[(sd, sd.n)])


def build_exchange(kb):
    nc, C, sem = kb.nc, kb.C, kb.sem
    with ExitStack() as ps:
        imp = kb.sb("ex_imp", [128, 4096], F32, stack=ps)
        _order = ("gpsimd", "sync", "vector", "scalar", "tensor")
        zr = kb.sb("ex_zr", [4, D], F32, stack=ps)
        sg = sem("ex_dma")
        sv = sem("ex_vec")
        with Phase(kb) as em:
            em.engine_order = _order
            em.on("vector", lambda e: e.memset(zr[:, :], 0.0), inc=(sv, 1))
            zv = sv.n
            em.dma("gpsimd", lambda e: e.dma_start(out=kb.x1pad[0:1, :], in_=zr[0:1, :]),
                   wait=[(sv, zv)], sem=sg)
            em.dma("gpsimd", lambda e: e.dma_start(out=kb.x1pad[T + 1:T + 5, :], in_=zr[:, :]),
                   sem=sg)
            if EXC_MODE != "nobarrier":
                bsem = sem("ex_bar")
                em.dma("gpsimd", lambda e: e.dma_start(out=kb.bar_in[0:1, :], in_=zr[0:1, 0:16]),
                       wait=[(sv, zv)], sem=sg)
                em.on("gpsimd", lambda e: e.collective_compute(
                          "AllReduce", ALU.add, replica_groups=[list(range(8))],
                          ins=[kb.bar_in.opt()], outs=[kb.bar_out.opt()]),
                      wait=[(sg, sg.n)], inc=(bsem, 1), kind="dma")
                em.on("gpsimd", None, wait=[(bsem, bsem.n)])
            if "MRK" in kb.dbg:
                def marker(e):
                    p2048 = kb.rtv(e, "gpsimd", "par2048")
                    return e.dma_start(out=kb.xsh[0:T + 1, :][bass.ds(p2048, 1), 0:16],
                                       in_=C.onesrow[0:1, 0:16])
                em.dma("gpsimd", marker, wait=[(sg, sg.n)], sem=sg)
            if EXC_MODE != "noimport":
                def imp_read(e):
                    zsv = kb.rtv(e, "gpsimd", "zsv")
                    tru = kb.rtv(e, "gpsimd", "truthy")
                    src = kb.xsh[0:T + 1, :][bass.ds(zsv, T), :]
                    return e.dma_start(out=kb.x1tmp[:, :], in_=src, cond=tru)
                em.dma("gpsimd", imp_read, wait=[(sg, sg.n)], sem=sg)
                for g in range(NB):
                    srow = 2048 * (g % 2) + 128 * (g // 2)
                    em.dma("gpsimd", lambda e, g=g, srow=srow:
                           e.dma_start(out=kb.x1pad[1 + 128 * g:1 + 128 * g + 128, :],
                                       in_=kb.x1tmp[srow:srow + 128, :]),
                           wait=[(sg, sg.n)] if g == 0 else (), sem=sg)
            em.on("gpsimd", None, wait=[(sg, sg.n)])


def build_bitffn(kb):
    nc, C, I, sem = kb.nc, kb.C, kb.I, kb.sem
    with ExitStack() as ps:
        sbl = lambda n, s_, d=F32: kb.sb("b_" + n, s_, d, stack=ps)
        sd = sem("b_dma")
        sp7 = sem("b_pool")
        sg = sem("b_gdma")
        so = sem("b_odma")
        sv = sem("b_vec")
        st = sem("b_ten")
        ssc = sem("b_scl")
        kb.cur_sv = "b_vec"
        kb.cur_ssc = "b_scl"

        kb.ps_b = ps.enter_context(nc.psum_tensor("ps_b", [128, 1024], F32))
        kb.ps_h7 = ps.enter_context(nc.psum_tensor("ps_h7", [128, 1024], F32))
        kb.ps_o7 = ps.enter_context(nc.psum_tensor("ps_o7", [128, 1024], F32))
        rs_w1_bc = sbl("rsw1", [128, 1])
        rs_w2_bc = sbl("rsw2", [128, 1])
        stat = sbl("stat", [128, 8])
        sc11 = sbl("sc11", [1, 4])

        with ExitStack() as pq:
            sbq = lambda n, s_, d=F32: kb.sb("bq_" + n, s_, d, stack=pq)
            wst = sbq("wst", [128, 4096])
            i32t = pq.enter_context(nc.sbuf_tensor("bq_i32", [128, 4096], I32))
            qf = sbq("qf", [128, 4096])
            cc = sbq("cc", [128, 4096])
            tstage = sbq("tstage", [128, 4096], BF16)
            acc = sbq("acc", [128, 2])

            with Phase(kb) as em:
                tr_k = [0]
                tr_done = []

                def transpose_tile(src_ap, dst_ap, extra_wait=(), cast=True):
                    col = (tr_k[0] % 2) * 512
                    w = list(extra_wait)
                    if tr_k[0] >= 2:
                        w.append((sv, tr_done[tr_k[0] - 2]))
                    em.on("tensor", lambda e, src_ap=src_ap, col=col:
                          e.transpose(kb.ps_b[:, col:col + 128], src_ap, C.ident[:, :]),
                          wait=w, inc=(st, 1))
                    tv = st.n
                    em.on("vector", lambda e, dst_ap=dst_ap, col=col:
                          e.tensor_copy(dst_ap, kb.ps_b[:, col:col + 128]),
                          wait=[(st, tv)], inc=(sv, 1))
                    tr_done.append(sv.n)
                    tr_k[0] += 1

                for wi, (wnm, nch, ncol, dT, accj) in enumerate(
                        (("w1f", 32, 1024, kb.dW1qT, 0), ("w2f", 8, 4096, kb.dW2qT, 1))):
                    for c in range(nch):
                        em.dma("sync", lambda e, wnm=wnm, c=c, ncol=ncol:
                               e.dma_start(out=wst[:, 0:ncol], in_=I[wnm][c * 128:(c + 1) * 128, :]),
                               wait=[(sv, sv.n)], sem=sd)
                        ld = sd.n
                        em.on("vector", lambda e, ncol=ncol: e.reduce_sum(stat[:, 0:1], wst[:, 0:ncol], AX.X,
                                                                          apply_absolute_value=True),
                              wait=[(sd, ld)], inc=(sv, 1))
                        if c == 0:
                            em.on("vector", lambda e, accj=accj: e.tensor_copy(acc[:, accj:accj + 1], stat[:, 0:1]),
                                  inc=(sv, 1))
                        else:
                            em.on("vector", lambda e, accj=accj: e.tensor_add(acc[:, accj:accj + 1],
                                                                              acc[:, accj:accj + 1], stat[:, 0:1]),
                                  inc=(sv, 1))
                    accv = sv.n
                    em.on("tensor", lambda e, accj=accj:
                          e.matmul(kb.ps_b[0:1, 512:513], acc[:, accj:accj + 1], C.ones[:, 0:1],
                                   start=True, stop=True),
                          wait=[(sv, accv), (sv, sv.n)], inc=(st, 1))
                    tv = st.n
                    em.on("vector", lambda e: e.tensor_copy(sc11[0:1, 0:1], kb.ps_b[0:1, 512:513]),
                          wait=[(st, tv)], inc=(sv, 1))
                    em.on("vector", lambda e, nch=nch, ncol=ncol:
                          e.tensor_scalar_mul(sc11[0:1, 0:1], sc11[0:1, 0:1], 1.0 / (nch * 128 * ncol)), inc=(sv, 1))
                    em.on("vector", lambda e: e.tensor_scalar_max(sc11[0:1, 0:1], sc11[0:1, 0:1], 1e-5), inc=(sv, 1))
                    em.on("vector", lambda e: e.reciprocal(sc11[0:1, 1:2], sc11[0:1, 0:1]), inc=(sv, 1))
                    em.on("vector", lambda e: e.reciprocal(sc11[0:1, 2:3], sc11[0:1, 1:2]), inc=(sv, 1))
                    scv = sv.n
                    dstb = rs_w1_bc if wi == 0 else rs_w2_bc
                    em.on("tensor", lambda e:
                          e.matmul(kb.ps_b[:, 512:513], C.onesrow[0:1, 0:128], sc11[0:1, 1:2],
                                   start=True, stop=True),
                          wait=[(sv, sv.n)], inc=(st, 1))
                    tv = st.n
                    em.on("vector", lambda e: e.tensor_copy(stat[:, 1:2], kb.ps_b[:, 512:513]),
                          wait=[(st, tv)], inc=(sv, 1))
                    em.on("tensor", lambda e:
                          e.matmul(kb.ps_b[:, 512:513], C.onesrow[0:1, 0:128], sc11[0:1, 2:3],
                                   start=True, stop=True),
                          wait=[(sv, sv.n)], inc=(st, 1))
                    tv = st.n
                    em.on("vector", lambda e, dstb=dstb: e.tensor_copy(dstb[:, 0:1], kb.ps_b[:, 512:513]),
                          wait=[(st, tv)], inc=(sv, 1))
                    for c in range(nch):
                        em.dma("sync", lambda e, wnm=wnm, c=c, ncol=ncol:
                               e.dma_start(out=wst[:, 0:ncol], in_=I[wnm][c * 128:(c + 1) * 128, :]),
                               wait=[(sv, sv.n)], sem=sd)
                        ld = sd.n
                        em.on("vector", lambda e, ncol=ncol:
                              e.tensor_scalar_mul(qf[:, 0:ncol], wst[:, 0:ncol], stat[:, 1:2]),
                              wait=[(sd, ld)], inc=(sv, 1))
                        em.on("vector", lambda e, ncol=ncol: e.tensor_copy(i32t[:, 0:ncol], qf[:, 0:ncol]), inc=(sv, 1))
                        em.on("vector", lambda e, ncol=ncol: e.tensor_copy(cc[:, 0:ncol], i32t[:, 0:ncol]), inc=(sv, 1))
                        em.on("vector", lambda e, ncol=ncol: e.tensor_sub(qf[:, 0:ncol], qf[:, 0:ncol], cc[:, 0:ncol]), inc=(sv, 1))
                        em.on("vector", lambda e, ncol=ncol: e.tensor_single_scalar(wst[:, 0:ncol], qf[:, 0:ncol], 0.5, ALU.is_ge), inc=(sv, 1))
                        em.on("vector", lambda e, ncol=ncol: e.tensor_add(cc[:, 0:ncol], cc[:, 0:ncol], wst[:, 0:ncol]), inc=(sv, 1))
                        em.on("vector", lambda e, ncol=ncol: e.tensor_single_scalar(wst[:, 0:ncol], qf[:, 0:ncol], -0.5, ALU.is_le), inc=(sv, 1))
                        em.on("vector", lambda e, ncol=ncol: e.tensor_sub(cc[:, 0:ncol], cc[:, 0:ncol], wst[:, 0:ncol]), inc=(sv, 1))
                        em.on("vector", lambda e, ncol=ncol:
                              e.tensor_scalar(cc[:, 0:ncol], cc[:, 0:ncol], -1.0, 1.0, ALU.max, ALU.min), inc=(sv, 1))
                        qv = sv.n
                        for j in range(ncol // 128):
                            transpose_tile(cc[:, j * 128:(j + 1) * 128],
                                           tstage[:, j * 128:(j + 1) * 128],
                                           extra_wait=[(sv, qv), (so, so.n)] if j == 0 else ())
                        trv = sv.n
                        em.dma("gpsimd", lambda e, c=c, dT=dT, ncol=ncol:
                               e.dma_start(out=dT.rearrange("(j p) x -> p j x", p=128)[:, :, c * 128:(c + 1) * 128],
                                           in_=tstage[:, 0:ncol].rearrange("p (j q) -> p j q", q=128)),
                               wait=[(sv, trv)], sem=so)
                em.on("gpsimd", None, wait=[(so, so.n)])
                em.on("sync", None, wait=[(sd, sd.n)])

        w1q = ps.enter_context(nc.sbuf_tensor("b_w1q", [128, 8 * 4096], BF16))
        w2qb = [ps.enter_context(nc.sbuf_tensor(f"b_w2q{i}", [128, 1024], BF16)) for i in range(2)]
        xnb = sbl("xnb", [128, D])
        vq = sbl("vq", [128, 4096])
        i32b = ps.enter_context(nc.sbuf_tensor("b_i32b", [128, 4096], I32))
        hcb = sbl("hcb", [128, 4096])
        hext = sbl("hext", [128, 32 * 130])
        aqb = ps.enter_context(nc.sbuf_tensor("b_aqb", [128, 4096], BF16))
        xqT = ps.enter_context(nc.sbuf_tensor("b_xqT", [128, 1024], BF16))
        hT = ps.enter_context(nc.sbuf_tensor("b_hT", [128, 256], BF16))
        hh_sb = sbl("hh", [128, 1024])
        hsc = sbl("hsc", [128, 32])
        rrow = sbl("rrow", [128, 128])
        row1 = sbl("row1", [1, 128])
        xo = sbl("xo", [128, D])
        asb = sbl("asb", [128, D])
        obuf = [sbl("obuf0", [128, 512]), sbl("obuf1", [128, 512])]

        with Phase(kb) as em:
            tr_k = [0]
            tr_done = []

            def transpose_tile(src_ap, dst_ap, extra_wait=(), wout=128, kin=128):
                col = (tr_k[0] % 2) * 512
                w = list(extra_wait)
                if tr_k[0] >= 2:
                    w.append((sv, tr_done[tr_k[0] - 2]))
                em.on("tensor", lambda e, src_ap=src_ap, col=col, wout=wout, kin=kin:
                      e.transpose(kb.ps_b[:, col:col + wout], src_ap, C.ident[0:kin, 0:kin]),
                      wait=w, inc=(st, 1))
                tv = st.n
                em.on("vector", lambda e, dst_ap=dst_ap, col=col, wout=wout:
                      e.tensor_copy(dst_ap, kb.ps_b[:, col:col + wout]),
                      wait=[(st, tv)], inc=(sv, 1))
                tr_done.append(sv.n)
                tr_k[0] += 1

            for dc in range(8):
                em.dma("sync", lambda e, dc=dc:
                       e.dma_start(out=w1q[:, dc * 4096:(dc + 1) * 4096], in_=kb.dW1qT[dc * 128:(dc + 1) * 128, :]),
                       sem=sd)
            w1dma = sd.n
            def xbulk(e):
                p128 = kb.rtv(e, "gpsimd", "par128")
                src = kb.x1v.rearrange("(i r) d -> i r d", r=256)[:, bass.ds(p128, 128), :]
                return e.dma_start(out=kb.xown[:, :, :], in_=src)
            em.dma("gpsimd", xbulk, sem=sg)
            xbulkv = sg.n
            def hload(e, base, dst0):
                p128 = kb.rtv(e, "gpsimd", "par128")
                src = kb.x1pad[base:base + 16 * 256, :].rearrange("(i r) d -> i r d", r=256)[:, bass.ds(p128, 1), :]
                return e.dma_start(out=xnb[dst0:dst0 + 16, :].rearrange("i (a d) -> i a d", a=1), in_=src)
            em.dma("gpsimd", lambda e: hload(e, 0, 0), sem=sg)
            em.dma("gpsimd", lambda e: hload(e, 129, 16), sem=sg)
            hloadv = sg.n

            for pb_ in (32, 64, 96):
                em.on("vector", lambda e, pb_=pb_: e.memset(xnb[pb_:pb_ + 32, :], 0.0), inc=(sv, 1))
            ln_steps(kb, em, xnb[:, :], xnb[:, :], stat[:, 0:1], stat[:, 1:2],
                     vq[:, 0:1024], 1e-5, first_wait=[(sg, hloadv)])
            em.on("vector", lambda e: e.reduce_max(stat[0:32, 2:3], xnb[0:32, :], AX.X,
                                                   apply_absolute_value=True), inc=(sv, 1))
            em.on("vector", lambda e: e.tensor_scalar_max(stat[0:32, 2:3], stat[0:32, 2:3], 1e-5), inc=(sv, 1))
            em.on("vector", lambda e: e.reciprocal(stat[0:32, 3:4], stat[0:32, 2:3]), inc=(sv, 1))
            em.on("vector", lambda e: e.tensor_scalar_mul(stat[0:32, 3:4], stat[0:32, 3:4], 127.0), inc=(sv, 1))
            em.on("vector", lambda e: e.reciprocal(stat[0:32, 4:5], stat[0:32, 3:4]), inc=(sv, 1))
            em.on("vector", lambda e: e.tensor_scalar_mul(vq[0:32, 0:1024], xnb[0:32, :], stat[0:32, 3:4]), inc=(sv, 1))
            em.on("vector", lambda e: e.tensor_copy(i32b[0:32, 0:1024], vq[0:32, 0:1024]), inc=(sv, 1))
            em.on("vector", lambda e: e.tensor_copy(hcb[0:32, 0:1024], i32b[0:32, 0:1024]), inc=(sv, 1))
            em.on("vector", lambda e: e.tensor_sub(vq[0:32, 0:1024], vq[0:32, 0:1024], hcb[0:32, 0:1024]), inc=(sv, 1))
            em.on("vector", lambda e: e.tensor_single_scalar(hext[0:32, 0:1024], vq[0:32, 0:1024], 0.5, ALU.is_ge), inc=(sv, 1))
            em.on("vector", lambda e: e.tensor_add(hcb[0:32, 0:1024], hcb[0:32, 0:1024], hext[0:32, 0:1024]), inc=(sv, 1))
            em.on("vector", lambda e: e.tensor_single_scalar(hext[0:32, 0:1024], vq[0:32, 0:1024], -0.5, ALU.is_le), inc=(sv, 1))
            em.on("vector", lambda e: e.tensor_sub(hcb[0:32, 0:1024], hcb[0:32, 0:1024], hext[0:32, 0:1024]), inc=(sv, 1))
            em.on("vector", lambda e: e.tensor_copy(xnb[0:32, :], hcb[0:32, 0:1024]), inc=(sv, 1))
            hq = sv.n
            for dc in range(8):
                transpose_tile(xnb[0:32, dc * 128:(dc + 1) * 128], hT[:, dc * 32:(dc + 1) * 32],
                               extra_wait=[(sv, hq)] if dc == 0 else (), wout=32, kin=32)
            em.on("tensor", lambda e: e.transpose(kb.ps_b[0:1, 512:544], stat[0:32, 4:5], C.ident[0:32, 0:32]),
                  wait=[(sv, sv.n)], inc=(st, 1))
            tv = st.n
            em.on("vector", lambda e: e.tensor_copy(row1[0:1, 0:32], kb.ps_b[0:1, 512:544]),
                  wait=[(st, tv)], inc=(sv, 1))
            rv = sv.n
            em.on("tensor", lambda e: e.matmul(kb.ps_b[:, 512:544], C.onesrow[0:1, 0:128], row1[0:1, 0:32],
                                               start=True, stop=True),
                  wait=[(sv, rv)], inc=(st, 1))
            tv = st.n
            em.on("vector", lambda e: e.tensor_copy(hsc[:, :], kb.ps_b[:, 512:544]),
                  wait=[(st, tv)], inc=(sv, 1))
            em.on("vector", lambda e: e.tensor_scalar_mul(hsc[:, :], hsc[:, :], rs_w1_bc[:, 0:1]), inc=(sv, 1))
            hscv = sv.n
            for ct in range(32):
                pb = ct % 2
                for dc in range(8):
                    w = []
                    if dc == 0:
                        w = [(sv, sv.n)]
                        if ct == 0:
                            w.append((sd, w1dma))
                    em.on("tensor", lambda e, ct=ct, dc=dc, pb=pb:
                          e.matmul(kb.ps_h7[:, pb * 512:pb * 512 + 32],
                                   w1q[:, dc * 4096 + 128 * ct: dc * 4096 + 128 * ct + 128],
                                   hT[:, dc * 32:(dc + 1) * 32],
                                   start=(dc == 0), stop=(dc == 7)),
                          wait=w, inc=(st, 1) if dc == 7 else None)
                tv = st.n
                em.on("vector", lambda e, ct=ct, pb=pb:
                      e.tensor_tensor(hh_sb[:, 32 * ct:32 * ct + 32], kb.ps_h7[:, pb * 512:pb * 512 + 32],
                                      hsc[:, :], ALU.mult),
                      wait=[(st, tv), (sv, hscv)], inc=(sv, 1))
            em.on("vector", lambda e: e.tensor_scalar(C.scratch1[:, :], C.par[:, :], -1.0, 1.0, ALU.mult, ALU.add),
                  inc=(sv, 1))
            hh3 = hh_sb[:, :].rearrange("p (c h) -> p c h", h=32)
            em.on("vector", lambda e: e.tensor_scalar_mul(hh3[:, :, 0:1], hh3[:, :, 0:1], C.par[:, 0:1]), inc=(sv, 1))
            em.on("vector", lambda e: e.tensor_scalar_mul(hh3[:, :, 31:32], hh3[:, :, 31:32], C.scratch1[:, 0:1]), inc=(sv, 1))
            hh_done = sv.n
            if "BH" in kb.dbg:
                em.dma("gpsimd", lambda e: e.dma_start(out=kb.dbg["BH"], in_=hh_sb[:, :]),
                       wait=[(sv, hh_done)], sem=sg)

            w2_free = [0, 0]
            obuf_free = [0, 0]
            for i in range(OB):
                em.dma("gpsimd", lambda e, i=i: e.dma_start(out=xo[:, :], in_=kb.xown[i, :, :]),
                       wait=[(sv, sv.n), (ssc, ssc.n)] + ([(sg, xbulkv)] if i == 0 else []), sem=sg)
                ldx = sg.n
                em.dma("sync", lambda e, i=i: e.dma_start(out=asb[:, :], in_=kb.dA[i, :, :]),
                       wait=[(sv, sv.n)], sem=sd)
                lda = sd.n
                ln_steps(kb, em, xo[:, :], xnb[:, :], stat[:, 0:1], stat[:, 1:2], vq[:, 0:1024], 1e-5,
                         first_wait=[(sg, ldx)])
                em.on("vector", lambda e: e.reduce_max(stat[:, 2:3], xnb[:, :], AX.X, apply_absolute_value=True), inc=(sv, 1))
                em.on("vector", lambda e: e.tensor_scalar_max(stat[:, 2:3], stat[:, 2:3], 1e-5), inc=(sv, 1))
                em.on("vector", lambda e: e.reciprocal(stat[:, 3:4], stat[:, 2:3]), inc=(sv, 1))
                em.on("vector", lambda e: e.tensor_scalar_mul(stat[:, 3:4], stat[:, 3:4], 127.0), inc=(sv, 1))
                em.on("vector", lambda e: e.reciprocal(stat[:, 4:5], stat[:, 3:4]), inc=(sv, 1))
                em.on("vector", lambda e: e.tensor_scalar_mul(vq[:, 0:1024], xnb[:, :], stat[:, 3:4]), inc=(sv, 1))
                em.on("vector", lambda e: e.tensor_copy(i32b[:, 0:1024], vq[:, 0:1024]), inc=(sv, 1))
                em.on("vector", lambda e: e.tensor_copy(hcb[:, 0:1024], i32b[:, 0:1024]), inc=(sv, 1))
                em.on("vector", lambda e: e.tensor_sub(vq[:, 0:1024], vq[:, 0:1024], hcb[:, 0:1024]), inc=(sv, 1))
                em.on("vector", lambda e: e.tensor_single_scalar(hext[:, 0:1024], vq[:, 0:1024], 0.5, ALU.is_ge), inc=(sv, 1))
                em.on("vector", lambda e: e.tensor_add(hcb[:, 0:1024], hcb[:, 0:1024], hext[:, 0:1024]), inc=(sv, 1))
                em.on("vector", lambda e: e.tensor_single_scalar(hext[:, 0:1024], vq[:, 0:1024], -0.5, ALU.is_le), inc=(sv, 1))
                em.on("vector", lambda e: e.tensor_sub(hcb[:, 0:1024], hcb[:, 0:1024], hext[:, 0:1024]), inc=(sv, 1))
                xqv = sv.n
                for dc in range(8):
                    transpose_tile(hcb[:, dc * 128:(dc + 1) * 128], xqT[:, dc * 128:(dc + 1) * 128],
                                   extra_wait=[(sv, xqv)] if dc == 0 else ())
                em.on("tensor", lambda e: e.transpose(kb.ps_b[0:1, 512:640], stat[:, 4:5], C.ident[:, :]),
                      wait=[(sv, sv.n)], inc=(st, 1))
                tv = st.n
                em.on("vector", lambda e: e.tensor_copy(row1[0:1, 0:128], kb.ps_b[0:1, 512:640]),
                      wait=[(st, tv)], inc=(sv, 1))
                rv = sv.n
                em.on("tensor", lambda e: e.matmul(kb.ps_b[:, 512:640], C.onesrow[0:1, 0:128], row1[0:1, 0:128],
                                                   start=True, stop=True),
                      wait=[(sv, rv)], inc=(st, 1))
                tv = st.n
                em.on("vector", lambda e: e.tensor_copy(rrow[:, :], kb.ps_b[:, 512:640]),
                      wait=[(st, tv), (sp7, sp7.n)], inc=(sv, 1))
                em.on("vector", lambda e: e.tensor_scalar_mul(rrow[:, :], rrow[:, :], rs_w1_bc[:, 0:1]), inc=(sv, 1))
                rrv = sv.n
                for ct in range(32):
                    pb = ct % 2
                    for dc in range(8):
                        w = [(sv, sv.n)] if dc == 0 else []
                        em.on("tensor", lambda e, ct=ct, dc=dc, pb=pb:
                              e.matmul(kb.ps_h7[:, pb * 512:pb * 512 + 128],
                                       w1q[:, dc * 4096 + 128 * ct: dc * 4096 + 128 * ct + 128],
                                       xqT[:, dc * 128:(dc + 1) * 128],
                                       start=(dc == 0), stop=(dc == 7)),
                              wait=w, inc=(st, 1) if dc == 7 else None)
                    tv = st.n
                    em.on("vector", lambda e, ct=ct, pb=pb:
                          e.tensor_tensor(hext[:, 130 * ct + 1:130 * ct + 129],
                                          kb.ps_h7[:, pb * 512:pb * 512 + 128], rrow[:, :], ALU.mult),
                          wait=[(st, tv), (sv, rrv)] + ([(sp7, sp7.n)] if ct == 0 else []), inc=(sv, 1))
                hx3 = hext[:, :].rearrange("p (c w) -> p c w", w=130)
                em.on("vector", lambda e, i=i: e.tensor_copy(hx3[:, :, 0:1], hh3[:, :, i:i + 1]),
                      wait=[(sv, hh_done)], inc=(sv, 1))
                em.on("vector", lambda e, i=i: e.tensor_copy(hx3[:, :, 129:130], hh3[:, :, 16 + i:17 + i]),
                      inc=(sv, 1))
                hx_done = sv.n
                for ct in range(32):
                    ceng = "vector"
                    csem = sv
                    w0 = []
                    em.on(ceng, lambda e, ct=ct:
                          e.tensor_scalar(hcb[:, 128 * ct:128 * ct + 128], hext[:, 130 * ct + 1:130 * ct + 129],
                                          C.cw[:, 3 * ct + 1:3 * ct + 2], C.cb[:, ct:ct + 1], ALU.mult, ALU.add),
                          wait=w0, inc=(csem, 1))
                    em.on(ceng, lambda e, ct=ct:
                          e.scalar_tensor_tensor(hcb[:, 128 * ct:128 * ct + 128], hext[:, 130 * ct:130 * ct + 128],
                                                 C.cw[:, 3 * ct:3 * ct + 1], hcb[:, 128 * ct:128 * ct + 128],
                                                 ALU.mult, ALU.add),
                          inc=(csem, 1))
                    em.on(ceng, lambda e, ct=ct:
                          e.scalar_tensor_tensor(hcb[:, 128 * ct:128 * ct + 128], hext[:, 130 * ct + 2:130 * ct + 130],
                                                 C.cw[:, 3 * ct + 2:3 * ct + 3], hcb[:, 128 * ct:128 * ct + 128],
                                                 ALU.mult, ALU.add),
                          inc=(csem, 1))
                cvv = sv.n
                cvv_g = sp7.n
                if i == 0 and "BX0" in kb.dbg:
                    em.dma("gpsimd", lambda e: e.dma_start(out=kb.dbg["BX0"], in_=hext[:, :]),
                           wait=[(sv, cvv)], sem=sg)
                    em.on("vector", None, wait=[(sg, sg.n)])
                if i == 0 and "BC0" in kb.dbg:
                    em.dma("gpsimd", lambda e: e.dma_start(out=kb.dbg["BC0"], in_=hcb[:, :]),
                           wait=[(sv, cvv)], sem=sg)
                    em.on("vector", None, wait=[(sg, sg.n)])
                em.on("scalar", lambda e: e.activation(hext[:, 0:4096], hcb[:, :], AF.Silu),
                      wait=[(sv, cvv), (sp7, cvv_g)], inc=(ssc, 1))
                silv = ssc.n
                em.on("scalar", lambda e: e.activation(vq[:, :], hext[:, 0:4096], AF.Abs),
                      wait=[(ssc, silv)], inc=(ssc, 1))
                absv = ssc.n
                em.on("vector", lambda e: e.tensor_tensor(vq[:, 0:2048], vq[:, 0:2048], vq[:, 2048:4096], ALU.max),
                      wait=[(ssc, absv)], inc=(sv, 1))
                em.on("vector", lambda e: e.tensor_tensor(vq[:, 0:1024], vq[:, 0:1024], vq[:, 1024:2048], ALU.max), inc=(sv, 1))
                em.on("vector", lambda e: e.tensor_tensor(vq[:, 0:512], vq[:, 0:512], vq[:, 512:1024], ALU.max), inc=(sv, 1))
                em.on("vector", lambda e: e.tensor_tensor(vq[:, 0:256], vq[:, 0:256], vq[:, 256:512], ALU.max), inc=(sv, 1))
                em.on("vector", lambda e: e.tensor_tensor(vq[:, 0:128], vq[:, 0:128], vq[:, 128:256], ALU.max), inc=(sv, 1))
                mtv = sv.n
                em.on("tensor", lambda e: e.transpose(kb.ps_b[:, 512:640], vq[:, 0:128], C.ident[:, :]),
                      wait=[(sv, mtv), (sv, sv.n)], inc=(st, 1))
                tv = st.n
                em.on("vector", lambda e: e.tensor_copy(rrow[:, :], kb.ps_b[:, 512:640]),
                      wait=[(st, tv)], inc=(sv, 1))
                em.on("vector", lambda e: e.reduce_max(stat[:, 2:3], rrow[:, :], AX.X), inc=(sv, 1))
                em.on("vector", lambda e: e.tensor_scalar_max(stat[:, 2:3], stat[:, 2:3], 1e-5), inc=(sv, 1))
                em.on("vector", lambda e: e.reciprocal(stat[:, 3:4], stat[:, 2:3]), inc=(sv, 1))
                em.on("vector", lambda e: e.tensor_scalar_mul(stat[:, 3:4], stat[:, 3:4], 127.0), inc=(sv, 1))
                em.on("vector", lambda e: e.reciprocal(stat[:, 4:5], stat[:, 3:4]), inc=(sv, 1))
                em.on("tensor", lambda e: e.transpose(kb.ps_b[0:1, 512:640], stat[:, 3:4], C.ident[:, :]),
                      wait=[(sv, sv.n)], inc=(st, 1))
                tv = st.n
                em.on("vector", lambda e: e.tensor_copy(row1[0:1, 0:128], kb.ps_b[0:1, 512:640]),
                      wait=[(st, tv)], inc=(sv, 1))
                rv = sv.n
                em.on("tensor", lambda e: e.matmul(kb.ps_b[:, 512:640], C.onesrow[0:1, 0:128], row1[0:1, 0:128],
                                                   start=True, stop=True),
                      wait=[(sv, rv)], inc=(st, 1))
                tv = st.n
                em.on("vector", lambda e: e.tensor_copy(rrow[:, :], kb.ps_b[:, 512:640]),
                      wait=[(st, tv)], inc=(sv, 1))
                scbv = sv.n
                for ct in range(32):
                    ceng = "vector"
                    csem = sv
                    w0 = [(sv, scbv)] if ct == 0 else []
                    em.on(ceng, lambda e, ct=ct:
                          e.tensor_tensor(vq[:, 128 * ct:128 * ct + 128], hext[:, 128 * ct:128 * ct + 128],
                                          rrow[:, :], ALU.mult),
                          wait=w0, inc=(csem, 1))
                for ceng, csem, c0, c1 in (("vector", sv, 0, 4096),):
                    em.on(ceng, lambda e, c0=c0, c1=c1: e.tensor_copy(i32b[:, c0:c1], vq[:, c0:c1]), inc=(csem, 1))
                    em.on(ceng, lambda e, c0=c0, c1=c1: e.tensor_copy(hcb[:, c0:c1], i32b[:, c0:c1]), inc=(csem, 1))
                    em.on(ceng, lambda e, c0=c0, c1=c1: e.tensor_sub(vq[:, c0:c1], vq[:, c0:c1], hcb[:, c0:c1]), inc=(csem, 1))
                    em.on(ceng, lambda e, c0=c0, c1=c1: e.tensor_single_scalar(hext[:, c0:c1], vq[:, c0:c1], 0.5, ALU.is_ge), inc=(csem, 1))
                    em.on(ceng, lambda e, c0=c0, c1=c1: e.tensor_add(hcb[:, c0:c1], hcb[:, c0:c1], hext[:, c0:c1]), inc=(csem, 1))
                    em.on(ceng, lambda e, c0=c0, c1=c1: e.tensor_single_scalar(hext[:, c0:c1], vq[:, c0:c1], -0.5, ALU.is_le), inc=(csem, 1))
                    em.on(ceng, lambda e, c0=c0, c1=c1: e.tensor_sub(hcb[:, c0:c1], hcb[:, c0:c1], hext[:, c0:c1]), inc=(csem, 1))
                    em.on(ceng, lambda e, c0=c0, c1=c1: e.tensor_copy(aqb[:, c0:c1], hcb[:, c0:c1]), inc=(csem, 1))
                aqv = sv.n
                aqv_g = sp7.n
                if i == 0 and "BQ0" in kb.dbg:
                    em.dma("gpsimd", lambda e: e.dma_start(out=kb.dbg["BQ0"], in_=hcb[:, :]),
                           wait=[(sv, aqv), (sp7, aqv_g)], sem=sg)
                    em.on("vector", None, wait=[(sg, sg.n)])
                em.on("vector", lambda e: e.tensor_tensor(stat[:, 5:6], stat[:, 4:5], rs_w2_bc[:, 0:1], ALU.mult),
                      inc=(sv, 1))
                csv = sv.n
                for ck in range(32):
                    b2_ = ck % 2
                    qd = sd if b2_ == 0 else sg
                    em.dma("sync" if b2_ == 0 else "gpsimd", lambda e, ck=ck, b2_=b2_:
                           e.dma_start(out=w2qb[b2_][:, :], in_=kb.dW2qT[128 * ck:128 * ck + 128, :]),
                           wait=[(st, w2_free[b2_])], sem=qd)
                    wdma = qd.n
                    for dw in range(2):
                        w = []
                        if dw == 0:
                            w = [(qd, wdma)]
                            if ck == 0:
                                w.append((sv, aqv))
                            if ck == 16:
                                w.append((sp7, aqv_g))
                        em.on("tensor", lambda e, ck=ck, dw=dw, b2_=b2_:
                              e.matmul(kb.ps_o7[:, dw * 512:dw * 512 + 512],
                                       aqb[:, 128 * ck:128 * ck + 128],
                                       w2qb[b2_][:, dw * 512:dw * 512 + 512],
                                       start=(ck == 0), stop=(ck == 31)),
                              wait=w, inc=(st, 1) if (dw == 1) else None)
                    w2_free[b2_] = st.n
                mmv = st.n
                for dw in range(2):
                    em.on("vector", lambda e, dw=dw:
                          e.scalar_tensor_tensor(obuf[dw][:, :], kb.ps_o7[:, dw * 512:dw * 512 + 512],
                                                 stat[:, 5:6], xo[:, dw * 512:dw * 512 + 512],
                                                 ALU.mult, ALU.add),
                          wait=[(st, mmv), (sv, csv), (sd, lda), (so, obuf_free[dw])], inc=(sv, 1))
                    em.on("vector", lambda e, dw=dw:
                          e.tensor_tensor(obuf[dw][:, :], obuf[dw][:, :], asb[:, dw * 512:dw * 512 + 512], ALU.add),
                          inc=(sv, 1))
                    ev = sv.n
                    em.dma("gpsimd", lambda e, i=i, dw=dw:
                           e.dma_start(out=kb.out[128 * i:128 * i + 128, 512 * dw:512 * dw + 512], in_=obuf[dw][:, :]),
                           wait=[(sv, ev)], sem=so)
                    obuf_free[dw] = so.n

            em.on("gpsimd", None, wait=[(so, so.n), (sg, sg.n)])
            em.on("sync", None, wait=[(sd, sd.n)])
    kb.out_written = True


DEBUG_SRC = {}


def finish(kb):
    nc, sem = kb.nc, kb.sem
    sd = sem("fin_dma")
    with Phase(kb) as em:
        if not kb.out_written:
            z = kb.sb("fin_zero", [128, D])
            em.on("vector", lambda e: e.memset(z[:, :], 0.0), inc=(sem("fin_v"), 1))
            vv = kb.sems["fin_v"].n
            for i in range(OB):
                em.dma("gpsimd", lambda e, i=i: e.dma_start(out=kb.out[i * 128:(i + 1) * 128, :], in_=z[:, :]),
                       wait=[(sem("fin_v"), vv)], sem=sd)
        for nm, t in kb.dbg.items():
            fn = DEBUG_SRC.get(nm)
            if fn is None:
                continue
            import inspect as _ins
            if len(_ins.signature(fn).parameters) == 2:
                em.dma("gpsimd", lambda e, t=t, fn=fn: e.dma_start(out=t, in_=fn(kb, e)), sem=sd)
            else:
                src = fn(kb)
                em.dma("gpsimd", lambda e, t=t, src=src: e.dma_start(out=t, in_=src), sem=sd)
        em.on("gpsimd", None, wait=[(sd, sd.n)])



def stage_inputs(inputs, core):
    s = core // 2
    x = np.ascontiguousarray(inputs["L0_attn_x"][s])
    m = {
        "xseq": x,
        "l0_wq": inputs["L0_attn_wq"], "l0_bq": inputs["L0_attn_bq"],
        "l0_wk": inputs["L0_attn_wk"], "l0_bk": inputs["L0_attn_bk"],
        "l0_wv": inputs["L0_attn_wv"], "l0_bv": inputs["L0_attn_bv"],
        "l0_wo": inputs["L0_attn_wo"], "l0_bo": inputs["L0_attn_bo"],
        "ln_g": inputs["L0_ffn_ln_g"], "ln_b": inputs["L0_ffn_ln_b"],
        "rW": inputs["L0_ffn_rW"], "rb": inputs["L0_ffn_rb"],
        "W1": inputs["L0_ffn_W1"], "b1": inputs["L0_ffn_b1"],
        "W2": inputs["L0_ffn_W2"], "b2": inputs["L0_ffn_b2"],
        "gam": inputs["L1_attn_ln_gamma"], "bet": inputs["L1_attn_ln_beta"],
        "l1_wq": inputs["L1_attn_Wq"], "l1_bq": inputs["L1_attn_bq"],
        "l1_wk": inputs["L1_attn_Wk"], "l1_bk": inputs["L1_attn_bk"],
        "l1_wv": inputs["L1_attn_Wv"], "l1_bv": inputs["L1_attn_bv"],
        "l1_wo": inputs["L1_attn_Wo"], "l1_bo": inputs["L1_attn_bo"],
        "mu": inputs["L1_attn_mu"].reshape(1, 1),
        "w1f": inputs["L1_ffn_w1"], "convw": inputs["L1_ffn_conv_w"].reshape(4096, 3),
        "convb": inputs["L1_ffn_conv_b"], "w2f": inputs["L1_ffn_w2"],
        "parconst": np.array([[0.0], [0.0], [1.0]], dtype=np.float32),
    }
    return {k: np.ascontiguousarray(np.asarray(v, dtype=np.float32)) for k, v in m.items()}


_NC_CACHE = {}


def run(inputs, upto=99):
    key = upto
    if key not in _NC_CACHE:
        _NC_CACHE[key] = build_nc(upto)
    nc = _NC_CACHE[key]
    in_maps = [stage_inputs(inputs, c) for c in range(8)]
    res = run_bass_kernel_spmd(nc, in_maps, core_ids=list(range(8)))
    return res


def kernel(**inputs):
    res = run(inputs)
    out = np.zeros((4, T, D), dtype=np.float32)
    for c in range(8):
        s, p = c // 2, c % 2
        xo = np.asarray(res.results[c]["xout"])
        for i in range(OB):
            g = 2 * i + p
            out[s, g * 128:(g + 1) * 128, :] = xo[i * 128:(i + 1) * 128, :]
    return out


# version 07
# speedup vs baseline: 1.0405x; 1.0332x over previous
import numpy as np
from contextlib import ExitStack

import concourse.bass as bass
import concourse.mybir as mybir
from concourse.bass_utils import run_bass_kernel_spmd

F32 = mybir.dt.float32
BF16 = mybir.dt.bfloat16
I32 = mybir.dt.int32
AF = mybir.ActivationFunctionType
ALU = mybir.AluOpType
AX = mybir.AxisListType

T, D, H, HD = 4096, 1024, 16, 64
NB = 32
OB = 16
OT = OB * 128

DEBUG = {}
EXC_MODE = 'full'

ENGINES = ("sync", "vector", "scalar", "tensor", "gpsimd")


class Sem:
    def __init__(self, handle, name):
        self.h = handle
        self.name = name
        self.n = 0

    def bump(self, amt):
        self.n += amt
        return self.n


AUTO_CHAIN = {"vector", "scalar", "gpsimd"}


class Emitter:

    def __init__(self, kb):
        self.kb = kb
        self.steps = {e: [] for e in ENGINES}

    def on(self, eng, fn, wait=(), inc=None, kind="op"):
        kb = self.kb
        waits = list(wait)
        auto = eng in AUTO_CHAIN and fn is not None and kind == "op"
        if auto:
            sig = kb.chain_sig.get(eng)
            if sig is not None:
                waits.append(sig)
            if inc is None:
                inc = (kb.sem("auto_" + eng), 1)
        post = None
        if inc is not None:
            sem, amt = inc
            post = sem.bump(amt)
            if auto:
                kb.chain_sig[eng] = (sem, post)
        self.steps[eng].append((fn, tuple(waits), None if inc is None else (inc[0], inc[1])))
        return post

    def dma(self, eng, fn, wait=(), sem=None):
        return self.on(eng, fn, wait=wait, inc=(sem, 16), kind="dma")

    def emit(self, block):
        for eng_name in getattr(self, "engine_order", None) or ENGINES:
            steps = self.steps[eng_name]
            if not steps:
                continue

            def make_body(steps=steps):
                def body(eng):
                    for fn, waits, inc in steps:
                        for sem, val in waits:
                            eng.wait_ge(sem.h, val)
                        if fn is None:
                            continue
                        inst = fn(eng)
                        if inc is not None:
                            inst.then_inc(inc[0].h, inc[1])
                return body

            getattr(block, eng_name)(make_body())
        self.steps = {e: [] for e in ENGINES}


class Phase:

    def __init__(self, kb):
        self.kb = kb
        self.em = Emitter(kb)

    def __enter__(self):
        return self.em

    def __exit__(self, et, ev, tb):
        if et is not None:
            return False
        kb = self.kb
        kb.phase_no += 1
        n = kb.phase_no
        for e in ENGINES:
            self.em.on(e, lambda x: x.nop(), inc=(kb.phase_sems[e], 1))
        for e in ENGINES:
            for e2 in ENGINES:
                if e2 != e:
                    self.em.on(e, None, wait=[(kb.phase_sems[e2], n)])
        self.em.emit(kb.block)
        return False


class KB:
    pass


def ceil_div(a, b):
    return -(-a // b)


def ln_steps(kb, em, src_ap, dst_ap, m, vvar, sq, eps, first_wait=()):
    C = kb.C
    sv = kb.sem(kb.cur_sv)
    ssc = kb.sem(kb.cur_ssc)
    if not isinstance(sq, bass.AP):
        sq = sq[:, :]
    epsap = C.fc[:, 0:1] if eps == 1e-5 else C.fc[:, 1:2]
    em.on("vector", lambda e: e.reduce_sum(m, src_ap, AX.X), wait=first_wait, inc=(sv, 1))
    em.on("vector", lambda e: e.tensor_scalar_mul(m, m, -1.0 / D), inc=(sv, 1))
    em.on("vector", lambda e: e.tensor_scalar_add(dst_ap, src_ap, m), inc=(sv, 1))
    pv = sv.n
    em.on("scalar", lambda e: e.activation(sq, dst_ap, AF.Square, accum_out=vvar),
          wait=[(sv, pv)], inc=(ssc, 1))
    sc0 = ssc.n
    em.on("vector", lambda e: e.tensor_scalar_mul(vvar, vvar, 1.0 / D),
          wait=[(ssc, sc0)], inc=(sv, 1))
    pv = sv.n
    em.on("scalar", lambda e: e.activation(vvar, vvar, AF.Sqrt, bias=epsap),
          wait=[(sv, pv)], inc=(ssc, 1))
    sc0 = ssc.n
    em.on("vector", lambda e: e.reciprocal(vvar, vvar), wait=[(ssc, sc0)], inc=(sv, 1))
    em.on("vector", lambda e: e.tensor_scalar_mul(dst_ap, dst_ap, vvar), inc=(sv, 1))



def build_nc(upto=99):
    nc = bass.Bass("TRN2", target_bir_lowering=False, num_devices=8)
    kb = KB()
    kb.nc = nc
    kb.phase_no = 0
    kb.out_written = False
    kb.chain_sig = {}
    kb.rt = {}

    def rtv(e, eng, name):
        key = (eng, name)
        sv = kb.rt.get(key)
        if sv is None:
            if name == "par":
                sv = e.partition_id() % 2
            elif name == "par128":
                sv = rtv(e, eng, "par") * 128
            elif name == "par127":
                sv = rtv(e, eng, "par") * 127
            elif name == "par2048":
                sv = rtv(e, eng, "par") * 2048
            elif name == "truthy":
                sv = e.partition_id() >= 0
            elif name == "zsv":
                e.reg_mov(kb.exreg, 0)
                sv = e.snap(kb.exreg, min_val=0, max_val=1)
            kb.rt[key] = sv
        return sv

    kb.rtv = rtv

    I = {}

    def din(name, shape):
        I[name] = nc.dram_tensor(name, list(shape), F32, kind="ExternalInput").ap()
        return I[name]

    din("xseq", (T, D))
    for nm, sh in [
        ("l0_wq", (D, D)), ("l0_bq", (D,)), ("l0_wk", (D, D)), ("l0_bk", (D,)),
        ("l0_wv", (D, D)), ("l0_bv", (D,)), ("l0_wo", (D, D)), ("l0_bo", (D,)),
        ("ln_g", (D,)), ("ln_b", (D,)), ("rW", (D, 8)), ("rb", (8,)),
        ("W1", (8, D, 2048)), ("b1", (8, 2048)), ("W2", (8, 2048, D)), ("b2", (8, D)),
        ("gam", (D,)), ("bet", (D,)),
        ("l1_wq", (D, D)), ("l1_bq", (D,)), ("l1_wk", (D, D)), ("l1_bk", (D,)),
        ("l1_wv", (D, D)), ("l1_bv", (D,)), ("l1_wo", (D, D)), ("l1_bo", (D,)),
        ("mu", (1, 1)),
        ("w1f", (4096, D)), ("convw", (4096, 3)), ("convb", (4096,)),
        ("w2f", (D, 4096)),
        ("parconst", (3, 1)),
    ]:
        din(nm, sh)
    kb.I = I

    out = nc.dram_tensor("xout", [OT, D], F32, kind="ExternalOutput").ap()
    kb.out = out
    kb.dbg = {}
    for nm, sh in DEBUG.items():
        kb.dbg[nm] = nc.dram_tensor("dbg_" + nm, list(sh), F32, kind="ExternalOutput").ap()

    kb.dKT = nc.dram_tensor("dKT", [8, D, 512], F32).ap()
    kb.dV = nc.dram_tensor("dV", [8, 512, 16 * 65], F32).ap()
    kb.dQT = nc.dram_tensor("dQT", [NB, D, 128], F32).ap()
    kb.dA = nc.dram_tensor("dA", [OB, 128, D], F32).ap()
    kb.dAS = nc.dram_tensor("dAS", [OB, 128, D], F32).ap()
    kb.xsh = nc.dram_tensor("xsh", [T + 1, D], F32, addr_space="Shared").ap()
    kb.x1tmp = nc.dram_tensor("x1tmp", [T, D], F32).ap()
    kb.bar_in = nc.dram_tensor("bar_in", [1, 16], F32).ap()
    kb.bar_out = nc.dram_tensor("bar_out", [1, 16], F32).ap()
    kb.x1pad = nc.dram_tensor("x1pad", [17 * 256 + 4, D], F32).ap()
    kb.x1v = kb.x1pad[1:T + 1, :]
    kb.dQTown = nc.dram_tensor("dQTown", [OB, D, 128], F32).ap()
    kb.xown = nc.dram_tensor("xown", [OB, 128, D], F32).ap()
    kb.x1own = nc.dram_tensor("x1own", [OB, 128, D], F32).ap()
    kb.dW1h = nc.dram_tensor("dW1h", [8, D, 2048], BF16).ap()
    kb.dW1l = nc.dram_tensor("dW1l", [8, D, 2048], BF16).ap()
    kb.dW1qT = nc.dram_tensor("dW1qT", [D, 4096], BF16).ap()
    kb.dW2qT = nc.dram_tensor("dW2qT", [4096, D], BF16).ap()
    kb.dUN = None

    es = ExitStack()
    kb.es = es

    kb.sems = {}

    def sem(name):
        if name not in kb.sems:
            kb.sems[name] = Sem(es.enter_context(nc.semaphore(name)), name)
        return kb.sems[name]

    kb.sem = sem
    kb.phase_sems = {e: sem("ph_" + e) for e in ENGINES}

    def sb(name, shape, dtype=F32, stack=None):
        return (stack or es).enter_context(nc.sbuf_tensor(name, list(shape), dtype))

    kb.sb = sb

    C = KB()
    kb.C = C
    C.ident = sb("c_ident", [128, 128])
    C.iota_km = sb("c_iotakm", [128, 128])
    C.maskA = sb("c_maskA", [128, 128])
    C.maskB = sb("c_maskB", [128, 128])
    C.par = sb("c_par", [128, 1])
    C.bq = sb("c_bq", [128, 8])
    C.bk = sb("c_bk", [128, 8])
    C.bv_bc = sb("c_bv_bc", [128, D])
    C.bo_bc = sb("c_bo_bc", [128, D])
    C.rb_bc = sb("c_rb_bc", [128, 8])
    C.lng_bc = sb("c_lng_bc", [128, D])
    C.lnb_bc = sb("c_lnb_bc", [128, D])
    C.gam_bc = sb("c_gam_bc", [128, D])
    C.bet_bc = sb("c_bet_bc", [128, D])
    C.b1 = sb("c_b1", [128, 8 * 16])
    C.b2 = sb("c_b2", [8, D])
    C.negmu = sb("c_negmu", [128, 1])
    C.cw = sb("c_cw", [128, 32 * 3])
    C.cb = sb("c_cb", [128, 32])
    C.ones = sb("c_ones", [128, 1])
    C.scratch1 = sb("c_scr1", [128, 1])
    C.fc = sb("c_fc", [128, 8])
    C.onesrow = sb("c_onesrow", [128, 128])
    C.identB = sb("c_identB", [128, 128])

    kb.ps0 = es.enter_context(nc.psum_tensor("ps_bcast", [128, 512], F32))
    kb.exreg = es.enter_context(nc.gpsimd.register("exreg"))
    kb.block = es.enter_context(nc.Block())

    def par_sv(eng):
        return eng.partition_id() % 2

    kb.par_sv = par_sv

    build_P0(kb)
    if upto >= 1:
        with nc.named_scope("P1prep0"):
            build_prep(kb, layer=0)
    if upto >= 2:
        with nc.named_scope("P2attn0"):
            build_attn(kb, layer=0)
    if upto >= 3:
        with nc.named_scope("P3moe"):
            build_moe(kb)
    if upto >= 4:
        with nc.named_scope("P4exch"):
            build_exchange(kb)
    if upto >= 5:
        with nc.named_scope("P5prep1"):
            build_prep(kb, layer=1)
    if upto >= 6:
        with nc.named_scope("P6attn1"):
            build_attn(kb, layer=1)
    if upto >= 7:
        with nc.named_scope("P7bitffn"):
            build_bitffn(kb)

    finish(kb)
    es.close()
    return nc



def build_P0(kb):
    nc, C, I, sem = kb.nc, kb.C, kb.I, kb.sem
    sd = sem("p0_dma")
    sc = sem("p0_cmp")
    with Phase(kb) as em:
        def ld(dst, src, wait=(), slow=False):
            em.dma("sync", lambda e, d=dst, s=src, slow=slow:
                   e.dma_start(out=d, in_=s, allow_slow_non_contiguous=slow),
                   wait=wait, sem=sd)

        ld(C.bq[:, :], I["l0_bq"].rearrange("(c p) -> p c", p=128), slow=True)
        ld(C.bk[:, :], I["l0_bk"].rearrange("(c p) -> p c", p=128), slow=True)
        ld(C.b1[:, :].rearrange("p (e c) -> p e c", e=8), I["b1"].rearrange("e (c p) -> p e c", p=128), slow=True)
        ld(C.b2[:, :], I["b2"][:, :])
        ld(C.cw[:, :].rearrange("p (c j) -> p c j", j=3), I["convw"].rearrange("(c p) j -> p c j", p=128), slow=True)
        ld(C.cb[:, :], I["convb"].rearrange("(c p) -> p c", p=128), slow=True)
        def par_load(e):
            parv = e.partition_id() % 2
            return e.dma_start(out=C.par[0:1, 0:1], in_=I["parconst"][bass.ds(parv + 1, 1), :])
        spar = sem("p0_pdma")
        em.dma("gpsimd", par_load, sem=spar)

        ps0 = kb.ps0
        stn = sem("p0_ten")

        em.on("gpsimd", lambda e: e.iota(C.iota_km[:, :], pattern=[[1, 128]], base=0,
                                         channel_multiplier=-1,
                                         allow_small_or_imprecise_dtypes=True),
              inc=(sc, 1))
        em.on("vector", lambda e: e.tensor_single_scalar(C.ident[:, :], C.iota_km[:, :], 0.0, ALU.is_equal),
              wait=[(sc, sc.n)], inc=(sc, 1))
        em.on("vector", lambda e: e.tensor_single_scalar(C.identB[:, :], C.iota_km[:, :], 64.0, ALU.is_equal),
              inc=(sc, 1))
        em.on("vector", lambda e: e.memset(C.onesrow[:, :], 1.0), inc=(sc, 1))
        onesrow_done = sc.n

        def bcast_mm(dst, src_row, n, wait=()):
            w = list(wait) + [(sc, onesrow_done), (sc, sc.n)]
            em.on("tensor", lambda e, dst=dst, src_row=src_row, n=n:
                  e.matmul(ps0[:, 0:n], C.onesrow[0:1, 0:128], src_row, start=True, stop=True),
                  wait=w, inc=(stn, 1))
            tv = stn.n
            em.on("vector", lambda e, dst=dst, n=n: e.tensor_copy(dst, ps0[:, 0:n]),
                  wait=[(stn, tv)], inc=(sc, 1))

        bcast_mm(C.par[:, 0:1], C.par[0:1, 0:1], 1, wait=[(spar, spar.n)])
        em.on("vector", lambda e: e.tensor_scalar_mul(C.scratch1[:, :], C.par[:, :], -128.0),
              wait=[(sc, sc.n)], inc=(sc, 1))
        em.on("vector", lambda e: e.tensor_single_scalar(C.maskA[:, :], C.iota_km[:, :], C.scratch1[:, 0:1], ALU.is_ge),
              wait=[(sc, sc.n)], inc=(sc, 1))
        em.on("vector", lambda e: e.tensor_scalar(C.scratch1[:, :], C.par[:, :], -128.0, 128.0, ALU.mult, ALU.add),
              wait=[(sc, sc.n)], inc=(sc, 1))
        em.on("vector", lambda e: e.tensor_single_scalar(C.maskB[:, :], C.iota_km[:, :], C.scratch1[:, 0:1], ALU.is_ge),
              wait=[(sc, sc.n)], inc=(sc, 1))
        em.on("vector", lambda e: e.memset(C.ones[:, :], 1.0), inc=(sc, 1))
        for j, v in [(0, 1e-5), (1, 1e-6), (2, 0.125), (3, 0.5)]:
            em.on("vector", lambda e, j=j, v=v: e.memset(C.fc[:, j:j + 1], v), inc=(sc, 1))

        row = kb.sb("p0_row", [1, D])

        def bcast(dst, nm, n=D, rearr="(a d) -> a d"):
            ld(row[0:1, 0:n], I[nm].rearrange(rearr, a=1) if rearr else I[nm],
               wait=[(sc, sc.n)])
            nl = sd.n
            for j in range(0, n, 512):
                w = min(512, n - j)
                bcast_mm(dst[:, j:j + w], row[0:1, j:j + w], w, wait=[(sd, nl)])

        bcast(C.bv_bc[:, :], "l0_bv")
        bcast(C.bo_bc[:, :], "l0_bo")
        bcast(C.lng_bc[:, :], "ln_g")
        bcast(C.lnb_bc[:, :], "ln_b")
        bcast(C.gam_bc[:, :], "gam")
        bcast(C.bet_bc[:, :], "bet")
        bcast(C.rb_bc[:, :], "rb", n=8)
        ld(row[0:1, 0:1], I["mu"][0:1, 0:1], wait=[(sc, sc.n)])
        nl = sd.n
        em.on("vector", lambda e: e.tensor_scalar_mul(row[0:1, 0:1], row[0:1, 0:1], -1.0),
              wait=[(sd, nl)], inc=(sc, 1))
        bcast_mm(C.negmu[:, :], row[0:1, 0:1], 1)
        em.on("sync", None, wait=[(sd, sd.n)])
        em.on("gpsimd", None, wait=[(sc, sc.n), (spar, spar.n)])



def build_prep(kb, layer):
    nc, C, I, sem = kb.nc, kb.C, kb.I, kb.sem
    L = f"l{layer}"
    with ExitStack() as ps:
        sbl = lambda n, s, d=F32: kb.sb(f"{L}p_" + n, s, d, stack=ps)
        wq = sbl("wq", [128, 8 * D])
        wk = sbl("wk", [128, 8 * D])
        wv = sbl("wv", [128, 8 * D])
        wstage = sbl("wstage", [128, D])
        xs = sbl("xs", [128, 4 * D])
        xn = sbl("xn", [128, 4 * D])
        xnT = sbl("xnT", [128, 8 * 512])
        kevac = sbl("kevac", [128, 512])
        vevac = sbl("vevac", [128, 16 * 65])
        stat = sbl("stat", [128, 8])
        sq = sbl("sqs", [128, D])
        ps_tr = ps.enter_context(nc.psum_tensor(f"{L}p_tr", [128, 1024], F32))
        ps_k = ps.enter_context(nc.psum_tensor(f"{L}p_k", [128, 1024], F32))
        ps_v = ps.enter_context(nc.psum_tensor(f"{L}p_v", [128, 1024], F32))

        kb.cur_sv = f"{L}p_vec"
        kb.cur_ssc = f"{L}p_scl"
        sd = sem(f"{L}p_dma")
        sw = sem(f"{L}p_wdma")
        so = sem(f"{L}p_odma")
        sv = sem(f"{L}p_vec")
        st = sem(f"{L}p_ten")
        ssc = sem(f"{L}p_scl")

        with Phase(kb) as em:
            tr_k = [0]
            tr_copy_done = []

            def transpose_tile(em, src_ap, dst_ap, extra_wait=()):
                col = (tr_k[0] % 2) * 512
                w = list(extra_wait)
                if tr_k[0] >= 2:
                    w.append((sv, tr_copy_done[tr_k[0] - 2]))
                em.on("tensor", lambda e, src_ap=src_ap, col=col:
                      e.transpose(ps_tr[:, col:col + 128], src_ap, C.ident[:, :]),
                      wait=w, inc=(st, 1))
                tv = st.n
                em.on("vector", lambda e, dst_ap=dst_ap, col=col:
                      e.tensor_copy(dst_ap, ps_tr[:, col:col + 128]),
                      wait=[(st, tv)], inc=(sv, 1))
                tr_copy_done.append(sv.n)
                tr_k[0] += 1

            if layer == 1:
                em.dma("sync", lambda e: e.dma_start(out=C.bq[:, :], in_=I["l1_bq"].rearrange("(c p) -> p c", p=128),
                                                     allow_slow_non_contiguous=True), sem=sw)
                em.dma("sync", lambda e: e.dma_start(out=C.bk[:, :], in_=I["l1_bk"].rearrange("(c p) -> p c", p=128),
                                                     allow_slow_non_contiguous=True), sem=sw)
                row1 = sbl("row1", [1, D])
                em.dma("sync", lambda e: e.dma_start(out=row1[0:1, :], in_=I["l1_bv"].rearrange("(a d) -> a d", a=1)),
                       sem=sw)
                rowdma = sw.n
                for j in range(0, D, 512):
                    em.on("tensor", lambda e, j=j:
                          e.matmul(kb.ps0[:, 0:512], C.onesrow[0:1, 0:128], row1[0:1, j:j + 512],
                                   start=True, stop=True),
                          wait=[(sw, rowdma), (sv, sv.n)], inc=(st, 1))
                    tv = st.n
                    em.on("vector", lambda e, j=j: e.tensor_copy(C.bv_bc[:, j:j + 512], kb.ps0[:, 0:512]),
                          wait=[(st, tv)], inc=(sv, 1))

            wnames = {0: ("l0_wq", "l0_wk", "l0_wv"), 1: ("l1_wq", "l1_wk", "l1_wv")}[layer]
            for wi, (wt, nm) in enumerate(zip((wq, wk, wv), wnames)):
                for c in range(8):
                    if layer == 0:
                        em.dma("sync", lambda e, wt=wt, nm=nm, c=c:
                               e.dma_start(out=wt[:, c * D:(c + 1) * D], in_=I[nm][c * 128:(c + 1) * 128, :]),
                               sem=sw)
                    else:
                        em.dma("sync", lambda e, nm=nm, c=c:
                               e.dma_start(out=wstage[:, :], in_=I[nm][c * 128:(c + 1) * 128, :]),
                               wait=[(st, st.n)], sem=sw)
                        need = sw.n
                        for j in range(8):
                            transpose_tile(em, wstage[:, j * 128:(j + 1) * 128],
                                           wt[:, j * D + c * 128: j * D + c * 128 + 128],
                                           extra_wait=[(sw, need)] if j == 0 else ())
            wload = sw.n

            src = kb.x1v if layer == 1 else I["xseq"]
            guard_tr = [0]

            for sup in range(8):
                for b in range(4):
                    em.dma("sync", lambda e, sup=sup, b=b:
                           e.dma_start(out=xs[:, b * D:(b + 1) * D],
                                       in_=src[(4 * sup + b) * 128:(4 * sup + b + 1) * 128, :]),
                           wait=[(sv, sv.n), (ssc, ssc.n)] if b == 0 else (), sem=sd)
                ld = sd.n

                for b in range(4):
                    xsb = xs[:, b * D:(b + 1) * D]
                    xnb = xn[:, b * D:(b + 1) * D]
                    m = stat[:, b:b + 1]
                    vvar = stat[:, 4 + b:5 + b]

                    ln_steps(kb, em, xsb, xnb, m, vvar, sq, 1e-5,
                             first_wait=[(sd, ld), (st, guard_tr[0])] if b == 0 else ())
                    if layer == 1:
                        ln_steps(kb, em, xnb, xnb, m, vvar, sq, 1e-6)
                        em.on("vector", lambda e, xnb=xnb: e.tensor_mul(xnb, xnb, C.gam_bc[:, :]), inc=(kb.sem(f"{L}p_vec"), 1))
                        em.on("vector", lambda e, xnb=xnb: e.tensor_add(xnb, xnb, C.bet_bc[:, :]), inc=(kb.sem(f"{L}p_vec"), 1))
                xn_done = sv.n
                if sup == 0 and "XN0" in kb.dbg:
                    em.dma("gpsimd", lambda e: e.dma_start(out=kb.dbg["XN0"], in_=xn[:, :]),
                           wait=[(sv, xn_done)], sem=so)
                    em.on("vector", None, wait=[(so, so.n)])

                for b in range(4):
                    for c in range(8):
                        w = []
                        if b == 0 and c == 0:
                            w.append((sv, xn_done))
                            if layer == 0 and sup == 0:
                                w.append((sw, wload))
                        transpose_tile(em, xn[:, b * D + c * 128: b * D + c * 128 + 128],
                                       xnT[:, c * 512 + b * 128: c * 512 + b * 128 + 128],
                                       extra_wait=w)
                tr_done = sv.n
                guard_tr[0] = st.n

                for wt_, bias_, isq in ((wk, C.bk, False), (wq, C.bq, True)):
                    for kt in range(8):
                        pb = kt % 2
                        for dc in range(8):
                            w = []
                            if dc == 0:
                                w.append((sv, sv.n))
                                if kt == 0 and not isq:
                                    w.append((sw, wload))
                            em.on("tensor", lambda e, wt_=wt_, kt=kt, dc=dc, pb=pb:
                                  e.matmul(ps_k[:, pb * 512:pb * 512 + 512],
                                           wt_[:, dc * D + kt * 128: dc * D + kt * 128 + 128],
                                           xnT[:, dc * 512:(dc + 1) * 512],
                                           start=(dc == 0), stop=(dc == 7)),
                                  wait=w, inc=(st, 1) if dc == 7 else None)
                        tv = st.n
                        em.on("vector", lambda e, bias_=bias_, kt=kt, pb=pb:
                              e.tensor_scalar_add(kevac[:, :], ps_k[:, pb * 512:pb * 512 + 512], bias_[:, kt:kt + 1]),
                              wait=[(st, tv)], inc=(sv, 1))
                        ev = sv.n
                        if not isq:
                            em.dma("gpsimd", lambda e, sup=sup, kt=kt:
                                   e.dma_start(out=kb.dKT[sup, kt * 128:(kt + 1) * 128, :], in_=kevac[:, :]),
                                   wait=[(sv, ev)], sem=so)
                        else:
                            for b in range(4):
                                em.dma("gpsimd", lambda e, sup=sup, kt=kt, b=b:
                                       e.dma_start(out=kb.dQT[4 * sup + b, kt * 128:(kt + 1) * 128, :],
                                                   in_=kevac[:, b * 128:(b + 1) * 128]),
                                       wait=[(sv, ev)] if b == 0 else (), sem=so)
                        em.on("vector", None, wait=[(so, so.n)])

                vev3 = vevac[:, :].rearrange("p (h j) -> p h j", j=65)
                for s4 in range(4):
                    for dw in range(2):
                        pb = dw
                        for dc in range(8):
                            w = [(sv, sv.n)] if dc == 0 else []
                            em.on("tensor", lambda e, s4=s4, dw=dw, dc=dc, pb=pb:
                                  e.matmul(ps_v[:, pb * 512:pb * 512 + 512],
                                           xnT[:, dc * 512 + s4 * 128: dc * 512 + s4 * 128 + 128],
                                           wv[:, dc * D + dw * 512: dc * D + dw * 512 + 512],
                                           start=(dc == 0), stop=(dc == 7)),
                                  wait=w, inc=(st, 1) if dc == 7 else None)
                        tv = st.n
                        em.on("vector", lambda e, dw=dw, pb=pb:
                              e.tensor_tensor(
                                  vev3[:, dw * 8:(dw + 1) * 8, 0:64],
                                  ps_v[:, pb * 512:pb * 512 + 512],
                                  C.bv_bc[:, dw * 512:(dw + 1) * 512], ALU.add),
                              wait=[(st, tv)], inc=(sv, 1))
                    em.on("vector", lambda e: e.memset(vev3[:, :, 64:65], 1.0),
                          inc=(sv, 1))
                    ev = sv.n
                    em.dma("gpsimd", lambda e, sup=sup, s4=s4:
                           e.dma_start(out=kb.dV[sup, s4 * 128:(s4 + 1) * 128, :], in_=vevac[:, :]),
                           wait=[(sv, ev)], sem=so)
                    em.on("vector", None, wait=[(so, so.n)])

            em.on("gpsimd", None, wait=[(so, so.n)])
            em.on("sync", None, wait=[(sd, sd.n), (sw, sw.n)])


def build_attn(kb, layer):
    nc, C, I, sem = kb.nc, kb.C, kb.I, kb.sem
    L = f"l{layer}"
    with ExitStack() as ps:
        sbl = lambda n, s, d=F32: kb.sb(f"{L}a_" + n, s, d, stack=ps)
        qwin = sbl("qwin", [128, 4096])
        ksup = [sbl(f"ksup{i}", [128, 4096]) for i in range(2)]
        vsup = [sbl(f"vsup{i}", [128, 4160]) for i in range(2)]
        num = [sbl(f"num{i}", [128, 1024]) for i in range(2)]
        sig = [sbl(f"sig{i}", [128, 1024]) for i in range(2)] if layer == 1 else None
        U_sb = sbl("U", [128, 8192])
        aevac = sbl("aevac", [128, 512])
        wo = sbl("wo", [128, 8192])
        wstage = sbl("wstage", [128, D]) if layer == 1 else None
        ps_sc = ps.enter_context(nc.psum_tensor(f"{L}a_sc", [128, 2048], F32))
        ps_u = ps.enter_context(nc.psum_tensor(f"{L}a_u", [128, 1024], F32))

        sd = sem(f"{L}a_dma")
        sqd = sem(f"{L}a_qdma")
        so = sem(f"{L}a_odma")
        sv = sem(f"{L}a_vec")
        st = sem(f"{L}a_ten")
        ssc = sem(f"{L}a_scl")
        kb.cur_sv = f"{L}a_vec"
        kb.cur_ssc = f"{L}a_scl"

        with Phase(kb) as em:
            tr_k = [0]
            tr_done = []

            def transpose_tile(src_ap, dst_ap, extra_wait=()):
                col = (tr_k[0] % 2) * 512
                w = list(extra_wait)
                if tr_k[0] >= 2:
                    w.append((sv, tr_done[tr_k[0] - 2]))
                em.on("tensor", lambda e, src_ap=src_ap, col=col:
                      e.transpose(ps_sc[:, col:col + 128], src_ap, C.ident[:, :]),
                      wait=w, inc=(st, 1))
                tv = st.n
                em.on("vector", lambda e, dst_ap=dst_ap, col=col:
                      e.tensor_copy(dst_ap, ps_sc[:, col:col + 128]),
                      wait=[(st, tv)], inc=(sv, 1))
                tr_done.append(sv.n)
                tr_k[0] += 1

            wnm = "l0_wo" if layer == 0 else "l1_wo"
            if layer == 1:
                row1 = kb.sb("l1a_row1", [1, D], F32, stack=ps)
                em.dma("sync", lambda e: e.dma_start(out=row1[0:1, :], in_=I["l1_bo"].rearrange("(a d) -> a d", a=1)),
                       sem=sd)
                rowdma = sd.n
                for j in range(0, D, 512):
                    em.on("tensor", lambda e, j=j:
                          e.matmul(kb.ps0[:, 0:512], C.onesrow[0:1, 0:128], row1[0:1, j:j + 512],
                                   start=True, stop=True),
                          wait=[(sd, rowdma), (sv, sv.n)], inc=(st, 1))
                    tv = st.n
                    em.on("vector", lambda e, j=j: e.tensor_copy(C.bo_bc[:, j:j + 512], kb.ps0[:, 0:512]),
                          wait=[(st, tv)], inc=(sv, 1))
            for c in range(8):
                if layer == 0:
                    em.dma("sync", lambda e, c=c:
                           e.dma_start(out=wo[:, c * D:(c + 1) * D], in_=I[wnm][c * 128:(c + 1) * 128, :]),
                           sem=sd)
                else:
                    em.dma("sync", lambda e, c=c:
                           e.dma_start(out=wstage[:, :], in_=I[wnm][c * 128:(c + 1) * 128, :]),
                           wait=[(st, st.n)], sem=sd)
                    need = sd.n
                    for j in range(8):
                        transpose_tile(wstage[:, j * 128:(j + 1) * 128],
                                       wo[:, j * D + c * 128: j * D + c * 128 + 128],
                                       extra_wait=[(sd, need)] if j == 0 else ())
            wo_dma = sd.n

            def qbulk(e):
                par = kb.rtv(e, "gpsimd", "par")
                src = kb.dQT.rearrange("(i b) d q -> i b d q", b=2)[:, bass.ds(par, 1), :, :]
                return e.dma_start(out=kb.dQTown[:, :, :].rearrange("i (a d) q -> i a d q", a=1), in_=src)
            em.dma("gpsimd", qbulk, wait=[(so, so.n)], sem=sqd)
            qbulk_done = sqd.n

            nsup_of = (lambda w: 2 * w + 2) if layer == 0 else (lambda w: 8)

            ksup_free = [0, 0]
            vsup_free = [0, 0]
            num_free = [0, 0]
            num_written = [0, 0]
            qwin_free = [0]

            for qw in range(4):
                for jq in range(4):
                    def qload(e, qw=qw, jq=jq):
                        src = kb.dQTown[4 * qw + jq, :, :].rearrange("(c p) q -> p c q", p=128)
                        dst = qwin[:, :].rearrange("p (c q) -> p c q", c=8)[:, :, 128 * jq:128 * jq + 128]
                        return e.dma_start(out=dst, in_=src)
                    em.dma("gpsimd", qload,
                           wait=([(st, qwin_free[0])] if jq == 0 else []) + ([(sqd, qbulk_done)] if qw == 0 and jq == 0 else []),
                           sem=sqd)
                qwin_dma = sqd.n

                nsup = nsup_of(qw)
                for sup in range(nsup):
                    buf = sup % 2
                    em.dma("sync", lambda e, sup=sup, buf=buf:
                           e.dma_start(out=ksup[buf][:, :].rearrange("p (c t) -> p c t", c=8),
                                       in_=kb.dKT[sup, :, :].rearrange("(c p) t -> p c t", p=128)),
                           wait=[(st, ksup_free[buf])], sem=sd)
                    em.dma("sync", lambda e, sup=sup, buf=buf:
                           e.dma_start(out=vsup[buf][:, :].rearrange("p (s v) -> p s v", s=4),
                                       in_=kb.dV[sup, :, :].rearrange("(s p) v -> p s v", p=128)),
                           wait=[(st, vsup_free[buf])], sem=sd)
                    kv_dma = sd.n

                    for hp in range(8):
                        hA, hB = 2 * hp, 2 * hp + 1

                        def scores(s, em=em, qw=qw, sup=sup, hp=hp, buf=buf):
                            slot = s % 2
                            w = []
                            if hp == 0 and s <= 1:
                                w.append((sd, kv_dma))
                                if sup == 0 and s == 0:
                                    w.append((sqd, qwin_dma))
                            if num_written[slot]:
                                w.append((ssc if layer == 0 else sv, num_written[slot]))
                            for hh in range(2):
                                h = 2 * hp + hh
                                rb = (h % 2) * 64
                                ct = h // 2
                                em.on("tensor", lambda e, h=h, rb=rb, ct=ct, s=s, slot=slot, buf=buf, hh=hh:
                                      e.matmul(ps_sc[:, slot * 1024 + hh * 512: slot * 1024 + hh * 512 + 512],
                                               ksup[buf][rb:rb + 64, ct * 512 + 128 * s: ct * 512 + 128 * s + 128],
                                               qwin[rb:rb + 64, ct * 512:(ct + 1) * 512],
                                               start=True, stop=True),
                                      wait=w if hh == 0 else (), inc=(st, 1))
                            return st.n

                        def numf(s, em=em, qw=qw, sup=sup, hp=hp, sc_done=None):
                            slot = s % 2
                            kc = 4 * sup + s
                            if layer == 0:
                                em.on("scalar", lambda e, slot=slot:
                                      e.activation(num[slot][:, :], ps_sc[:, slot * 1024:slot * 1024 + 1024],
                                                   AF.Exp, scale=C.fc[:, 2:3]),
                                      wait=[(st, sc_done), (st, num_free[slot])], inc=(ssc, 1))
                                num_written[slot] = ssc.n
                                done_sem, done_val = ssc, ssc.n
                                mops = []
                                for jj in range(4):
                                    Ga = 8 * qw + 2 * jj
                                    if kc == Ga:
                                        mops.append((jj, C.maskA))
                                    elif kc == Ga + 1:
                                        mops.append((jj, C.maskB))
                                    elif kc > Ga + 1:
                                        mops.append((jj, None))
                                for jj, msk in mops:
                                    for hh in range(2):
                                        cl = hh * 512 + 128 * jj
                                        if msk is None:
                                            em.on("vector", lambda e, slot=slot, cl=cl:
                                                  e.memset(num[slot][:, cl:cl + 128], 0.0),
                                                  wait=[(ssc, num_written[slot])], inc=(sv, 1))
                                        else:
                                            em.on("vector", lambda e, slot=slot, cl=cl, msk=msk:
                                                  e.tensor_mul(num[slot][:, cl:cl + 128], num[slot][:, cl:cl + 128], msk[:, :]),
                                                  wait=[(ssc, num_written[slot])], inc=(sv, 1))
                                if mops:
                                    done_sem, done_val = sv, sv.n
                                return (done_sem, done_val)
                            else:
                                em.on("scalar", lambda e, slot=slot:
                                      e.activation(sig[slot][:, :], ps_sc[:, slot * 1024:slot * 1024 + 1024],
                                                   AF.Sigmoid, scale=C.fc[:, 2:3], bias=C.negmu[:, 0:1]),
                                      wait=[(st, sc_done), (sv, num_written[slot]), (st, num_free[slot])],
                                      inc=(ssc, 1))
                                em.on("scalar", lambda e, slot=slot:
                                      e.activation(num[slot][:, :], ps_sc[:, slot * 1024:slot * 1024 + 1024],
                                                   AF.Exp, scale=C.fc[:, 2:3]),
                                      inc=(ssc, 1))
                                sc2 = ssc.n
                                em.on("vector", lambda e, slot=slot:
                                      e.tensor_mul(num[slot][:, :], num[slot][:, :], sig[slot][:, :]),
                                      wait=[(ssc, sc2)], inc=(sv, 1))
                                num_written[slot] = sv.n
                                return (sv, sv.n)

                        def av(s, ndone, em=em, sup=sup, hp=hp, buf=buf, qw=qw):
                            slot = s % 2
                            w = [(ndone[0], ndone[1])]
                            if s == 0:
                                w.append((sv, u_evac_done[0]))
                            for hh in range(2):
                                h = 2 * hp + hh
                                em.on("tensor", lambda e, h=h, hh=hh, s=s, slot=slot, buf=buf:
                                      e.matmul(ps_u[0:65, hh * 512:(hh + 1) * 512],
                                               vsup[buf][:, s * 1040 + 65 * h: s * 1040 + 65 * h + 65],
                                               num[slot][:, hh * 512:(hh + 1) * 512],
                                               start=(s == 0), stop=(s == 3)),
                                      wait=w if hh == 0 else (), inc=(st, 1))
                            num_free[slot] = st.n
                            return st.n

                        u_evac_done = getattr(kb, "_ued_" + L, None)
                        if u_evac_done is None:
                            u_evac_done = [0]
                            setattr(kb, "_ued_" + L, u_evac_done)

                        scd = {}
                        nd = {}
                        scd[0] = scores(0)
                        nd[0] = numf(0, sc_done=scd[0])
                        scd[1] = scores(1)
                        av(0, nd[0])
                        nd[1] = numf(1, sc_done=scd[1])
                        scd[2] = scores(2)
                        av(1, nd[1])
                        nd[2] = numf(2, sc_done=scd[2])
                        scd[3] = scores(3)
                        av(2, nd[2])
                        nd[3] = numf(3, sc_done=scd[3])
                        av_done = av(3, nd[3])
                        for hh in range(2):
                            h = 2 * hp + hh
                            if sup == 0:
                                em.on("vector", lambda e, h=h, hh=hh:
                                      e.tensor_copy(U_sb[0:65, 512 * h:512 * h + 512], ps_u[0:65, hh * 512:hh * 512 + 512]),
                                      wait=[(st, av_done)], inc=(sv, 1))
                            else:
                                em.on("vector", lambda e, h=h, hh=hh:
                                      e.tensor_tensor(U_sb[0:65, 512 * h:512 * h + 512],
                                                      U_sb[0:65, 512 * h:512 * h + 512],
                                                      ps_u[0:65, hh * 512:hh * 512 + 512], ALU.add),
                                      wait=[(st, av_done)], inc=(sv, 1))
                        u_evac_done[0] = sv.n
                    ksup_free[buf] = st.n
                    vsup_free[buf] = st.n
                qwin_free[0] = st.n

                uev = u_evac_done[0]
                if qw == 0 and layer == 0 and "USB0" in kb.dbg:
                    em.dma("gpsimd", lambda e: e.dma_start(out=kb.dbg["USB0"], in_=U_sb[:, :]),
                           wait=[(sv, uev)], sem=so)
                    em.on("vector", None, wait=[(so, so.n)])
                em.on("vector", lambda e: e.reciprocal(U_sb[64:65, :], U_sb[64:65, :]),
                      wait=[(sv, uev)], inc=(sv, 1))
                for c in range(16):
                    em.on("tensor", lambda e, c=c:
                          e.matmul(ps_sc[0:64, 0:512], C.onesrow[64:65, 0:64],
                                   U_sb[64:65, c * 512:(c + 1) * 512],
                                   start=True, stop=True),
                          wait=[(sv, sv.n)], inc=(st, 1))
                    tv = st.n
                    em.on("vector", lambda e, c=c:
                          e.tensor_tensor(U_sb[0:64, c * 512:(c + 1) * 512],
                                          U_sb[0:64, c * 512:(c + 1) * 512],
                                          ps_sc[0:64, 0:512], ALU.mult),
                          wait=[(st, tv)], inc=(sv, 1))
                un_done = sv.n
                if qw == 0 and layer == 0 and "UN0" in kb.dbg:
                    em.dma("gpsimd", lambda e: e.dma_start(out=kb.dbg["UN0"], in_=U_sb[:, :]),
                           wait=[(sv, un_done)], sem=so)
                    em.on("vector", None, wait=[(so, so.n)])
                for hp in range(8):
                    em.on("tensor", lambda e, hp=hp:
                          e.matmul(ps_sc[:, 1024:1536], C.ident[0:64, 0:128],
                                   U_sb[0:64, 512 * (2 * hp):512 * (2 * hp) + 512],
                                   start=True, stop=False),
                          wait=[(sv, un_done)] if hp == 0 else [(sv, sv.n)], inc=(st, 1))
                    em.on("tensor", lambda e, hp=hp:
                          e.matmul(ps_sc[:, 1024:1536], C.identB[0:64, 0:128],
                                   U_sb[0:64, 512 * (2 * hp + 1):512 * (2 * hp + 1) + 512],
                                   start=False, stop=True),
                          inc=(st, 1))
                    tv = st.n
                    em.on("vector", lambda e, hp=hp:
                          e.tensor_copy(U_sb[:, 1024 * hp:1024 * hp + 512], ps_sc[:, 1024:1536]),
                          wait=[(st, tv)], inc=(sv, 1))
                pk_done = sv.n
                if qw == 0 and layer == 0 and "UPK0" in kb.dbg:
                    em.dma("gpsimd", lambda e: e.dma_start(out=kb.dbg["UPK0"], in_=U_sb[:, :]),
                           wait=[(sv, pk_done)], sem=so)
                    em.on("vector", None, wait=[(so, so.n)])
                for jq in range(4):
                    for dw in range(2):
                        pb = dw
                        for hp in range(8):
                            w = []
                            if hp == 0:
                                w = [(sv, pk_done), (sv, sv.n)]
                                if jq == 0 and dw == 0:
                                    w.append((sd, wo_dma))
                            em.on("tensor", lambda e, jq=jq, dw=dw, hp=hp, pb=pb:
                                  e.matmul(ps_u[:, pb * 512:pb * 512 + 512],
                                           U_sb[:, 1024 * hp + 128 * jq: 1024 * hp + 128 * jq + 128],
                                           wo[:, 1024 * hp + 512 * dw: 1024 * hp + 512 * dw + 512],
                                           start=(hp == 0), stop=(hp == 7)),
                                  wait=w, inc=(st, 1) if hp == 7 else None)
                        tv = st.n
                        em.on("vector", lambda e, dw=dw, pb=pb:
                              e.tensor_tensor(aevac[:, :], ps_u[:, pb * 512:pb * 512 + 512],
                                              C.bo_bc[:, dw * 512:(dw + 1) * 512], ALU.add),
                              wait=[(st, tv)], inc=(sv, 1))
                        ev = sv.n
                        em.dma("gpsimd", lambda e, qw=qw, jq=jq, dw=dw:
                               e.dma_start(out=kb.dA[4 * qw + jq, :, 512 * dw:512 * dw + 512], in_=aevac[:, :]),
                               wait=[(sv, ev)], sem=so)
                        em.on("vector", None, wait=[(so, so.n)])
                u_evac_done[0] = sv.n
                ksup_free[0] = st.n
                ksup_free[1] = st.n
                vsup_free[0] = st.n
                vsup_free[1] = st.n
                num_written[0] = 0
                num_written[1] = 0

            em.on("gpsimd", None, wait=[(so, so.n), (sqd, sqd.n)])
            em.on("sync", None, wait=[(sd, sd.n)])


def build_moe(kb):
    nc, C, I, sem = kb.nc, kb.C, kb.I, kb.sem
    with ExitStack() as ps:
        sbl = lambda n, s, d=F32: kb.sb("m_" + n, s, d, stack=ps)
        zhiT = ps.enter_context(nc.sbuf_tensor("m_zhiT", [128, 8 * 2048], BF16))
        zloT = ps.enter_context(nc.sbuf_tensor("m_zloT", [128, 8 * 2048], BF16))
        zsc = sbl("zsc", [128, 128])
        wsp = sbl("wsp", [128, 2048])
        wsc = sbl("wsc", [128, 2048])
        whb = ps.enter_context(nc.sbuf_tensor("m_whb", [128, 2048], BF16))
        wlb = ps.enter_context(nc.sbuf_tensor("m_wlb", [128, 2048], BF16))
        rwh = ps.enter_context(nc.sbuf_tensor("m_rwh", [128, 64], BF16))
        rwl = ps.enter_context(nc.sbuf_tensor("m_rwl", [128, 64], BF16))
        rsc = sbl("rsc", [128, 64])
        xo = sbl("xo", [128, D])
        xn = sbl("xn", [128, D])
        zb = sbl("zb", [128, D])
        sq = sbl("sq", [128, D])
        stat = sbl("stat", [128, 8])
        asb = sbl("asb", [128, D])
        lsb = sbl("lsb", [128, 8])
        tp2 = sbl("tp2", [128, 8])
        w_sb = sbl("w_sb", [128, 8 * 16])
        wrow = sbl("wrow", [1, 256])
        b2row = sbl("b2row", [1, 1024])
        wbc = sbl("wbc", [128, 256])
        rw_sb = sbl("rw", [128, 64])
        w1hb = [ps.enter_context(nc.sbuf_tensor(f"m_w1h{i}", [128, 8 * 512], BF16)) for i in range(2)]
        w1lb = [ps.enter_context(nc.sbuf_tensor(f"m_w1l{i}", [128, 8 * 512], BF16)) for i in range(2)]
        w2b = [sbl(f"w2b{i}", [128, 1024]) for i in range(2)]
        gbuf = sbl("gbuf", [128, 512])
        x1b = sbl("x1b", [128, 512])
        ps_h = ps.enter_context(nc.psum_tensor("m_h", [128, 512], F32))
        ps_h2 = ps.enter_context(nc.psum_tensor("m_h2", [128, 512], F32))
        ps_x1 = ps.enter_context(nc.psum_tensor("m_x1", [128, 2048], F32))

        sd = sem("m_dma")
        sg = sem("m_gdma")
        so = sem("m_odma")
        sv = sem("m_vec")
        st = sem("m_ten")
        ssc = sem("m_scl")
        swd = sem("m_w2d")
        kb.cur_sv = "m_vec"
        kb.cur_ssc = "m_scl"

        with Phase(kb) as em:
            gb_free = [0]

            def xbulk(e):
                p128 = kb.rtv(e, "gpsimd", "par128")
                src = I["xseq"].rearrange("(i r) d -> i r d", r=256)[:, bass.ds(p128, 128), :]
                return e.dma_start(out=kb.xown[:, :, :], in_=src)
            em.dma("gpsimd", xbulk, sem=sg)
            xbulk_done = sg.n

            for dc in range(8):
                em.dma("sync", lambda e, dc=dc: e.dma_start(out=rw_sb[:, dc * 8:(dc + 1) * 8],
                                                            in_=I["rW"][dc * 128:(dc + 1) * 128, :]),
                       sem=sd)
            rwd = sd.n
            em.on("vector", lambda e: e.tensor_copy(rwh[:, :], rw_sb[:, :]), wait=[(sd, rwd)], inc=(sv, 1))
            em.on("vector", lambda e: e.tensor_copy(rsc[:, :], rwh[:, :]), inc=(sv, 1))
            em.on("vector", lambda e: e.tensor_sub(rsc[:, :], rw_sb[:, :], rsc[:, :]), inc=(sv, 1))
            em.on("vector", lambda e: e.tensor_copy(rwl[:, :], rsc[:, :]), inc=(sv, 1))

            w1s_ld = [0]
            w1s_st = [0]
            w1split_done = [0] * 8

            def w1split_tile(t):
                e_, rc = t // 8, t % 8
                em.dma("sync", lambda e, e_=e_, rc=rc:
                       e.dma_start(out=wsp[:, :], in_=I["W1"][e_, rc * 128:(rc + 1) * 128, :]),
                       wait=[(sv, sv.n), (so, w1s_st[0])], sem=sd)
                ld = sd.n
                em.on("vector", lambda e: e.tensor_copy(whb[:, :], wsp[:, :]), wait=[(sd, ld)], inc=(sv, 1))
                em.on("vector", lambda e: e.tensor_copy(wsc[:, :], whb[:, :]), inc=(sv, 1))
                em.on("vector", lambda e: e.tensor_sub(wsc[:, :], wsp[:, :], wsc[:, :]), inc=(sv, 1))
                em.on("vector", lambda e: e.tensor_copy(wlb[:, :], wsc[:, :]), inc=(sv, 1))
                qv = sv.n
                em.dma("gpsimd", lambda e, e_=e_, rc=rc:
                       e.dma_start(out=kb.dW1h[e_, rc * 128:(rc + 1) * 128, :], in_=whb[:, :]),
                       wait=[(sv, qv)], sem=so)
                em.dma("gpsimd", lambda e, e_=e_, rc=rc:
                       e.dma_start(out=kb.dW1l[e_, rc * 128:(rc + 1) * 128, :], in_=wlb[:, :]),
                       sem=so)
                w1s_st[0] = so.n
                if rc == 7:
                    w1split_done[e_] = so.n

            tr_k = [0]
            tr_copy_done = []

            def transpose_tile(src_ap, hi_ap, lo_ap, extra_wait=()):
                col = 0
                w = list(extra_wait)
                if tr_k[0] >= 1:
                    w.append((sv, tr_copy_done[tr_k[0] - 1]))
                em.on("tensor", lambda e, src_ap=src_ap:
                      e.transpose(ps_h[:, 0:128], src_ap, C.ident[:, :]),
                      wait=w, inc=(st, 1))
                tv = st.n
                em.on("vector", lambda e, hi_ap=hi_ap:
                      e.tensor_copy(hi_ap, ps_h[:, 0:128]),
                      wait=[(st, tv)], inc=(sv, 1))
                em.on("vector", lambda e, hi_ap=hi_ap: e.tensor_copy(zsc[:, :], hi_ap), inc=(sv, 1))
                em.on("vector", lambda e: e.tensor_sub(zsc[:, :], ps_h[:, 0:128], zsc[:, :]), inc=(sv, 1))
                em.on("vector", lambda e, lo_ap=lo_ap: e.tensor_copy(lo_ap, zsc[:, :]), inc=(sv, 1))
                tr_copy_done.append(sv.n)
                tr_k[0] += 1

            for i in range(OB):
                em.dma("gpsimd", lambda e, i=i: e.dma_start(out=xo[:, :], in_=kb.xown[i, :, :]),
                       wait=[(sv, sv.n), (ssc, ssc.n)] + ([(sg, xbulk_done)] if i == 0 else []), sem=sg)
                em.dma("sync", lambda e, i=i: e.dma_start(out=asb[:, :], in_=kb.dA[i, :, :]),
                       wait=[(sv, sv.n)], sem=sd)
                ld = sg.n
                ld2 = sd.n
                ln_steps(kb, em, xo[:, :], xn[:, :], stat[:, 0:1], stat[:, 1:2], sq, 1e-5,
                         first_wait=[(sg, ld)])
                em.on("vector", lambda e: e.tensor_add(asb[:, :], asb[:, :], xo[:, :]),
                      wait=[(sd, ld2)], inc=(sv, 1))
                em.on("vector", lambda e: e.tensor_add(asb[:, :], asb[:, :], xn[:, :]), inc=(sv, 1))
                asd = sv.n
                em.dma("gpsimd", lambda e, i=i: e.dma_start(out=kb.dAS[i, :, :], in_=asb[:, :]),
                       wait=[(sv, asd)], sem=so)
                ln_steps(kb, em, xn[:, :], zb[:, :], stat[:, 2:3], stat[:, 3:4], sq, 1e-5)
                em.on("vector", lambda e: e.tensor_mul(zb[:, :], zb[:, :], C.lng_bc[:, :]), inc=(sv, 1))
                em.on("vector", lambda e: e.tensor_add(zb[:, :], zb[:, :], C.lnb_bc[:, :]), inc=(sv, 1))
                z_done = sv.n
                for dc in range(8):
                    transpose_tile(zb[:, dc * 128:(dc + 1) * 128],
                                   zhiT[:, dc * 2048 + 128 * i: dc * 2048 + 128 * i + 128],
                                   zloT[:, dc * 2048 + 128 * i: dc * 2048 + 128 * i + 128],
                                   extra_wait=[(sv, z_done)] if dc == 0 else ())
                for j_ in range(4):
                    w1split_tile(4 * i + j_)
                ztr = sv.n
                for pas in range(3):
                    zz = (zhiT, zhiT, zloT)[pas]
                    rr = (rwh, rwl, rwh)[pas]
                    for dc in range(8):
                        em.on("tensor", lambda e, dc=dc, i=i, zz=zz, rr=rr, pas=pas:
                              e.matmul(ps_h2[:, 0:8],
                                       zz[:, dc * 2048 + 128 * i: dc * 2048 + 128 * i + 128],
                                       rr[:, dc * 8:(dc + 1) * 8],
                                       start=(pas == 0 and dc == 0), stop=(pas == 2 and dc == 7)),
                              wait=[(sv, ztr), (sv, sv.n)] if (pas == 0 and dc == 0) else (),
                              inc=(st, 1) if (pas == 2 and dc == 7) else None)
                tv = st.n
                em.on("vector", lambda e: e.tensor_tensor(lsb[:, :], ps_h2[:, 0:8], C.rb_bc[:, :], ALU.add),
                      wait=[(st, tv)], inc=(sv, 1))
                em.on("vector", lambda e: e.reduce_max(stat[:, 4:5], lsb[:, :], AX.X, negate=True), inc=(sv, 1))
                pv = sv.n
                em.on("scalar", lambda e: e.activation(lsb[:, :], lsb[:, :], AF.Exp, bias=stat[:, 4:5]),
                      wait=[(sv, pv)], inc=(ssc, 1))
                sc0 = ssc.n
                em.on("vector", lambda e: e.reduce_sum(stat[:, 5:6], lsb[:, :], AX.X),
                      wait=[(ssc, sc0)], inc=(sv, 1))
                em.on("vector", lambda e: e.reciprocal(stat[:, 5:6], stat[:, 5:6]), inc=(sv, 1))
                em.on("vector", lambda e: e.tensor_scalar_mul(lsb[:, :], lsb[:, :], stat[:, 5:6]), inc=(sv, 1))
                em.on("vector", lambda e: e.reduce_max(stat[:, 4:5], lsb[:, :], AX.X), inc=(sv, 1))
                em.on("vector", lambda e: e.tensor_single_scalar(tp2[:, :], lsb[:, :], stat[:, 4:5], ALU.is_ge), inc=(sv, 1))
                em.on("vector", lambda e: e.reduce_sum(stat[:, 6:7], tp2[:, :], AX.X), inc=(sv, 1))
                em.on("vector", lambda e: e.scalar_tensor_tensor(tp2[:, :], tp2[:, :], -1e9, lsb[:, :], ALU.mult, ALU.add), inc=(sv, 1))
                em.on("vector", lambda e: e.reduce_max(stat[:, 5:6], tp2[:, :], AX.X), inc=(sv, 1))
                em.on("vector", lambda e: e.tensor_single_scalar(stat[:, 6:7], stat[:, 6:7], 1.5, ALU.is_ge), inc=(sv, 1))
                em.on("vector", lambda e: e.tensor_tensor(stat[:, 7:8], stat[:, 4:5], stat[:, 5:6], ALU.subtract), inc=(sv, 1))
                em.on("vector", lambda e: e.tensor_tensor(stat[:, 6:7], stat[:, 6:7], stat[:, 7:8], ALU.mult), inc=(sv, 1))
                em.on("vector", lambda e: e.tensor_tensor(stat[:, 6:7], stat[:, 6:7], stat[:, 5:6], ALU.add), inc=(sv, 1))
                em.on("vector", lambda e: e.tensor_single_scalar(tp2[:, :], lsb[:, :], stat[:, 6:7], ALU.is_ge), inc=(sv, 1))
                em.on("vector", lambda e: e.tensor_mul(tp2[:, :], tp2[:, :], lsb[:, :]), inc=(sv, 1))
                em.on("vector", lambda e: e.reduce_sum(stat[:, 7:8], tp2[:, :], AX.X), inc=(sv, 1))
                em.on("vector", lambda e: e.tensor_scalar_max(stat[:, 7:8], stat[:, 7:8], 1e-8), inc=(sv, 1))
                em.on("vector", lambda e: e.reciprocal(stat[:, 7:8], stat[:, 7:8]), inc=(sv, 1))
                em.on("vector", lambda e, i=i: e.tensor_scalar_mul(w_sb[:, 8 * i:8 * i + 8], tp2[:, :], stat[:, 7:8]), inc=(sv, 1))
                wv_done = sv.n

            if "W8" in kb.dbg:
                em.dma("gpsimd", lambda e: e.dma_start(out=kb.dbg["W8"], in_=w_sb[:, :]),
                       wait=[(sv, sv.n)], sem=so)
            w1_free = [0, 0]
            b2_free = [0]
            w2_free = [0, 0]
            gb_free2 = [0, 0]
            psh_free = [0, 0]
            pend = []

            def emit_mm2():
                if not pend:
                    return
                ft, b2_, gw_done, w2_dma, first = pend.pop()
                gslot = ft % 2
                for ts in range(2):
                    for dw in range(2):
                        em.on("tensor", lambda e, ts=ts, dw=dw, b2_=b2_, first=first, gslot=gslot:
                              e.matmul(ps_x1[:, (2 * ts + dw) * 512:(2 * ts + dw) * 512 + 512],
                                       gbuf[:, 256 * gslot + 128 * ts: 256 * gslot + 128 * ts + 128],
                                       w2b[b2_][:, 512 * dw:512 * dw + 512],
                                       start=first, stop=False),
                              wait=[(sv, gw_done), (swd, w2_dma)] if (ts == 0 and dw == 0) else (),
                              inc=(st, 1))
                w2_free[b2_] = st.n
                gb_free2[ft % 2] = st.n

            for twin in range(8):
                for e_ in range(8):
                    for ts in range(2):
                        i_b = 2 * twin + ts
                        em.on("tensor", lambda e, e_=e_, i_b=i_b:
                              e.transpose(ps_h2[0:1, 256:384], w_sb[:, 8 * i_b + e_:8 * i_b + e_ + 1], C.ident[:, :]),
                              wait=[(sv, sv.n)], inc=(st, 1))
                        tv = st.n
                        em.on("vector", lambda e, ts=ts: e.tensor_copy(wrow[0:1, 128 * ts:128 * ts + 128], ps_h2[0:1, 256:384]),
                              wait=[(st, tv)], inc=(sv, 1))
                    wrow_done = sv.n
                    em.on("tensor", lambda e:
                          e.matmul(kb.ps0[:, 0:256], C.onesrow[0:1, 0:128], wrow[0:1, 0:256],
                                   start=True, stop=True),
                          wait=[(sv, wrow_done)], inc=(st, 1))
                    tv = st.n
                    em.on("vector", lambda e: e.tensor_copy(wbc[:, :], kb.ps0[:, 0:256]),
                          wait=[(st, tv)], inc=(sv, 1))
                    wbc_done = sv.n
                    for fw in range(4):
                        b1_ = (e_ * 4 + fw) % 2
                        em.dma("sync", lambda e, e_=e_, fw=fw, b1_=b1_:
                               e.dma_start(out=w1hb[b1_][:, :].rearrange("p (c f) -> p c f", c=8),
                                           in_=kb.dW1h[e_, :, 512 * fw:512 * fw + 512].rearrange("(c p) f -> p c f", p=128)),
                               wait=[(st, w1_free[b1_]), (so, w1split_done[e_])], sem=sd)
                        em.dma("sync", lambda e, e_=e_, fw=fw, b1_=b1_:
                               e.dma_start(out=w1lb[b1_][:, :].rearrange("p (c f) -> p c f", c=8),
                                           in_=kb.dW1l[e_, :, 512 * fw:512 * fw + 512].rearrange("(c p) f -> p c f", p=128)),
                               sem=sd)
                        w1_dma = sd.n
                        for fi in range(4):
                            ft = 4 * fw + fi
                            b2_ = ft % 2
                            em.dma("gpsimd", lambda e, e_=e_, ft=ft, b2_=b2_:
                                   e.dma_start(out=w2b[b2_][:, :], in_=I["W2"][e_, 128 * ft:128 * ft + 128, :]),
                                   wait=[(st, w2_free[b2_])], sem=swd)
                            w2_dma = swd.n
                            psh = ps_h if (ft % 2 == 0) else ps_h2
                            for pas in range(3):
                                ww = (w1hb, w1hb, w1lb)[pas][b1_]
                                zz = (zhiT, zloT, zhiT)[pas]
                                for dc in range(8):
                                    w = []
                                    if pas == 0 and dc == 0:
                                        w = [(sd, w1_dma), (ssc, psh_free[ft % 2])]
                                    em.on("tensor", lambda e, dc=dc, fi=fi, twin=twin, psh=psh, ww=ww, zz=zz, pas=pas:
                                          e.matmul(psh[:, 0:256],
                                                   ww[:, dc * 512 + 128 * fi: dc * 512 + 128 * fi + 128],
                                                   zz[:, dc * 2048 + 256 * twin: dc * 2048 + 256 * twin + 256],
                                                   start=(pas == 0 and dc == 0), stop=(pas == 2 and dc == 7)),
                                          wait=w, inc=(st, 1) if (pas == 2 and dc == 7) else None)
                            tv = st.n
                            emit_mm2()
                            gslot = ft % 2
                            em.on("scalar", lambda e, e_=e_, ft=ft, psh=psh, gslot=gslot:
                                  e.activation(gbuf[:, 256 * gslot:256 * gslot + 256], psh[:, 0:256],
                                               AF.Gelu, bias=C.b1[:, 16 * e_ + ft:16 * e_ + ft + 1]),
                                  wait=[(st, tv), (st, gb_free2[gslot])], inc=(ssc, 1))
                            psh_free[ft % 2] = ssc.n
                            sc0 = ssc.n
                            em.on("vector", lambda e, gslot=gslot:
                                  e.tensor_mul(gbuf[:, 256 * gslot:256 * gslot + 256],
                                               gbuf[:, 256 * gslot:256 * gslot + 256], wbc[:, :]),
                                  wait=[(ssc, sc0), (sv, wbc_done)], inc=(sv, 1))
                            gw_done = sv.n
                            pend.append((ft, b2_, gw_done, w2_dma, (e_ == 0 and ft == 0)))
                        w1_free[b1_] = st.n
                    emit_mm2()
                    em.dma("sync", lambda e, e_=e_: e.dma_start(out=b2row[0:1, :], in_=I["b2"][e_:e_ + 1, :]),
                           wait=[(st, b2_free[0])], sem=sd)
                    b2_dma = sd.n
                    for ts in range(2):
                        for dw in range(2):
                            em.on("tensor", lambda e, ts=ts, dw=dw, e_=e_, twin=twin:
                                  e.matmul(ps_x1[:, (2 * ts + dw) * 512:(2 * ts + dw) * 512 + 512],
                                           wrow[0:1, 128 * ts:128 * ts + 128],
                                           b2row[0:1, 512 * dw:512 * dw + 512],
                                           start=False, stop=(e_ == 7)),
                                  wait=[(sd, b2_dma)] if (ts == 0 and dw == 0) else (),
                                  inc=(st, 1))
                    b2_free[0] = st.n
                mix_done = st.n
                for ts in range(2):
                    i = 2 * twin + ts
                    em.dma("sync", lambda e, i=i: e.dma_start(out=asb[:, :], in_=kb.dAS[i, :, :]),
                           wait=[(sv, sv.n), (so, so.n)], sem=sd)
                    asd = sd.n
                    for dw in range(2):
                        em.on("vector", lambda e, ts=ts, dw=dw:
                              e.tensor_tensor(x1b[:, :], ps_x1[:, (2 * ts + dw) * 512:(2 * ts + dw) * 512 + 512],
                                              asb[:, dw * 512:dw * 512 + 512], ALU.add),
                              wait=[(st, mix_done), (sd, asd)], inc=(sv, 1))
                        ev = sv.n
                        em.dma("gpsimd", lambda e, i=i, dw=dw:
                               e.dma_start(out=kb.x1own[i, :, 512 * dw:512 * dw + 512], in_=x1b[:, :]),
                               wait=[(sv, ev)], sem=so)
                        em.on("vector", None, wait=[(so, so.n)])

            def x1bulk(e):
                p2048 = kb.rtv(e, "gpsimd", "par2048")
                dst = kb.xsh[0:T + 1, :][bass.ds(p2048, 2048), :]
                return e.dma_start(out=dst, in_=kb.x1own[:, :, :].rearrange("i r d -> (i r) d"))
            em.dma("gpsimd", x1bulk, wait=[(so, so.n)], sem=so)
            em.on("gpsimd", None, wait=[(so, so.n), (sg, sg.n), (swd, swd.n)])
            em.on("sync", None, wait=[(sd, sd.n)])


def build_exchange(kb):
    nc, C, sem = kb.nc, kb.C, kb.sem
    with ExitStack() as ps:
        imp = kb.sb("ex_imp", [128, 4096], F32, stack=ps)
        _order = ("gpsimd", "sync", "vector", "scalar", "tensor")
        zr = kb.sb("ex_zr", [4, D], F32, stack=ps)
        sg = sem("ex_dma")
        sv = sem("ex_vec")
        with Phase(kb) as em:
            em.engine_order = _order
            em.on("vector", lambda e: e.memset(zr[:, :], 0.0), inc=(sv, 1))
            zv = sv.n
            em.dma("gpsimd", lambda e: e.dma_start(out=kb.x1pad[0:1, :], in_=zr[0:1, :]),
                   wait=[(sv, zv)], sem=sg)
            em.dma("gpsimd", lambda e: e.dma_start(out=kb.x1pad[T + 1:T + 5, :], in_=zr[:, :]),
                   sem=sg)
            if EXC_MODE != "nobarrier":
                bsem = sem("ex_bar")
                em.dma("gpsimd", lambda e: e.dma_start(out=kb.bar_in[0:1, :], in_=zr[0:1, 0:16]),
                       wait=[(sv, zv)], sem=sg)
                em.on("gpsimd", lambda e: e.collective_compute(
                          "AllReduce", ALU.add, replica_groups=[list(range(8))],
                          ins=[kb.bar_in.opt()], outs=[kb.bar_out.opt()]),
                      wait=[(sg, sg.n)], inc=(bsem, 1), kind="dma")
                em.on("gpsimd", None, wait=[(bsem, bsem.n)])
            if "MRK" in kb.dbg:
                def marker(e):
                    p2048 = kb.rtv(e, "gpsimd", "par2048")
                    return e.dma_start(out=kb.xsh[0:T + 1, :][bass.ds(p2048, 1), 0:16],
                                       in_=C.onesrow[0:1, 0:16])
                em.dma("gpsimd", marker, wait=[(sg, sg.n)], sem=sg)
            if EXC_MODE != "noimport":
                def imp_read(e):
                    zsv = kb.rtv(e, "gpsimd", "zsv")
                    tru = kb.rtv(e, "gpsimd", "truthy")
                    src = kb.xsh[0:T + 1, :][bass.ds(zsv, T), :]
                    return e.dma_start(out=kb.x1tmp[:, :], in_=src, cond=tru)
                em.dma("gpsimd", imp_read, wait=[(sg, sg.n)], sem=sg)
                for g in range(NB):
                    srow = 2048 * (g % 2) + 128 * (g // 2)
                    em.dma("gpsimd", lambda e, g=g, srow=srow:
                           e.dma_start(out=kb.x1pad[1 + 128 * g:1 + 128 * g + 128, :],
                                       in_=kb.x1tmp[srow:srow + 128, :]),
                           wait=[(sg, sg.n)] if g == 0 else (), sem=sg)
            em.on("gpsimd", None, wait=[(sg, sg.n)])


def build_bitffn(kb):
    nc, C, I, sem = kb.nc, kb.C, kb.I, kb.sem
    with ExitStack() as ps:
        sbl = lambda n, s_, d=F32: kb.sb("b_" + n, s_, d, stack=ps)
        sd = sem("b_dma")
        sg = sem("b_gdma")
        so = sem("b_odma")
        sv = sem("b_vec")
        st = sem("b_ten")
        ssc = sem("b_scl")
        kb.cur_sv = "b_vec"
        kb.cur_ssc = "b_scl"

        kb.ps_b = ps.enter_context(nc.psum_tensor("ps_b", [128, 1024], F32))
        kb.ps_h7 = ps.enter_context(nc.psum_tensor("ps_h7", [128, 1024], F32))
        kb.ps_o7 = ps.enter_context(nc.psum_tensor("ps_o7", [128, 1024], F32))
        rs_w1_bc = sbl("rsw1", [128, 1])
        rs_w2_bc = sbl("rsw2", [128, 1])
        stat = sbl("stat", [128, 8])
        sc11 = sbl("sc11", [1, 4])

        with ExitStack() as pq:
            sbq = lambda n, s_, d=F32: kb.sb("bq_" + n, s_, d, stack=pq)
            wst = sbq("wst", [128, 4096])
            i32t = pq.enter_context(nc.sbuf_tensor("bq_i32", [128, 4096], I32))
            qf = sbq("qf", [128, 4096])
            cc = sbq("cc", [128, 4096])
            tstage = sbq("tstage", [128, 4096], BF16)
            acc = sbq("acc", [128, 2])

            with Phase(kb) as em:
                tr_k = [0]
                tr_done = []

                def transpose_tile(src_ap, dst_ap, extra_wait=(), cast=True):
                    col = (tr_k[0] % 2) * 512
                    w = list(extra_wait)
                    if tr_k[0] >= 2:
                        w.append((sv, tr_done[tr_k[0] - 2]))
                    em.on("tensor", lambda e, src_ap=src_ap, col=col:
                          e.transpose(kb.ps_b[:, col:col + 128], src_ap, C.ident[:, :]),
                          wait=w, inc=(st, 1))
                    tv = st.n
                    em.on("vector", lambda e, dst_ap=dst_ap, col=col:
                          e.tensor_copy(dst_ap, kb.ps_b[:, col:col + 128]),
                          wait=[(st, tv)], inc=(sv, 1))
                    tr_done.append(sv.n)
                    tr_k[0] += 1

                for wi, (wnm, nch, ncol, dT, accj) in enumerate(
                        (("w1f", 32, 1024, kb.dW1qT, 0), ("w2f", 8, 4096, kb.dW2qT, 1))):
                    for c in range(nch):
                        em.dma("sync", lambda e, wnm=wnm, c=c, ncol=ncol:
                               e.dma_start(out=wst[:, 0:ncol], in_=I[wnm][c * 128:(c + 1) * 128, :]),
                               wait=[(sv, sv.n)], sem=sd)
                        ld = sd.n
                        em.on("vector", lambda e, ncol=ncol: e.reduce_sum(stat[:, 0:1], wst[:, 0:ncol], AX.X,
                                                                          apply_absolute_value=True),
                              wait=[(sd, ld)], inc=(sv, 1))
                        if c == 0:
                            em.on("vector", lambda e, accj=accj: e.tensor_copy(acc[:, accj:accj + 1], stat[:, 0:1]),
                                  inc=(sv, 1))
                        else:
                            em.on("vector", lambda e, accj=accj: e.tensor_add(acc[:, accj:accj + 1],
                                                                              acc[:, accj:accj + 1], stat[:, 0:1]),
                                  inc=(sv, 1))
                    accv = sv.n
                    em.on("tensor", lambda e, accj=accj:
                          e.matmul(kb.ps_b[0:1, 512:513], acc[:, accj:accj + 1], C.ones[:, 0:1],
                                   start=True, stop=True),
                          wait=[(sv, accv), (sv, sv.n)], inc=(st, 1))
                    tv = st.n
                    em.on("vector", lambda e: e.tensor_copy(sc11[0:1, 0:1], kb.ps_b[0:1, 512:513]),
                          wait=[(st, tv)], inc=(sv, 1))
                    em.on("vector", lambda e, nch=nch, ncol=ncol:
                          e.tensor_scalar_mul(sc11[0:1, 0:1], sc11[0:1, 0:1], 1.0 / (nch * 128 * ncol)), inc=(sv, 1))
                    em.on("vector", lambda e: e.tensor_scalar_max(sc11[0:1, 0:1], sc11[0:1, 0:1], 1e-5), inc=(sv, 1))
                    em.on("vector", lambda e: e.reciprocal(sc11[0:1, 1:2], sc11[0:1, 0:1]), inc=(sv, 1))
                    em.on("vector", lambda e: e.reciprocal(sc11[0:1, 2:3], sc11[0:1, 1:2]), inc=(sv, 1))
                    scv = sv.n
                    dstb = rs_w1_bc if wi == 0 else rs_w2_bc
                    em.on("tensor", lambda e:
                          e.matmul(kb.ps_b[:, 512:513], C.onesrow[0:1, 0:128], sc11[0:1, 1:2],
                                   start=True, stop=True),
                          wait=[(sv, sv.n)], inc=(st, 1))
                    tv = st.n
                    em.on("vector", lambda e: e.tensor_copy(stat[:, 1:2], kb.ps_b[:, 512:513]),
                          wait=[(st, tv)], inc=(sv, 1))
                    em.on("tensor", lambda e:
                          e.matmul(kb.ps_b[:, 512:513], C.onesrow[0:1, 0:128], sc11[0:1, 2:3],
                                   start=True, stop=True),
                          wait=[(sv, sv.n)], inc=(st, 1))
                    tv = st.n
                    em.on("vector", lambda e, dstb=dstb: e.tensor_copy(dstb[:, 0:1], kb.ps_b[:, 512:513]),
                          wait=[(st, tv)], inc=(sv, 1))
                    for c in range(nch):
                        em.dma("sync", lambda e, wnm=wnm, c=c, ncol=ncol:
                               e.dma_start(out=wst[:, 0:ncol], in_=I[wnm][c * 128:(c + 1) * 128, :]),
                               wait=[(sv, sv.n)], sem=sd)
                        ld = sd.n
                        em.on("vector", lambda e, ncol=ncol:
                              e.tensor_scalar_mul(qf[:, 0:ncol], wst[:, 0:ncol], stat[:, 1:2]),
                              wait=[(sd, ld)], inc=(sv, 1))
                        em.on("vector", lambda e, ncol=ncol: e.tensor_copy(i32t[:, 0:ncol], qf[:, 0:ncol]), inc=(sv, 1))
                        em.on("vector", lambda e, ncol=ncol: e.tensor_copy(cc[:, 0:ncol], i32t[:, 0:ncol]), inc=(sv, 1))
                        em.on("vector", lambda e, ncol=ncol: e.tensor_sub(qf[:, 0:ncol], qf[:, 0:ncol], cc[:, 0:ncol]), inc=(sv, 1))
                        em.on("vector", lambda e, ncol=ncol: e.tensor_single_scalar(wst[:, 0:ncol], qf[:, 0:ncol], 0.5, ALU.is_ge), inc=(sv, 1))
                        em.on("vector", lambda e, ncol=ncol: e.tensor_add(cc[:, 0:ncol], cc[:, 0:ncol], wst[:, 0:ncol]), inc=(sv, 1))
                        em.on("vector", lambda e, ncol=ncol: e.tensor_single_scalar(wst[:, 0:ncol], qf[:, 0:ncol], -0.5, ALU.is_le), inc=(sv, 1))
                        em.on("vector", lambda e, ncol=ncol: e.tensor_sub(cc[:, 0:ncol], cc[:, 0:ncol], wst[:, 0:ncol]), inc=(sv, 1))
                        em.on("vector", lambda e, ncol=ncol:
                              e.tensor_scalar(cc[:, 0:ncol], cc[:, 0:ncol], -1.0, 1.0, ALU.max, ALU.min), inc=(sv, 1))
                        qv = sv.n
                        for j in range(ncol // 128):
                            transpose_tile(cc[:, j * 128:(j + 1) * 128],
                                           tstage[:, j * 128:(j + 1) * 128],
                                           extra_wait=[(sv, qv), (so, so.n)] if j == 0 else ())
                        trv = sv.n
                        em.dma("gpsimd", lambda e, c=c, dT=dT, ncol=ncol:
                               e.dma_start(out=dT.rearrange("(j p) x -> p j x", p=128)[:, :, c * 128:(c + 1) * 128],
                                           in_=tstage[:, 0:ncol].rearrange("p (j q) -> p j q", q=128)),
                               wait=[(sv, trv)], sem=so)
                em.on("gpsimd", None, wait=[(so, so.n)])
                em.on("sync", None, wait=[(sd, sd.n)])

        w1q = ps.enter_context(nc.sbuf_tensor("b_w1q", [128, 8 * 4096], BF16))
        w2qb = [ps.enter_context(nc.sbuf_tensor(f"b_w2q{i}", [128, 1024], BF16)) for i in range(2)]
        xo = sbl("xo", [128, D])
        xnb = sbl("xnb", [128, D])
        vq = sbl("vq", [128, 4096])
        i32b = ps.enter_context(nc.sbuf_tensor("b_i32b", [128, 4096], I32))
        hcb = sbl("hcb", [128, 4096])
        hext = sbl("hext", [128, 32 * 130])
        aqb = ps.enter_context(nc.sbuf_tensor("b_aqb", [128, 4096], BF16))
        xqT = ps.enter_context(nc.sbuf_tensor("b_xqT", [128, 1024], BF16))
        hT = ps.enter_context(nc.sbuf_tensor("b_hT", [128, 256], BF16))
        hh_sb = sbl("hh", [128, 1024])
        hsc = sbl("hsc", [128, 32])
        rrow = sbl("rrow", [128, 128])
        row1 = sbl("row1", [1, 128])
        asb = sbl("asb", [128, D])
        obuf = [sbl("obuf0", [128, 512]), sbl("obuf1", [128, 512])]

        with Phase(kb) as em:
            tr_k = [0]
            tr_done = []

            def transpose_tile(src_ap, dst_ap, extra_wait=(), wout=128, kin=128):
                col = (tr_k[0] % 2) * 512
                w = list(extra_wait)
                if tr_k[0] >= 2:
                    w.append((sv, tr_done[tr_k[0] - 2]))
                em.on("tensor", lambda e, src_ap=src_ap, col=col, wout=wout, kin=kin:
                      e.transpose(kb.ps_b[:, col:col + wout], src_ap, C.ident[0:kin, 0:kin]),
                      wait=w, inc=(st, 1))
                tv = st.n
                em.on("vector", lambda e, dst_ap=dst_ap, col=col, wout=wout:
                      e.tensor_copy(dst_ap, kb.ps_b[:, col:col + wout]),
                      wait=[(st, tv)], inc=(sv, 1))
                tr_done.append(sv.n)
                tr_k[0] += 1

            for dc in range(8):
                em.dma("sync", lambda e, dc=dc:
                       e.dma_start(out=w1q[:, dc * 4096:(dc + 1) * 4096], in_=kb.dW1qT[dc * 128:(dc + 1) * 128, :]),
                       sem=sd)
            w1dma = sd.n
            def xbulk(e):
                p128 = kb.rtv(e, "gpsimd", "par128")
                src = kb.x1v.rearrange("(i r) d -> i r d", r=256)[:, bass.ds(p128, 128), :]
                return e.dma_start(out=kb.xown[:, :, :], in_=src)
            em.dma("gpsimd", xbulk, sem=sg)
            xbulkv = sg.n
            def hload(e, base, dst0):
                p128 = kb.rtv(e, "gpsimd", "par128")
                src = kb.x1pad[base:base + 16 * 256, :].rearrange("(i r) d -> i r d", r=256)[:, bass.ds(p128, 1), :]
                return e.dma_start(out=xnb[dst0:dst0 + 16, :].rearrange("i (a d) -> i a d", a=1), in_=src)
            em.dma("gpsimd", lambda e: hload(e, 0, 0), sem=sg)
            em.dma("gpsimd", lambda e: hload(e, 129, 16), sem=sg)
            hloadv = sg.n

            for pb_ in (32, 64, 96):
                em.on("vector", lambda e, pb_=pb_: e.memset(xnb[pb_:pb_ + 32, :], 0.0), inc=(sv, 1))
            ln_steps(kb, em, xnb[:, :], xnb[:, :], stat[:, 0:1], stat[:, 1:2],
                     vq[:, 0:1024], 1e-5, first_wait=[(sg, hloadv)])
            em.on("vector", lambda e: e.reduce_max(stat[0:32, 2:3], xnb[0:32, :], AX.X,
                                                   apply_absolute_value=True), inc=(sv, 1))
            em.on("vector", lambda e: e.tensor_scalar_max(stat[0:32, 2:3], stat[0:32, 2:3], 1e-5), inc=(sv, 1))
            em.on("vector", lambda e: e.reciprocal(stat[0:32, 3:4], stat[0:32, 2:3]), inc=(sv, 1))
            em.on("vector", lambda e: e.tensor_scalar_mul(stat[0:32, 3:4], stat[0:32, 3:4], 127.0), inc=(sv, 1))
            em.on("vector", lambda e: e.reciprocal(stat[0:32, 4:5], stat[0:32, 3:4]), inc=(sv, 1))
            em.on("vector", lambda e: e.tensor_scalar_mul(vq[0:32, 0:1024], xnb[0:32, :], stat[0:32, 3:4]), inc=(sv, 1))
            em.on("vector", lambda e: e.tensor_copy(i32b[0:32, 0:1024], vq[0:32, 0:1024]), inc=(sv, 1))
            em.on("vector", lambda e: e.tensor_copy(hcb[0:32, 0:1024], i32b[0:32, 0:1024]), inc=(sv, 1))
            em.on("vector", lambda e: e.tensor_sub(vq[0:32, 0:1024], vq[0:32, 0:1024], hcb[0:32, 0:1024]), inc=(sv, 1))
            em.on("vector", lambda e: e.tensor_single_scalar(hext[0:32, 0:1024], vq[0:32, 0:1024], 0.5, ALU.is_ge), inc=(sv, 1))
            em.on("vector", lambda e: e.tensor_add(hcb[0:32, 0:1024], hcb[0:32, 0:1024], hext[0:32, 0:1024]), inc=(sv, 1))
            em.on("vector", lambda e: e.tensor_single_scalar(hext[0:32, 0:1024], vq[0:32, 0:1024], -0.5, ALU.is_le), inc=(sv, 1))
            em.on("vector", lambda e: e.tensor_sub(hcb[0:32, 0:1024], hcb[0:32, 0:1024], hext[0:32, 0:1024]), inc=(sv, 1))
            em.on("vector", lambda e: e.tensor_copy(xnb[0:32, :], hcb[0:32, 0:1024]), inc=(sv, 1))
            hq = sv.n
            for dc in range(8):
                transpose_tile(xnb[0:32, dc * 128:(dc + 1) * 128], hT[:, dc * 32:(dc + 1) * 32],
                               extra_wait=[(sv, hq)] if dc == 0 else (), wout=32, kin=32)
            em.on("tensor", lambda e: e.transpose(kb.ps_b[0:1, 512:544], stat[0:32, 4:5], C.ident[0:32, 0:32]),
                  wait=[(sv, sv.n)], inc=(st, 1))
            tv = st.n
            em.on("vector", lambda e: e.tensor_copy(row1[0:1, 0:32], kb.ps_b[0:1, 512:544]),
                  wait=[(st, tv)], inc=(sv, 1))
            rv = sv.n
            em.on("tensor", lambda e: e.matmul(kb.ps_b[:, 512:544], C.onesrow[0:1, 0:128], row1[0:1, 0:32],
                                               start=True, stop=True),
                  wait=[(sv, rv)], inc=(st, 1))
            tv = st.n
            em.on("vector", lambda e: e.tensor_copy(hsc[:, :], kb.ps_b[:, 512:544]),
                  wait=[(st, tv)], inc=(sv, 1))
            em.on("vector", lambda e: e.tensor_scalar_mul(hsc[:, :], hsc[:, :], rs_w1_bc[:, 0:1]), inc=(sv, 1))
            hscv = sv.n
            for ct in range(32):
                pb = ct % 2
                for dc in range(8):
                    w = []
                    if dc == 0:
                        w = [(sv, sv.n)]
                        if ct == 0:
                            w.append((sd, w1dma))
                    em.on("tensor", lambda e, ct=ct, dc=dc, pb=pb:
                          e.matmul(kb.ps_h7[:, pb * 512:pb * 512 + 32],
                                   w1q[:, dc * 4096 + 128 * ct: dc * 4096 + 128 * ct + 128],
                                   hT[:, dc * 32:(dc + 1) * 32],
                                   start=(dc == 0), stop=(dc == 7)),
                          wait=w, inc=(st, 1) if dc == 7 else None)
                tv = st.n
                em.on("vector", lambda e, ct=ct, pb=pb:
                      e.tensor_tensor(hh_sb[:, 32 * ct:32 * ct + 32], kb.ps_h7[:, pb * 512:pb * 512 + 32],
                                      hsc[:, :], ALU.mult),
                      wait=[(st, tv), (sv, hscv)], inc=(sv, 1))
            em.on("vector", lambda e: e.tensor_scalar(C.scratch1[:, :], C.par[:, :], -1.0, 1.0, ALU.mult, ALU.add),
                  inc=(sv, 1))
            hh3 = hh_sb[:, :].rearrange("p (c h) -> p c h", h=32)
            em.on("vector", lambda e: e.tensor_scalar_mul(hh3[:, :, 0:1], hh3[:, :, 0:1], C.par[:, 0:1]), inc=(sv, 1))
            em.on("vector", lambda e: e.tensor_scalar_mul(hh3[:, :, 31:32], hh3[:, :, 31:32], C.scratch1[:, 0:1]), inc=(sv, 1))
            hh_done = sv.n
            if "BH" in kb.dbg:
                em.dma("gpsimd", lambda e: e.dma_start(out=kb.dbg["BH"], in_=hh_sb[:, :]),
                       wait=[(sv, hh_done)], sem=sg)

            w2_free = [0, 0]
            obuf_free = [0, 0]
            for i in range(OB):
                em.dma("gpsimd", lambda e, i=i: e.dma_start(out=xo[:, :], in_=kb.xown[i, :, :]),
                       wait=[(sv, sv.n), (ssc, ssc.n)] + ([(sg, xbulkv)] if i == 0 else []), sem=sg)
                ldx = sg.n
                em.dma("sync", lambda e, i=i: e.dma_start(out=asb[:, :], in_=kb.dA[i, :, :]),
                       wait=[(sv, sv.n)], sem=sd)
                lda = sd.n
                ln_steps(kb, em, xo[:, :], xnb[:, :], stat[:, 0:1], stat[:, 1:2], vq[:, 0:1024], 1e-5,
                         first_wait=[(sg, ldx)])
                em.on("vector", lambda e: e.reduce_max(stat[:, 2:3], xnb[:, :], AX.X, apply_absolute_value=True), inc=(sv, 1))
                em.on("vector", lambda e: e.tensor_scalar_max(stat[:, 2:3], stat[:, 2:3], 1e-5), inc=(sv, 1))
                em.on("vector", lambda e: e.reciprocal(stat[:, 3:4], stat[:, 2:3]), inc=(sv, 1))
                em.on("vector", lambda e: e.tensor_scalar_mul(stat[:, 3:4], stat[:, 3:4], 127.0), inc=(sv, 1))
                em.on("vector", lambda e: e.reciprocal(stat[:, 4:5], stat[:, 3:4]), inc=(sv, 1))
                em.on("vector", lambda e: e.tensor_scalar_mul(vq[:, 0:1024], xnb[:, :], stat[:, 3:4]), inc=(sv, 1))
                em.on("vector", lambda e: e.tensor_copy(i32b[:, 0:1024], vq[:, 0:1024]), inc=(sv, 1))
                em.on("vector", lambda e: e.tensor_copy(hcb[:, 0:1024], i32b[:, 0:1024]), inc=(sv, 1))
                em.on("vector", lambda e: e.tensor_sub(vq[:, 0:1024], vq[:, 0:1024], hcb[:, 0:1024]), inc=(sv, 1))
                em.on("vector", lambda e: e.tensor_single_scalar(hext[:, 0:1024], vq[:, 0:1024], 0.5, ALU.is_ge), inc=(sv, 1))
                em.on("vector", lambda e: e.tensor_add(hcb[:, 0:1024], hcb[:, 0:1024], hext[:, 0:1024]), inc=(sv, 1))
                em.on("vector", lambda e: e.tensor_single_scalar(hext[:, 0:1024], vq[:, 0:1024], -0.5, ALU.is_le), inc=(sv, 1))
                em.on("vector", lambda e: e.tensor_sub(hcb[:, 0:1024], hcb[:, 0:1024], hext[:, 0:1024]), inc=(sv, 1))
                xqv = sv.n
                for dc in range(8):
                    transpose_tile(hcb[:, dc * 128:(dc + 1) * 128], xqT[:, dc * 128:(dc + 1) * 128],
                                   extra_wait=[(sv, xqv)] if dc == 0 else ())
                em.on("tensor", lambda e: e.transpose(kb.ps_b[0:1, 512:640], stat[:, 4:5], C.ident[:, :]),
                      wait=[(sv, sv.n)], inc=(st, 1))
                tv = st.n
                em.on("vector", lambda e: e.tensor_copy(row1[0:1, 0:128], kb.ps_b[0:1, 512:640]),
                      wait=[(st, tv)], inc=(sv, 1))
                rv = sv.n
                em.on("tensor", lambda e: e.matmul(kb.ps_b[:, 512:640], C.onesrow[0:1, 0:128], row1[0:1, 0:128],
                                                   start=True, stop=True),
                      wait=[(sv, rv)], inc=(st, 1))
                tv = st.n
                em.on("vector", lambda e: e.tensor_copy(rrow[:, :], kb.ps_b[:, 512:640]),
                      wait=[(st, tv)], inc=(sv, 1))
                em.on("vector", lambda e: e.tensor_scalar_mul(rrow[:, :], rrow[:, :], rs_w1_bc[:, 0:1]), inc=(sv, 1))
                rrv = sv.n
                for ct in range(32):
                    pb = ct % 2
                    for dc in range(8):
                        w = [(sv, sv.n)] if dc == 0 else []
                        em.on("tensor", lambda e, ct=ct, dc=dc, pb=pb:
                              e.matmul(kb.ps_h7[:, pb * 512:pb * 512 + 128],
                                       w1q[:, dc * 4096 + 128 * ct: dc * 4096 + 128 * ct + 128],
                                       xqT[:, dc * 128:(dc + 1) * 128],
                                       start=(dc == 0), stop=(dc == 7)),
                              wait=w, inc=(st, 1) if dc == 7 else None)
                    tv = st.n
                    em.on("vector", lambda e, ct=ct, pb=pb:
                          e.tensor_tensor(hext[:, 130 * ct + 1:130 * ct + 129],
                                          kb.ps_h7[:, pb * 512:pb * 512 + 128], rrow[:, :], ALU.mult),
                          wait=[(st, tv), (sv, rrv)], inc=(sv, 1))
                hx3 = hext[:, :].rearrange("p (c w) -> p c w", w=130)
                em.on("vector", lambda e, i=i: e.tensor_copy(hx3[:, :, 0:1], hh3[:, :, i:i + 1]),
                      wait=[(sv, hh_done)], inc=(sv, 1))
                em.on("vector", lambda e, i=i: e.tensor_copy(hx3[:, :, 129:130], hh3[:, :, 16 + i:17 + i]),
                      inc=(sv, 1))
                for ct in range(32):
                    em.on("vector", lambda e, ct=ct:
                          e.tensor_scalar(hcb[:, 128 * ct:128 * ct + 128], hext[:, 130 * ct + 1:130 * ct + 129],
                                          C.cw[:, 3 * ct + 1:3 * ct + 2], C.cb[:, ct:ct + 1], ALU.mult, ALU.add),
                          inc=(sv, 1))
                    em.on("vector", lambda e, ct=ct:
                          e.scalar_tensor_tensor(hcb[:, 128 * ct:128 * ct + 128], hext[:, 130 * ct:130 * ct + 128],
                                                 C.cw[:, 3 * ct:3 * ct + 1], hcb[:, 128 * ct:128 * ct + 128],
                                                 ALU.mult, ALU.add),
                          inc=(sv, 1))
                    em.on("vector", lambda e, ct=ct:
                          e.scalar_tensor_tensor(hcb[:, 128 * ct:128 * ct + 128], hext[:, 130 * ct + 2:130 * ct + 130],
                                                 C.cw[:, 3 * ct + 2:3 * ct + 3], hcb[:, 128 * ct:128 * ct + 128],
                                                 ALU.mult, ALU.add),
                          inc=(sv, 1))
                cvv = sv.n
                if i == 0 and "BX0" in kb.dbg:
                    em.dma("gpsimd", lambda e: e.dma_start(out=kb.dbg["BX0"], in_=hext[:, :]),
                           wait=[(sv, cvv)], sem=sg)
                    em.on("vector", None, wait=[(sg, sg.n)])
                if i == 0 and "BC0" in kb.dbg:
                    em.dma("gpsimd", lambda e: e.dma_start(out=kb.dbg["BC0"], in_=hcb[:, :]),
                           wait=[(sv, cvv)], sem=sg)
                    em.on("vector", None, wait=[(sg, sg.n)])
                em.on("scalar", lambda e: e.activation(hext[:, 0:4096], hcb[:, :], AF.Silu),
                      wait=[(sv, cvv)], inc=(ssc, 1))
                silv = ssc.n
                em.on("scalar", lambda e: e.activation(vq[:, :], hext[:, 0:4096], AF.Abs),
                      wait=[(ssc, silv)], inc=(ssc, 1))
                absv = ssc.n
                em.on("vector", lambda e: e.tensor_tensor(vq[:, 0:2048], vq[:, 0:2048], vq[:, 2048:4096], ALU.max),
                      wait=[(ssc, absv)], inc=(sv, 1))
                em.on("vector", lambda e: e.tensor_tensor(vq[:, 0:1024], vq[:, 0:1024], vq[:, 1024:2048], ALU.max), inc=(sv, 1))
                em.on("vector", lambda e: e.tensor_tensor(vq[:, 0:512], vq[:, 0:512], vq[:, 512:1024], ALU.max), inc=(sv, 1))
                em.on("vector", lambda e: e.tensor_tensor(vq[:, 0:256], vq[:, 0:256], vq[:, 256:512], ALU.max), inc=(sv, 1))
                em.on("vector", lambda e: e.tensor_tensor(vq[:, 0:128], vq[:, 0:128], vq[:, 128:256], ALU.max), inc=(sv, 1))
                mtv = sv.n
                em.on("tensor", lambda e: e.transpose(kb.ps_b[:, 512:640], vq[:, 0:128], C.ident[:, :]),
                      wait=[(sv, mtv), (sv, sv.n)], inc=(st, 1))
                tv = st.n
                em.on("vector", lambda e: e.tensor_copy(rrow[:, :], kb.ps_b[:, 512:640]),
                      wait=[(st, tv)], inc=(sv, 1))
                em.on("vector", lambda e: e.reduce_max(stat[:, 2:3], rrow[:, :], AX.X), inc=(sv, 1))
                em.on("vector", lambda e: e.tensor_scalar_max(stat[:, 2:3], stat[:, 2:3], 1e-5), inc=(sv, 1))
                em.on("vector", lambda e: e.reciprocal(stat[:, 3:4], stat[:, 2:3]), inc=(sv, 1))
                em.on("vector", lambda e: e.tensor_scalar_mul(stat[:, 3:4], stat[:, 3:4], 127.0), inc=(sv, 1))
                em.on("vector", lambda e: e.reciprocal(stat[:, 4:5], stat[:, 3:4]), inc=(sv, 1))
                em.on("tensor", lambda e: e.transpose(kb.ps_b[0:1, 512:640], stat[:, 3:4], C.ident[:, :]),
                      wait=[(sv, sv.n)], inc=(st, 1))
                tv = st.n
                em.on("vector", lambda e: e.tensor_copy(row1[0:1, 0:128], kb.ps_b[0:1, 512:640]),
                      wait=[(st, tv)], inc=(sv, 1))
                rv = sv.n
                em.on("tensor", lambda e: e.matmul(kb.ps_b[:, 512:640], C.onesrow[0:1, 0:128], row1[0:1, 0:128],
                                                   start=True, stop=True),
                      wait=[(sv, rv)], inc=(st, 1))
                tv = st.n
                em.on("vector", lambda e: e.tensor_copy(rrow[:, :], kb.ps_b[:, 512:640]),
                      wait=[(st, tv)], inc=(sv, 1))
                scbv = sv.n
                for ct in range(32):
                    em.on("vector", lambda e, ct=ct:
                          e.tensor_tensor(vq[:, 128 * ct:128 * ct + 128], hext[:, 128 * ct:128 * ct + 128],
                                          rrow[:, :], ALU.mult),
                          wait=[(sv, scbv)] if ct == 0 else (), inc=(sv, 1))
                em.on("vector", lambda e: e.tensor_copy(i32b[:, :], vq[:, :]), inc=(sv, 1))
                em.on("vector", lambda e: e.tensor_copy(hcb[:, :], i32b[:, :]), inc=(sv, 1))
                em.on("vector", lambda e: e.tensor_sub(vq[:, :], vq[:, :], hcb[:, :]), inc=(sv, 1))
                em.on("vector", lambda e: e.tensor_single_scalar(hext[:, 0:4096], vq[:, :], 0.5, ALU.is_ge), inc=(sv, 1))
                em.on("vector", lambda e: e.tensor_add(hcb[:, :], hcb[:, :], hext[:, 0:4096]), inc=(sv, 1))
                em.on("vector", lambda e: e.tensor_single_scalar(hext[:, 0:4096], vq[:, :], -0.5, ALU.is_le), inc=(sv, 1))
                em.on("vector", lambda e: e.tensor_sub(hcb[:, :], hcb[:, :], hext[:, 0:4096]), inc=(sv, 1))
                em.on("vector", lambda e: e.tensor_copy(aqb[:, :], hcb[:, :]), inc=(sv, 1))
                aqv = sv.n
                if i == 0 and "BQ0" in kb.dbg:
                    em.dma("gpsimd", lambda e: e.dma_start(out=kb.dbg["BQ0"], in_=hcb[:, :]),
                           wait=[(sv, aqv)], sem=sg)
                    em.on("vector", None, wait=[(sg, sg.n)])
                em.on("vector", lambda e: e.tensor_tensor(stat[:, 5:6], stat[:, 4:5], rs_w2_bc[:, 0:1], ALU.mult),
                      inc=(sv, 1))
                csv = sv.n
                for ck in range(32):
                    b2_ = ck % 2
                    qd = sd if b2_ == 0 else sg
                    em.dma("sync" if b2_ == 0 else "gpsimd", lambda e, ck=ck, b2_=b2_:
                           e.dma_start(out=w2qb[b2_][:, :], in_=kb.dW2qT[128 * ck:128 * ck + 128, :]),
                           wait=[(st, w2_free[b2_])], sem=qd)
                    wdma = qd.n
                    for dw in range(2):
                        w = []
                        if dw == 0:
                            w = [(qd, wdma)]
                            if ck == 0:
                                w.append((sv, aqv))
                        em.on("tensor", lambda e, ck=ck, dw=dw, b2_=b2_:
                              e.matmul(kb.ps_o7[:, dw * 512:dw * 512 + 512],
                                       aqb[:, 128 * ck:128 * ck + 128],
                                       w2qb[b2_][:, dw * 512:dw * 512 + 512],
                                       start=(ck == 0), stop=(ck == 31)),
                              wait=w, inc=(st, 1) if (dw == 1) else None)
                    w2_free[b2_] = st.n
                mmv = st.n
                for dw in range(2):
                    em.on("vector", lambda e, dw=dw:
                          e.scalar_tensor_tensor(obuf[dw][:, :], kb.ps_o7[:, dw * 512:dw * 512 + 512],
                                                 stat[:, 5:6], xo[:, dw * 512:dw * 512 + 512],
                                                 ALU.mult, ALU.add),
                          wait=[(st, mmv), (sv, csv), (sd, lda), (so, obuf_free[dw])], inc=(sv, 1))
                    em.on("vector", lambda e, dw=dw:
                          e.tensor_tensor(obuf[dw][:, :], obuf[dw][:, :], asb[:, dw * 512:dw * 512 + 512], ALU.add),
                          inc=(sv, 1))
                    ev = sv.n
                    em.dma("gpsimd", lambda e, i=i, dw=dw:
                           e.dma_start(out=kb.out[128 * i:128 * i + 128, 512 * dw:512 * dw + 512], in_=obuf[dw][:, :]),
                           wait=[(sv, ev)], sem=so)
                    obuf_free[dw] = so.n

            em.on("gpsimd", None, wait=[(so, so.n), (sg, sg.n)])
            em.on("sync", None, wait=[(sd, sd.n)])
    kb.out_written = True


DEBUG_SRC = {}


def finish(kb):
    nc, sem = kb.nc, kb.sem
    sd = sem("fin_dma")
    with Phase(kb) as em:
        if not kb.out_written:
            z = kb.sb("fin_zero", [128, D])
            em.on("vector", lambda e: e.memset(z[:, :], 0.0), inc=(sem("fin_v"), 1))
            vv = kb.sems["fin_v"].n
            for i in range(OB):
                em.dma("gpsimd", lambda e, i=i: e.dma_start(out=kb.out[i * 128:(i + 1) * 128, :], in_=z[:, :]),
                       wait=[(sem("fin_v"), vv)], sem=sd)
        for nm, t in kb.dbg.items():
            fn = DEBUG_SRC.get(nm)
            if fn is None:
                continue
            import inspect as _ins
            if len(_ins.signature(fn).parameters) == 2:
                em.dma("gpsimd", lambda e, t=t, fn=fn: e.dma_start(out=t, in_=fn(kb, e)), sem=sd)
            else:
                src = fn(kb)
                em.dma("gpsimd", lambda e, t=t, src=src: e.dma_start(out=t, in_=src), sem=sd)
        em.on("gpsimd", None, wait=[(sd, sd.n)])



def stage_inputs(inputs, core):
    s = core // 2
    x = np.ascontiguousarray(inputs["L0_attn_x"][s])
    m = {
        "xseq": x,
        "l0_wq": inputs["L0_attn_wq"], "l0_bq": inputs["L0_attn_bq"],
        "l0_wk": inputs["L0_attn_wk"], "l0_bk": inputs["L0_attn_bk"],
        "l0_wv": inputs["L0_attn_wv"], "l0_bv": inputs["L0_attn_bv"],
        "l0_wo": inputs["L0_attn_wo"], "l0_bo": inputs["L0_attn_bo"],
        "ln_g": inputs["L0_ffn_ln_g"], "ln_b": inputs["L0_ffn_ln_b"],
        "rW": inputs["L0_ffn_rW"], "rb": inputs["L0_ffn_rb"],
        "W1": inputs["L0_ffn_W1"], "b1": inputs["L0_ffn_b1"],
        "W2": inputs["L0_ffn_W2"], "b2": inputs["L0_ffn_b2"],
        "gam": inputs["L1_attn_ln_gamma"], "bet": inputs["L1_attn_ln_beta"],
        "l1_wq": inputs["L1_attn_Wq"], "l1_bq": inputs["L1_attn_bq"],
        "l1_wk": inputs["L1_attn_Wk"], "l1_bk": inputs["L1_attn_bk"],
        "l1_wv": inputs["L1_attn_Wv"], "l1_bv": inputs["L1_attn_bv"],
        "l1_wo": inputs["L1_attn_Wo"], "l1_bo": inputs["L1_attn_bo"],
        "mu": inputs["L1_attn_mu"].reshape(1, 1),
        "w1f": inputs["L1_ffn_w1"], "convw": inputs["L1_ffn_conv_w"].reshape(4096, 3),
        "convb": inputs["L1_ffn_conv_b"], "w2f": inputs["L1_ffn_w2"],
        "parconst": np.array([[0.0], [0.0], [1.0]], dtype=np.float32),
    }
    return {k: np.ascontiguousarray(np.asarray(v, dtype=np.float32)) for k, v in m.items()}


_NC_CACHE = {}


def run(inputs, upto=99):
    key = upto
    if key not in _NC_CACHE:
        _NC_CACHE[key] = build_nc(upto)
    nc = _NC_CACHE[key]
    in_maps = [stage_inputs(inputs, c) for c in range(8)]
    res = run_bass_kernel_spmd(nc, in_maps, core_ids=list(range(8)))
    return res


def kernel(**inputs):
    res = run(inputs)
    out = np.zeros((4, T, D), dtype=np.float32)
    for c in range(8):
        s, p = c // 2, c % 2
        xo = np.asarray(res.results[c]["xout"])
        for i in range(OB):
            g = 2 * i + p
            out[s, g * 128:(g + 1) * 128, :] = xo[i * 128:(i + 1) * 128, :]
    return out
